# version 42
# baseline (speedup 1.0000x reference)
import contextlib
from collections import defaultdict

import numpy as np
import ml_dtypes
import concourse.bass as bass
import concourse.mybir as mybir
from concourse.bass_utils import run_bass_kernel_spmd

F32 = mybir.dt.float32
BF16 = mybir.dt.bfloat16
U8 = mybir.dt.uint8
AF = mybir.ActivationFunctionType
ALU = mybir.AluOpType
NPBF16 = ml_dtypes.bfloat16

NCORES = 8
ENGINES = ("sync", "pe", "act", "dve", "pool")


def make_cfg(T=8192, D=4096, NH=32, TC=512):
    c = {}
    c["T"], c["D"], c["NH"], c["TC"] = T, D, NH, TC
    c["DSL"] = D // NCORES
    c["HLOC"] = NH // NCORES
    c["NCH"] = T // TC
    c["NDT"] = c["DSL"] // 128
    c["NDNT"] = c["DSL"] * 16 // 128
    c["NST"] = 16
    c["DTR"] = 128
    c["DN"] = 128
    c["DV"] = 128
    c["DQ"] = 192
    c["KT"] = TC // 128
    c["G"] = 64
    c["SHIFT"] = 12.0
    return c


class Prog:

    def __init__(self):
        self.ops = []
        self.sem_idx = defaultdict(int)
        self.pending = {e: [] for e in ENGINES}
        self.last_ev = {e: None for e in ENGINES}

    def op(self, eng, emit, sem=None, waits=(), dma=False):
        sem = sem or eng
        idx = self.sem_idx[sem]
        self.sem_idx[sem] = idx + 1
        w = list(self.pending[eng]) + [x for x in waits if x is not None]
        self.pending[eng] = []
        import inspect as _insp
        fr = _insp.currentframe().f_back
        if fr.f_code.co_name in ("dma", "cc", "act", "dve", "pool", "mk_mm"):
            fr = fr.f_back
        self.ops.append((eng, emit, sem, w, 16 if dma else 1, dma, fr.f_lineno))
        self.last_ev[eng] = (sem, idx)
        return (sem, idx)

    def describe(self, sem, lo, hi):
        semc = 0
        for (eng, emit, s, w, amt, dma, ln) in self.ops:
            if s != sem:
                continue
            semc += 1
            if lo <= semc <= hi:
                print(f"sem {sem} #{semc}: eng={eng} line={ln}")

    def full_barrier(self, extra=()):
        evs = [self.last_ev[e] for e in ENGINES if self.last_ev[e] is not None]
        evs = evs + [x for x in extra if x is not None]
        for e in ENGINES:
            self.pending[e] = list(evs)

    def emit_into(self, nc, ctx):
        waited = defaultdict(set)
        per_sem = defaultdict(list)
        semidx = defaultdict(int)
        op_semidx = []
        for k, (eng, emit, sem, waits, amt, dma, _ln) in enumerate(self.ops):
            op_semidx.append(semidx[sem])
            semidx[sem] += 1
            per_sem[sem].append((k, amt, dma))
            for (s, i) in waits:
                waited[s].add(i)
        need_inc = {}
        val_of = {}
        for sem, lst in per_sem.items():
            wset = waited[sem]
            cum = 0
            for j, (k, amt, dma) in enumerate(lst):
                inc = dma or (j in wset)
                need_inc[k] = inc
                if inc:
                    cum += amt
                val_of[(sem, j)] = cum
        sems = {s: ctx.enter_context(nc.semaphore(f"S_{s}")) for s in per_sem}

        def emit_engine(eng, engobj):
            lastwait = {}
            for k, (e, emit, sem, waits, amt, dma, _ln) in enumerate(self.ops):
                if e != eng:
                    continue
                wmax = {}
                for (s, i) in waits:
                    v = val_of[(s, i)]
                    if v > wmax.get(s, -1):
                        wmax[s] = v
                for s, v in wmax.items():
                    if v == 0 or lastwait.get(s, -1) >= v:
                        continue
                    engobj.wait_ge(sems[s], v)
                    lastwait[s] = v
                instr = emit(engobj)
                if need_inc[k]:
                    if instr is None:
                        raise RuntimeError(f"op {k} on {eng}/{sem} must inc but emitted None")
                    instr.then_inc(sems[sem], amt)
        return emit_engine


def build_program(cfg):
    T, D, NH, TC = cfg["T"], cfg["D"], cfg["NH"], cfg["TC"]
    DSL, HLOC, NCH = cfg["DSL"], cfg["HLOC"], cfg["NCH"]
    NDT, NDNT, NST, DTR = cfg["NDT"], cfg["NDNT"], cfg["NST"], cfg["DTR"]
    DN, DV, KT, G = cfg["DN"], cfg["DV"], cfg["KT"], cfg["G"]
    XPR = DTR + 2 * NST
    NKT = D // 128
    SCALE = cfg["DQ"] ** -0.5
    SHIFT = cfg["SHIFT"]
    RG = [list(range(NCORES))]

    nc = bass.Bass("TRN2", target_bir_lowering=False)
    P = Prog()
    globals()['LAST_PROG'] = P
    ctx = contextlib.ExitStack()

    x0 = nc.dram_tensor("x0", [DSL, T], F32, kind="ExternalInput").ap()
    out = nc.dram_tensor("xout", [DSL, T], F32, kind="ExternalOutput").ap()
    wq_d = nc.dram_tensor("wq_sl", [D, HLOC * DN], F32, kind="ExternalInput").ap()
    wk_d = nc.dram_tensor("wk_sl", [D, HLOC * DN], F32, kind="ExternalInput").ap()
    wv_d = nc.dram_tensor("wv_sl", [D, HLOC * DV], F32, kind="ExternalInput").ap()
    wo_d = nc.dram_tensor("wo_sl", [HLOC * DV, D], F32, kind="ExternalInput").ap()
    xpw_d = nc.dram_tensor("xpw_sl", [DSL, XPR], F32, kind="ExternalInput").ap()
    dtw_d = nc.dram_tensor("dtw_sl", [DTR, DSL], F32, kind="ExternalInput").ap()
    dtb_d = nc.dram_tensor("dtb_sl", [128, NDT], F32, kind="ExternalInput").ap()
    alog_d = nc.dram_tensor("alog_sl", [128, NDNT], F32, kind="ExternalInput").ap()
    dvec_d = nc.dram_tensor("dvec_sl", [128, NDT], F32, kind="ExternalInput").ap()
    wqt_d = nc.dram_tensor("wqt_sl", [D, DSL], U8, kind="ExternalInput").ap()
    scl_d = nc.dram_tensor("scl_sl", [D, DSL], F32, kind="ExternalInput").ap()
    zro_d = nc.dram_tensor("zro_sl", [D, DSL], F32, kind="ExternalInput").ap()
    nibsel_d = nc.dram_tensor("nibsel", [128, 2], F32, kind="ExternalInput").ap()
    rk_d = nc.dram_tensor("rk_all", [128, 16 * 128], BF16, kind="ExternalInput").ap()
    sk_d = nc.dram_tensor("sk_all", [128, 16 * 128], BF16, kind="ExternalInput").ap()
    rb_d = nc.dram_tensor("rb_bc", [32, 2 * 128], BF16, kind="ExternalInput").ap()
    tri_d = nc.dram_tensor("trimask", [128, 128], F32, kind="ExternalInput").ap()
    if KT == 4:
        mask4_d = nc.dram_tensor("mask4", [128, KT * TC], F32, kind="ExternalInput").ap()
    iden_d = nc.dram_tensor("ident", [128, 128], F32, kind="ExternalInput").ap()

    xi = [nc.dram_tensor(f"x{i}_int", [DSL, T], F32).ap() for i in (1, 2, 3)]
    xstream = [x0, xi[0], xi[1], xi[2], out]
    wqb_d = nc.dram_tensor("wqb", [D, HLOC * DN], BF16).ap()
    wkb_d = nc.dram_tensor("wkb", [D, HLOC * DN], BF16).ap()
    wvb_d = nc.dram_tensor("wvb", [D, HLOC * DV], BF16).ap()
    wob_d = nc.dram_tensor("wob", [HLOC * DV, D], BF16).ap()
    wtb_d = nc.dram_tensor("wtb", [D, DSL], BF16).ap()
    stats_c = nc.dram_tensor("stats_c", [1, NCH * 2 * TC], F32).ap()
    stats_g = nc.dram_tensor("stats_g", [1, NCH * 2 * TC], F32).ap()
    xp_c = nc.dram_tensor("xp_c", [NCH * XPR, TC], F32).ap()
    xp_g = nc.dram_tensor("xp_g", [NCH * XPR, TC], F32).ap()
    xln_c = nc.dram_tensor("xln_c", [NCH * DSL, TC], BF16).ap()
    xg_d = nc.dram_tensor("xg", [NCH * D, TC], BF16).ap()
    k_dram = nc.dram_tensor("k_dram", [HLOC * DN, T], BF16).ap()
    q_dram = nc.dram_tensor("q_dram", [HLOC * DN, T], BF16).ap()
    v_dram = nc.dram_tensor("v_dram", [HLOC, NCH, 128, KT * DV], BF16).ap()
    wop_c = nc.dram_tensor("wop_c", [NCH * D, TC], BF16).ap()
    wop_g = nc.dram_tensor("wop_g", [NCH * DSL, TC], BF16).ap()
    at_dram = nc.dram_tensor("at_dram", [64, 128], F32).ap()

    psa = ctx.enter_context(nc.psum_tensor("PSA", [128, 2048], F32))
    psb = ctx.enter_context(nc.psum_tensor("PSB", [128, 1024], F32))
    ps = {f"PS{i}": psa[:, 512 * i:512 * (i + 1)] for i in range(4)}
    ps["PS6"] = psb[:, 0:512]
    ps["PS7"] = psb[:, 512:1024]
    for i in (4, 5):
        ps[f"PS{i}"] = ctx.enter_context(nc.psum_tensor(f"PS{i}", [128, 512], F32))[:, :]
    ps_writer = {k: None for k in ps}
    ps_readers = {k: [] for k in ps}

    def mk_mm(slot, out_ap, lhsT, rhs, start, stop, waits=()):
        w = list(waits)
        if start:
            w += ps_readers[slot] + ([ps_writer[slot]] if ps_writer[slot] else [])
        ev = P.op("pe", lambda e, o=out_ap, l=lhsT, r=rhs, s1=start, s2=stop:
                  e.matmul(out=o, lhsT=l, rhs=r, start=s1, stop=s2), waits=w)
        if start:
            ps_writer[slot] = ev
            ps_readers[slot] = []
        else:
            ps_writer[slot] = ev
        return ev

    def ps_read(slot, ev):
        ps_readers[slot].append(ev)
        return ev

    def ps_write_dve(slot, ev):
        ps_writer[slot] = ev
        return ev

    def dma(stream, outap, inap, waits=(), eng="sync"):
        return P.op(eng, lambda e, o=outap, i=inap: e.dma_start(out=o, in_=i),
                    sem=stream, waits=waits, dma=True)

    def cc(kind, op, ins, outs, waits=()):
        return P.op("pool", lambda e, k=kind, o=op, i=ins, u=outs:
                    e.collective_compute(k, o, replica_groups=RG, ins=i, outs=u),
                    sem="cc", waits=waits, dma=False)

    def act(emit, waits=()):
        return P.op("act", emit, waits=waits)

    def dve(emit, waits=()):
        return P.op("dve", emit, waits=waits)

    def pool(emit, waits=()):
        return P.op("pool", emit, waits=waits)

    WAR = defaultdict(list)
    RSB_EV = {}

    def sb(stack, name, shape, dt, side=None):
        if side:
            return stack.enter_context(nc.sbuf_tensor(name, shape, dt, side=side))
        return stack.enter_context(nc.sbuf_tensor(name, shape, dt))

    rk_sb = sb(ctx, "rk_sb", [128, 16 * 128], BF16)
    sk_sb = sb(ctx, "sk_sb", [128, 16 * 128], BF16)
    rb_sb = sb(ctx, "rb_sb", [32, 2 * 128], BF16)
    tri_sb = sb(ctx, "tri_sb", [128, 128], F32)
    iden_sb = sb(ctx, "iden_sb", [128, 128], F32)

    rka_sb = sb(ctx, "rka_sb", [128, 0], BF16) if False else None
    xpw_sb = sb(ctx, "xpw_sb", [128, NDT * XPR], BF16)
    dtw_sb = sb(ctx, "dtw_sb", [DTR, DSL], BF16)
    dtb_sb = sb(ctx, "dtb_sb", [128, NDT], F32)
    a_sb = sb(ctx, "a_sb", [128, NDNT], F32)
    dvec_sb = sb(ctx, "dvec_sb", [128, NDT], F32)
    nibsel_sb = sb(ctx, "nibsel_sb", [128, 2], F32)
    ones_bf = sb(ctx, "ones_bf", [128, 1], BF16)
    ones_f1 = sb(ctx, "ones_f1", [1, 128], F32)
    ones_fp = sb(ctx, "ones_fp", [128, 1], F32)
    eps_sb = sb(ctx, "eps_sb", [1, 1], F32)
    one_sb = sb(ctx, "one_sb", [128, 1], F32)
    nshift_sb = sb(ctx, "nshift_sb", [128, 1], F32)
    carry_sb = sb(ctx, "carry_sb", [128, NDNT], F32)
    rka_sb = sb(ctx, "rka_sb", [128, NDNT * 128], BF16)
    x_sb = [sb(ctx, f"x_sb{p}", [128, NDT * TC], F32) for p in range(2)]
    xsq_sb = sb(ctx, "xsq_sb", [128, NDT * TC], F32)
    xlb_sb = [sb(ctx, f"xlb_sb{p}", [128, NDT * TC], BF16) for p in range(2)]
    st1_sb = [sb(ctx, f"st1_sb{p}", [1, 2 * TC], F32) for p in range(2)]
    st2_sb = [sb(ctx, f"st2_sb{p}", [1, TC], F32) for p in range(2)]
    rsm_sb = [sb(ctx, f"rsm_sb{p}", [1, 2 * TC], F32) for p in range(2)]
    rsb_sb = [sb(ctx, f"rsb_sb{p}", [128, 2 * TC], F32) for p in range(2)]
    stc_sb = [sb(ctx, f"stc_sb{p}", [1, 2 * TC], F32) for p in range(2)]
    ph0 = contextlib.ExitStack()
    at_sb = sb(ph0, "at_sb", [64, 128], F32, side="right")
    atf_sb = sb(ph0, "atf_sb", [1, 16 * 128], F32, side="right")
    cast_sb = sb(ph0, "cast_sb", [128, 1024], F32, side="right")
    castb_sb = sb(ph0, "castb_sb", [128, 1024], BF16, side="right")

    def stats_prepass(xin):
        xc = [[], []]
        st_evs = []
        ar_evs = []
        for c in range(NCH):
            p = c % 2
            ev_x = load_x(xin, c, p, waits=xc[p])
            e_sq = dve(lambda v, p=p: v.tensor_tensor(out=xsq_sb[:], in0=x_sb[p][:],
                                                      in1=x_sb[p][:], op=ALU.mult),
                       waits=[ev_x] + WAR["xsq"])
            ev1 = ev2 = None
            for a in range(NDT):
                ev1 = mk_mm("PS6", ps["PS6"][0:1, 0:TC], ones_fp[:, 0:1],
                            x_sb[p][:, a * TC:(a + 1) * TC], start=(a == 0),
                            stop=(a == NDT - 1), waits=[ev_x, EV["memset"]])
                ev2 = mk_mm("PS7", ps["PS7"][0:1, 0:TC], ones_fp[:, 0:1],
                            xsq_sb[:, a * TC:(a + 1) * TC], start=(a == 0),
                            stop=(a == NDT - 1), waits=[e_sq])
            WAR["xsq"] = [ev2]
            e_e1 = act(lambda a_, p=p: a_.copy(out=stc_sb[p][:, 0:TC], in_=ps["PS6"][0:1, 0:TC]),
                       waits=[ev1] + WAR[f"stc{p}"])
            ps_read("PS6", e_e1)
            e_e2 = act(lambda a_, p=p: a_.copy(out=stc_sb[p][:, TC:2 * TC], in_=ps["PS7"][0:1, 0:TC]),
                       waits=[ev2])
            ps_read("PS7", e_e2)
            e_st = dma(f"st_stats{p}", stats_c[:, c * 2 * TC:(c + 1) * 2 * TC], stc_sb[p][:, :],
                       waits=[e_e1, e_e2], eng="pool")
            WAR[f"stc{p}"] = [e_st]
            st_evs.append(e_st)
            xc[p] = [ev1, ev2]
            Q = max(1, NCH // 4)
            if (c + 1) % Q == 0:
                qi = c // Q
                ar_evs.append(cc("AllReduce", ALU.add,
                                 [stats_c[:, qi * Q * 2 * TC:(qi + 1) * Q * 2 * TC].opt()],
                                 [stats_g[:, qi * Q * 2 * TC:(qi + 1) * Q * 2 * TC].opt()],
                                 waits=st_evs))
                st_evs = []
        return ar_evs, xc

    def ln_norm(c, p, ev_x, e_ar, want_f32, xlf_sb=None, neng=None):
        neng = neng or dve
        reg = slice(c * 2 * TC, (c + 1) * 2 * TC)
        e_ld = dma(f"ld_stats{p}", stc_sb[p][:], stats_g[:, reg],
                   waits=[e_ar[c // max(1, NCH // 4)]] + WAR[f"stc{p}"])
        e_m = dve(lambda v, p=p: v.tensor_scalar(out=st1_sb[p][:], in0=stc_sb[p][:],
                                                 scalar1=1.0 / D, scalar2=None, op0=ALU.mult),
                  waits=[e_ld] + WAR[f"st1{p}"])
        WAR[f"stc{p}"] = [e_st for e_st in ()] + [e_m]
        e_m2 = dve(lambda v, p=p: v.tensor_tensor(out=st2_sb[p][:], in0=st1_sb[p][:, 0:TC],
                                                  in1=st1_sb[p][:, 0:TC], op=ALU.mult),
                   waits=[e_m] + WAR[f"st2{p}"])
        e_var = dve(lambda v, p=p: v.tensor_tensor(out=st2_sb[p][:], in0=st1_sb[p][:, TC:2 * TC],
                                                   in1=st2_sb[p][:], op=ALU.subtract),
                    waits=[e_m2])
        e_ln = act(lambda a_, p=p: a_.activation(out=st2_sb[p][:], in_=st2_sb[p][:],
                                                 func=AF.Ln, bias=eps_sb[:, :]), waits=[e_var])
        e_rs = act(lambda a_, p=p: a_.activation(out=rsm_sb[p][:, 0:TC], in_=st2_sb[p][:],
                                                 func=AF.Exp, scale=-0.5),
                   waits=[e_ln] + WAR[f"rsm{p}"])
        WAR[f"st2{p}"] = [e_rs]
        e_mrs = dve(lambda v, p=p: v.tensor_tensor(out=rsm_sb[p][:, TC:2 * TC],
                                                   in0=st1_sb[p][:, 0:TC],
                                                   in1=rsm_sb[p][:, 0:TC], op=ALU.mult),
                    waits=[e_rs])
        WAR[f"st1{p}"] = [e_mrs]
        e_b1 = mk_mm("PS6", ps["PS6"][0:128, 0:TC], ones_f1[0:1, :], rsm_sb[p][0:1, 0:TC],
                     start=True, stop=True, waits=[e_rs])
        e_b2 = mk_mm("PS7", ps["PS7"][0:128, 0:TC], ones_f1[0:1, :], rsm_sb[p][0:1, TC:2 * TC],
                     start=True, stop=True, waits=[e_mrs])
        e_eb1 = act(lambda a_, p=p: a_.copy(out=rsb_sb[p][:, 0:TC], in_=ps["PS6"][0:128, 0:TC]),
                    waits=[e_b1] + WAR[f"rsb{p}"])
        ps_read("PS6", e_eb1)
        e_eb2 = act(lambda a_, p=p: a_.copy(out=rsb_sb[p][:, TC:2 * TC], in_=ps["PS7"][0:128, 0:TC]),
                    waits=[e_b2])
        ps_read("PS7", e_eb2)
        WAR[f"rsm{p}"] = [e_eb1, e_eb2]
        RSB_EV[p] = (e_eb1, e_eb2)
        x3 = x_sb[p][:, :].rearrange("q (a t) -> q a t", a=NDT)
        rs3 = rsb_sb[p][:, 0:TC].unsqueeze(1).broadcast_to([128, NDT, TC])
        mrs3 = rsb_sb[p][:, TC:2 * TC].unsqueeze(1).broadcast_to([128, NDT, TC])
        if want_f32:
            f3 = xlf_sb[:, :].rearrange("q (a t) -> q a t", a=NDT)
            e_n1 = dve(lambda v: v.tensor_tensor(out=f3, in0=x3, in1=rs3, op=ALU.mult),
                       waits=[e_eb1, ev_x] + WAR["xlf"])
            e_n2 = dve(lambda v: v.tensor_tensor(out=f3, in0=f3, in1=mrs3, op=ALU.subtract),
                       waits=[e_eb2, e_n1])
            e_n3 = dve(lambda v, p=p: v.tensor_copy(out=xlb_sb[p][:], in_=xlf_sb[:]),
                       waits=[e_n2] + WAR[f"xlb{p}"])
            WAR[f"rsb{p}"] = [e_n2]
            return e_n3, e_n2, [e_n1]
        else:
            s3 = xsq_sb[:, :].rearrange("q (a t) -> q a t", a=NDT)
            e_n1 = neng(lambda v, s3=s3, x3=x3, rs3=rs3: v.tensor_tensor(
                out=s3, in0=x3, in1=rs3, op=ALU.mult),
                waits=[e_eb1, ev_x] + WAR["xsq"])
            b3 = xlb_sb[p][:, :].rearrange("p (a t) -> p a t", t=TC)
            e_n2 = neng(lambda v, b3=b3, s3=s3, mrs3=mrs3: v.tensor_tensor(
                out=b3, in0=s3, in1=mrs3, op=ALU.subtract),
                waits=[e_eb2, e_n1] + WAR[f"xlb{p}"])
            WAR["xsq"] = [e_n2]
            WAR[f"rsb{p}"] = [e_n2]
            return e_n2, None, [e_n1]

    def load_x(xdram, c, p, waits=(), eng="sync"):
        return dma(f"ld_x{p}", x_sb[p][:, :].rearrange("p (a t) -> p a t", t=TC),
                   xdram[:, c * TC:(c + 1) * TC].rearrange("(a p) t -> p a t", p=128),
                   waits=list(waits), eng=eng)

    EV = {}

    e = dve(lambda v: v.memset(ones_bf[:], 1.0))
    e = dve(lambda v: v.memset(ones_f1[:], 1.0), waits=[e])
    e = dve(lambda v: v.memset(ones_fp[:], 1.0), waits=[e])
    e = dve(lambda v: v.memset(eps_sb[:], 1e-5), waits=[e])
    e = dve(lambda v: v.memset(one_sb[:], 1.0), waits=[e])
    e = dve(lambda v: v.memset(nshift_sb[:], -SHIFT), waits=[e])
    EV["memset"] = e

    dma("ld_c", rk_sb[:], rk_d[:, :])
    dma("ld_c", sk_sb[:], sk_d[:, :])
    dma("ld_c", rb_sb[:], rb_d[:, :])
    dma("ld_c", tri_sb[:], tri_d[:, :])
    dma("ld_c", nibsel_sb[:], nibsel_d[:, :])
    dma("ld_c", dtb_sb[:], dtb_d[:, :])
    dma("ld_c", dvec_sb[:], dvec_d[:, :])
    dma("ld_c", iden_sb[:], iden_d[:, :])
    e_al = dma("ld_c", a_sb[:], alog_d[:, :])
    for kk_ in ("rk", "sk", "rb", "tri", "sh", "dtb", "dv"):
        EV[kk_] = e_al

    e_a1 = act(lambda a_: a_.activation(out=a_sb[:], in_=a_sb[:], func=AF.Exp), waits=[e_al])
    EV["A"] = dve(lambda v: v.tensor_scalar(out=a_sb[:], in0=a_sb[:], scalar1=-1.0,
                                            scalar2=None, op0=ALU.mult), waits=[e_a1])
    e_tr = P.op("pe", lambda e: e.transpose(out=ps["PS6"][0:NDNT, 0:128],
                                            in_=a_sb[:, 0:NDNT], identity=iden_sb[:]),
                waits=[EV["A"], EV["tri"]] + [w for w in (ps_writer["PS6"],) if w]
                + ps_readers["PS6"])
    ps_writer["PS6"], ps_readers["PS6"] = e_tr, []
    e_at = act(lambda a_: a_.copy(out=at_sb[0:NDNT, :], in_=ps["PS6"][0:NDNT, 0:128]),
               waits=[e_tr])
    ps_read("PS6", e_at)
    e_atd = dma("ld_at", at_dram[0:NDNT, :], at_sb[0:NDNT, :], waits=[e_at], eng="pool")
    e_rka_prev = None
    e_atf = None
    for k in range(NDNT):
        if k % 16 == 0:
            kw = min(16, NDNT - k)
            e_atf = dma("ld_atf", atf_sb[0:1, 0:kw * 128],
                        at_dram[k:k + kw, :].rearrange("a b -> (a b)").unsqueeze(0),
                        waits=[e_atd] + ([e_rka_prev] if e_rka_prev else []))
        e_bc0 = mk_mm("PS7", ps["PS7"][0:128, 0:128], ones_f1[0:1, :],
                      atf_sb[0:1, (k % 16) * 128:((k % 16) + 1) * 128],
                      start=True, stop=True, waits=[e_atf])
        e_rka = dve(lambda v, k=k: v.tensor_tensor(
            out=rka_sb[:, k * 128:(k + 1) * 128],
            in0=rk_sb[:, (k % 16) * 128:((k % 16) + 1) * 128],
            in1=ps["PS7"][0:128, 0:128], op=ALU.mult),
            waits=[e_bc0, EV["rk"]] + ([e_rka_prev] if e_rka_prev else []))
        ps_read("PS7", e_rka)
        e_rka_prev = e_rka
    EV["rka"] = e_rka_prev

    e_l = dma("ld_w0", cast_sb[0:DTR, 0:DSL], dtw_d[:, :])
    EV["dtw"] = dve(lambda v: v.tensor_copy(out=dtw_sb[:], in_=cast_sb[0:DTR, 0:DSL]),
                    waits=[e_l])
    e_l = dma("ld_w0", cast_sb[0:128, 0:NDT * XPR].rearrange("p (a r) -> p a r", r=XPR),
              xpw_d[:, :].rearrange("(a p) r -> p a r", p=128), waits=[EV["dtw"]])
    EV["xpw"] = dve(lambda v: v.tensor_copy(out=xpw_sb[:], in_=cast_sb[0:128, 0:NDT * XPR]),
                    waits=[e_l])

    cast2_sb = sb(ph0, "cast2_sb", [128, 1024], F32, side="right")
    castb2_sb = sb(ph0, "castb2_sb", [128, 1024], BF16, side="right")
    cbufs = [(cast_sb, castb_sb), (cast2_sb, castb2_sb)]
    cst = {"i": 0, "c": [EV["xpw"], None], "s": [None, None], "last": None}
    cast_blocks = []
    for (s_, d_, r_, c_) in ((wq_d, wqb_d, D, HLOC * DN), (wk_d, wkb_d, D, HLOC * DN),
                             (wv_d, wvb_d, D, HLOC * DV), (wo_d, wob_d, HLOC * DV, D)):
        for r0 in range(0, r_, 128):
            for c0 in range(0, c_, 1024):
                cast_blocks.append((s_, d_, r0, c0, min(1024, c_ - c0)))

    def emit_cast_block(k):
        src, dst, r0, c0, cw = cast_blocks[k]
        b = cst["i"] % 2
        cs_, cb_ = cbufs[b]
        evl = dma("ld_wc", cs_[0:128, 0:cw], src[r0:r0 + 128, c0:c0 + cw],
                  waits=[cst["c"][b]] if cst["c"][b] else [])
        evc = act(lambda a_, cw=cw, cs_=cs_, cb_=cb_: a_.copy(
            out=cb_[0:128, 0:cw], in_=cs_[0:128, 0:cw]),
            waits=[evl] + ([cst["s"][b]] if cst["s"][b] else []))
        evs_ = dma("st_wc", dst[r0:r0 + 128, c0:c0 + cw], cb_[0:128, 0:cw],
                   waits=[evc], eng="act")
        cst["c"][b], cst["s"][b] = evc, evs_
        cst["i"] += 1
        cst["last"] = evs_

    dqsets = [dict(
        dq8=sb(ph0, "dq8_0", [128, DSL], U8, side="right"),
        dqn=sb(ph0, "dqn_0", [128, DSL], U8, side="right"),
        dqn2=sb(ph0, "dqn2_0", [128, DSL], U8, side="right"),
        dqz=sb(ph0, "dqz_0", [128, DSL], F32, side="right"),
        dqsc=sb(ph0, "dqsc_0", [128, DSL], F32, side="right"),
        dqt=sb(ph0, "dqt_0", [128, DSL], F32, side="right"),
        dqb=sb(ph0, "dqb_0", [128, DSL], BF16, side="right"),
        prev=None)]

    def emit_dq_tile(a, lane, engf):
        S = dqsets[0]
        w0 = [S["prev"]] if S["prev"] else []
        e1 = dma("ld_dq", S["dq8"][:], wqt_d[a * 128:(a + 1) * 128, :], waits=w0)
        e2 = dma("ld_dq2", S["dqz"][:], zro_d[a * 128:(a + 1) * 128, :], waits=w0)
        e3 = dma("ld_dq3", S["dqsc"][:], scl_d[a * 128:(a + 1) * 128, :], waits=w0)
        e4 = engf(lambda v, S=S: v.tensor_scalar(out=S["dqn"][:], in0=S["dq8"][:],
                                                 scalar1=4, scalar2=15,
                                                 op0=ALU.logical_shift_right,
                                                 op1=ALU.bitwise_and),
                  waits=[e1] + w0)
        e4b = engf(lambda v, S=S: v.tensor_scalar(out=S["dqn2"][:], in0=S["dq8"][:],
                                                  scalar1=15, scalar2=None,
                                                  op0=ALU.bitwise_and),
                   waits=[e1] + w0)
        e4c = engf(lambda v, S=S: v.tensor_scalar(out=S["dqt"][:], in0=S["dqn2"][:],
                                                  scalar1=nibsel_sb[:, 1:2], scalar2=None,
                                                  op0=ALU.mult),
                   waits=[e4b, EV["sh"]] + w0)
        e4d = engf(lambda v, S=S: v.scalar_tensor_tensor(out=S["dqt"][:], in0=S["dqn"][:],
                                                         scalar=nibsel_sb[:, 0:1],
                                                         in1=S["dqt"][:], op0=ALU.mult,
                                                         op1=ALU.add),
                   waits=[e4, e4c])
        e5 = engf(lambda v, S=S: v.tensor_tensor(out=S["dqt"][:], in0=S["dqt"][:],
                                                 in1=S["dqz"][:],
                                                 op=ALU.subtract), waits=[e4d, e2])
        e6 = engf(lambda v, S=S: v.tensor_tensor(out=S["dqb"][:], in0=S["dqt"][:],
                                                 in1=S["dqsc"][:],
                                                 op=ALU.mult), waits=[e5, e3])
        e7 = dma("st_dq", wtb_d[a * 128:(a + 1) * 128, :], S["dqb"][:], waits=[e6],
                 eng="act" if lane == 0 else "pool")
        S["prev"] = e7

    with contextlib.ExitStack() as ms:
        xlf_sb = sb(ms, "xlf_sb", [128, NDT * TC], F32, side="right")
        xp1_sb = [sb(ms, f"xp1_{p}", [DTR, TC], F32, side="right") for p in range(2)]
        xp2_sb = [sb(ms, f"xp2_{p}", [32, TC], F32, side="right") for p in range(2)]
        xpg1_sb = [sb(ms, f"xpg1_{p}", [DTR, TC], BF16, side="right") for p in range(2)]
        xpbc_sb = [sb(ms, f"xpbc_{p}", [32, TC], BF16, side="right") for p in range(2)]
        dt_sb = sb(ms, "dt_sb", [128, NDT * TC], BF16, side="right")
        dtx_sb = sb(ms, "dtx_sb", [128, NDT * TC], BF16, side="right")
        sp_sb = sb(ms, "sp_sb", [128, TC], F32, side="right")
        brep_sb = [sb(ms, f"brep_{p}", [128, TC], F32, side="right") for p in range(2)]
        crep_sb = [sb(ms, f"crep_{p}", [128, TC], F32, side="right") for p in range(2)]
        da_sb = [sb(ms, f"da_{q}", [128, 2 * TC], F32, side="right") for q in range(2)]
        dbx_sb = [sb(ms, f"dbx_{q}", [128, 2 * TC], F32, side="right") for q in range(2)]
        h_sb = [sb(ms, f"h_{q}", [128, 2 * TC], F32, side="right") for q in range(2)]
        hc_sb = [sb(ms, f"hc_{q}", [128, 2 * TC], BF16, side="right") for q in range(4)]
        mam_sb = sb(ms, "mam_sb", [128, TC], F32, side="right")

        e_ar0, xcons = stats_prepass(xstream[0])
        m_stores = []
        ar_xp = [None] * NCH
        evx = [None] * NCH

        def xp_stage(cs):
            pc_ = cs % 2
            ev_x = load_x(xstream[0], cs, pc_, waits=xcons[pc_])
            xcons[pc_] = []
            evx[cs] = ev_x
            ev_xlnb, _, xreaders = ln_norm(cs, pc_, ev_x, e_ar0, want_f32=False, neng=pool)
            e_xp1 = e_xp2 = None
            for a in range(NDT):
                e_xp1 = mk_mm("PS5", ps["PS5"][0:DTR, 0:TC],
                              xpw_sb[:, a * XPR:a * XPR + DTR],
                              xlb_sb[pc_][:, a * TC:(a + 1) * TC], start=(a == 0),
                              stop=(a == NDT - 1), waits=[ev_xlnb, EV["xpw"]])
                e_xp2 = mk_mm("PS7", ps["PS7"][0:32, 0:TC],
                              xpw_sb[:, a * XPR + DTR:(a + 1) * XPR],
                              xlb_sb[pc_][:, a * TC:(a + 1) * TC], start=(a == 0),
                              stop=(a == NDT - 1), waits=[])
            e_v1 = act(lambda a_, pc_=pc_: a_.copy(out=xp1_sb[pc_][:], in_=ps["PS5"][0:DTR, 0:TC]),
                       waits=[e_xp1] + WAR[f"xp1{pc_}"])
            ps_read("PS5", e_v1)
            e_v2 = act(lambda a_, pc_=pc_: a_.copy(out=xp2_sb[pc_][:], in_=ps["PS7"][0:32, 0:TC]),
                       waits=[e_xp2] + WAR[f"xp2{pc_}"])
            ps_read("PS7", e_v2)
            xpb_ = cs * XPR
            e_s1 = dma("st_xp", xp_c[xpb_:xpb_ + DTR, :], xp1_sb[pc_][:], waits=[e_v1], eng="act")
            e_s2 = dma("st_xp", xp_c[xpb_ + DTR:xpb_ + XPR, :], xp2_sb[pc_][:], waits=[e_v2], eng="act")
            ar_xp[cs] = cc("AllReduce", ALU.add, [xp_c[xpb_:xpb_ + XPR, :].opt()],
                           [xp_g[xpb_:xpb_ + XPR, :].opt()], waits=[e_s1, e_s2])
            WAR[f"xp1{pc_}"] = [e_s1]
            WAR[f"xp2{pc_}"] = [e_s2]
            WAR[f"xlb{pc_}"] = [e_xp1, e_xp2]
            xcons[pc_] = xreaders

        xp_stage(0)
        CBN = len(cast_blocks)
        for c in range(NCH):
            p = c % 2
            creg = slice(c * TC, (c + 1) * TC)
            xpb = c * XPR
            e_l1 = dma(f"ld_xpg{p}", xp1_sb[p][:], xp_g[xpb:xpb + DTR, :],
                       waits=[ar_xp[c]] + WAR[f"xp1{p}"])
            e_l2 = dma(f"ld_xpg2{p}", xp2_sb[p][:], xp_g[xpb + DTR:xpb + XPR, :],
                       waits=[ar_xp[c]] + WAR[f"xp2{p}"])
            WAR[f"xp1{p}"] = [e_l1]
            WAR[f"xp2{p}"] = [e_l2]
            if c + 1 < NCH:
                xp_stage(c + 1)
            ev_x = evx[c]
            e_c1 = act(lambda a_, p=p: a_.copy(out=xpg1_sb[p][:], in_=xp1_sb[p][:]),
                       waits=[e_l1] + WAR[f"xpg1{p}"])
            e_c2 = act(lambda a_, p=p: a_.copy(out=xpbc_sb[p][:], in_=xp2_sb[p][:]),
                       waits=[e_l2] + WAR[f"xpbc{p}"])
            WAR[f"xp1{p}"] += [e_c1]
            WAR[f"xp2{p}"] += [e_c2]
            e_bb = mk_mm("PS7", ps["PS7"][0:128, 0:TC], rb_sb[:, 0:128], xpbc_sb[p][:],
                         start=True, stop=True, waits=[e_c2, EV["rb"]])
            e_bc2 = mk_mm("PS5", ps["PS5"][0:128, 0:TC], rb_sb[:, 128:256], xpbc_sb[p][:],
                          start=True, stop=True, waits=[e_c2])
            e_brep = act(lambda a_, p=p: a_.copy(out=brep_sb[p][:], in_=ps["PS7"][0:128, 0:TC]),
                         waits=[e_bb] + WAR[f"brep{p}"])
            ps_read("PS7", e_brep)
            e_crep = act(lambda a_, p=p: a_.copy(out=crep_sb[p][:], in_=ps["PS5"][0:128, 0:TC]),
                         waits=[e_bc2] + WAR[f"crep{p}"])
            ps_read("PS5", e_crep)
            WAR[f"xpbc{p}"] = [e_bb, e_bc2]
            dt_evs = []
            for a in range(NDT):
                e_mm = mk_mm("PS7", ps["PS7"][0:128, 0:TC], dtw_sb[:, a * 128:(a + 1) * 128],
                             xpg1_sb[p][:], start=True, stop=True,
                             waits=[e_c1, EV["dtw"]])
                e_p1 = act(lambda a_, a2=a: a_.activation(out=sp_sb[:], in_=ps["PS7"][0:128, 0:TC],
                                                          func=AF.Abs, bias=dtb_sb[:, a2:a2 + 1]),
                           waits=[e_mm, EV["dtb"]] + WAR["sp"])
                e_p2 = act(lambda a_: a_.activation(out=sp_sb[:], in_=sp_sb[:],
                                                    func=AF.Exp, scale=-1.0), waits=[e_p1])
                e_p3 = act(lambda a_: a_.activation(out=sp_sb[:], in_=sp_sb[:],
                                                    func=AF.Ln, bias=one_sb[:, :]),
                           waits=[e_p2])
                e_p4 = act(lambda a_, a2=a: a_.activation(out=dt_sb[:, a2 * TC:(a2 + 1) * TC],
                                                          in_=ps["PS7"][0:128, 0:TC],
                                                          func=AF.Relu, bias=dtb_sb[:, a2:a2 + 1]),
                           waits=[e_p3] + WAR["dt"])
                ps_read("PS7", e_p4)
                e_p5 = pool(lambda v, a2=a: v.tensor_tensor(out=dt_sb[:, a2 * TC:(a2 + 1) * TC],
                                                            in0=dt_sb[:, a2 * TC:(a2 + 1) * TC],
                                                            in1=sp_sb[:], op=ALU.add),
                            waits=[e_p4])
                WAR["sp"] = [e_p5]
                dt_evs.append(e_p5)
            WAR["dt"] = []
            e_dtbf = dt_evs[-1]
            rs3 = rsb_sb[p][:, 0:TC].unsqueeze(1).broadcast_to([128, NDT, TC])
            mrs3 = rsb_sb[p][:, TC:2 * TC].unsqueeze(1).broadcast_to([128, NDT, TC])
            x3 = x_sb[p][:, :].rearrange("q (a t) -> q a t", a=NDT)
            f3 = xlf_sb[:, :].rearrange("q (a t) -> q a t", a=NDT)
            e_f1 = pool(lambda v, f3=f3, x3=x3, rs3=rs3: v.tensor_tensor(
                out=f3, in0=x3, in1=rs3, op=ALU.mult),
                waits=[RSB_EV[p][0], ev_x] + WAR["xlf"])
            e_f2 = pool(lambda v, f3=f3, mrs3=mrs3: v.tensor_tensor(
                out=f3, in0=f3, in1=mrs3, op=ALU.subtract),
                waits=[RSB_EV[p][1], e_f1])
            WAR[f"rsb{p}"] += [e_f2]
            e_dtx = pool(lambda v: v.tensor_tensor(out=dtx_sb[:], in0=dt_sb[:],
                                                   in1=xlf_sb[:], op=ALU.mult),
                         waits=dt_evs + [e_f2] + WAR["dtx"])
            WAR["dt"] = [e_dtx] + WAR["dtbf"]
            y_done = []
            pend_y = []
            pend_cr = []

            def flush_y(pend, upto):
                evs = []
                while len(pend) > upto:
                    (e_hc2, q2, k02, k12, jj2) = pend.pop(0)
                    e_ya = mk_mm("PS4", ps["PS4"][0:128, 0:TC],
                                 sk_sb[:, (k02 % 16) * 128:((k02 % 16) + 1) * 128],
                                 hc_sb[q2][:, 0:TC], start=(jj2 == 0), stop=False,
                                 waits=[e_hc2, EV["sk"]])
                    e_yb = mk_mm("PS4", ps["PS4"][0:128, 0:TC],
                                 sk_sb[:, (k12 % 16) * 128:((k12 % 16) + 1) * 128],
                                 hc_sb[q2][:, TC:2 * TC], start=False, stop=(jj2 == 7),
                                 waits=[e_hc2])
                    WAR[f"hc{q2}"] = [e_yb]
                    evs.append(e_yb)
                return evs

            for a in range(NDT):
                e_y = None
                for jj in range(8):
                    k0 = a * 16 + 2 * jj
                    k1 = k0 + 1
                    q = jj % 2
                    qh = jj % 4
                    dA_, dB_ = ("PS0", "PS1") if jj % 2 == 0 else ("PS6", "PS7")
                    dsrc = psa[0:128, 0:1024] if jj % 2 == 0 else psb[0:128, 0:1024]
                    e_tp0 = mk_mm(dA_, ps[dA_][0:128, 0:TC],
                                  rka_sb[:, k0 * 128:(k0 + 1) * 128],
                                  dt_sb[:, a * TC:(a + 1) * TC], start=True, stop=True,
                                  waits=[e_dtbf, EV["rka"]])
                    e_tp1 = mk_mm(dB_, ps[dB_][0:128, 0:TC],
                                  rka_sb[:, k1 * 128:(k1 + 1) * 128],
                                  dt_sb[:, a * TC:(a + 1) * TC], start=True, stop=True,
                                  waits=[e_dtbf])
                    e_tq0 = mk_mm("PS2", ps["PS2"][0:128, 0:TC],
                                  rk_sb[:, (k0 % 16) * 128:((k0 % 16) + 1) * 128],
                                  dtx_sb[:, a * TC:(a + 1) * TC], start=True, stop=True,
                                  waits=[e_dtx, EV["rk"]])
                    e_tq1 = mk_mm("PS3", ps["PS3"][0:128, 0:TC],
                                  rk_sb[:, (k1 % 16) * 128:((k1 % 16) + 1) * 128],
                                  dtx_sb[:, a * TC:(a + 1) * TC], start=True, stop=True,
                                  waits=[e_dtx])
                    e_da = act(lambda a_, q=q, dsrc=dsrc: a_.activation(
                        out=da_sb[q][:, :].rearrange("p (b c) -> p b c", c=TC),
                        in_=dsrc.rearrange("p (b c) -> p b c", c=512)[:, :, 0:TC],
                        func=AF.Exp), waits=[e_tp0, e_tp1] + WAR[f"da{q}"])
                    ps_read(dA_, e_da)
                    ps_read(dB_, e_da)
                    e_dbx = dve(lambda v, q=q, p=p: v.tensor_tensor(
                        out=dbx_sb[q][:, :].rearrange("p (b c) -> p b c", c=TC),
                        in0=psa[0:128, 1024:2048].rearrange("p (b c) -> p b c", c=512)[:, :, 0:TC],
                        in1=brep_sb[p][:, :].unsqueeze(1).broadcast_to([128, 2, TC]),
                        op=ALU.mult), waits=[e_tq0, e_tq1, e_brep] + WAR[f"dbx{q}"])
                    ps_read("PS2", e_dbx)
                    ps_read("PS3", e_dbx)
                    init0 = 0.0 if c == 0 else carry_sb[:, k0:k0 + 1]
                    init1 = 0.0 if c == 0 else carry_sb[:, k1:k1 + 1]
                    cw0 = WAR.get(f"carry{k0}", []) if c > 0 else []
                    cw1 = WAR.get(f"carry{k1}", []) if c > 0 else []
                    e_h0 = dve(lambda v, q=q, i=init0: v.tensor_tensor_scan(
                        out=h_sb[q][:, 0:TC], data0=da_sb[q][:, 0:TC], data1=dbx_sb[q][:, 0:TC],
                        initial=i, op0=ALU.mult, op1=ALU.add),
                        waits=[e_da, e_dbx] + WAR[f"h{q}"] + cw0)
                    e_h1 = dve(lambda v, q=q, i=init1: v.tensor_tensor_scan(
                        out=h_sb[q][:, TC:2 * TC], data0=da_sb[q][:, TC:2 * TC],
                        data1=dbx_sb[q][:, TC:2 * TC],
                        initial=i, op0=ALU.mult, op1=ALU.add),
                        waits=[e_da, e_dbx] + cw1)
                    pend_cr.append((q, k0, e_h0, e_h1))
                    if len(pend_cr) > 1:
                        (q2, k02, x_h0, x_h1) = pend_cr.pop(0)
                        e_cr = act(lambda a_, q=q2, k=k02: a_.copy(
                            out=carry_sb[:, k:k + 2].unsqueeze(2),
                            in_=h_sb[q][:, :].rearrange("p (b t) -> p b t", t=TC)[:, :, TC - 1:TC]),
                            waits=[x_h0, x_h1])
                        WAR[f"carry{k02}"] = [e_cr]
                        WAR[f"carry{k02 + 1}"] = [e_cr]
                        WAR[f"hcr{q2}"] = [e_cr]
                    e_hc = pool(lambda g, q=q, qh=qh, p=p: g.tensor_tensor(
                        out=hc_sb[qh][:, :].rearrange("p (b c) -> p b c", c=TC),
                        in0=h_sb[q][:, :].rearrange("p (b c) -> p b c", c=TC),
                        in1=crep_sb[p][:, :].unsqueeze(1).broadcast_to([128, 2, TC]),
                        op=ALU.mult), waits=[e_h0, e_h1, e_crep] + WAR[f"hc{qh}"])
                    pend_y.append((e_hc, qh, k0, k1, jj))
                    ys = flush_y(pend_y, 2)
                    if ys:
                        e_y = ys[-1]
                    WAR[f"da{q}"] = [e_h0, e_h1]
                    WAR[f"dbx{q}"] = [e_h0, e_h1]
                    WAR[f"h{q}"] = [e_hc] + WAR[f"hcr{q}"]
                    e_tp_last, e_tq_last = e_tp1, e_tq1
                    e_dbx_last, e_hc_last = e_dbx, e_hc
                ys = flush_y(pend_y, 0)
                if ys:
                    e_y = ys[-1]
                e_ev = dve(lambda v, a=a, p=p: v.scalar_tensor_tensor(
                    out=mam_sb[:], in0=xlf_sb[:, a * TC:(a + 1) * TC],
                    scalar=dvec_sb[:, a:a + 1], in1=ps["PS4"][0:128, 0:TC],
                    op0=ALU.mult, op1=ALU.add),
                    waits=[e_y, e_f2, EV["dv"]] + WAR["mam"])
                ps_read("PS4", e_ev)
                e_add = dve(lambda v, a=a, p=p: v.tensor_tensor(
                    out=x_sb[p][:, a * TC:(a + 1) * TC], in0=x_sb[p][:, a * TC:(a + 1) * TC],
                    in1=mam_sb[:], op=ALU.add), waits=[e_ev])
                WAR["mam"] = [e_add]
                y_done.append(e_add)
            while pend_cr:
                (q2, k02, x_h0, x_h1) = pend_cr.pop(0)
                e_cr = act(lambda a_, q=q2, k=k02: a_.copy(
                    out=carry_sb[:, k:k + 2].unsqueeze(2),
                    in_=h_sb[q][:, :].rearrange("p (b t) -> p b t", t=TC)[:, :, TC - 1:TC]),
                    waits=[x_h0, x_h1])
                WAR[f"carry{k02}"] = [e_cr]
                WAR[f"carry{k02 + 1}"] = [e_cr]
                WAR[f"hcr{q2}"] = [e_cr]
            e_stx = dma(f"st_x{p}", xstream[1][:, creg].rearrange("(a p) t -> p a t", p=128),
                        x_sb[p][:, :].rearrange("p (a t) -> p a t", t=TC), waits=y_done, eng="act")
            m_stores.append(e_stx)
            xcons[p] += y_done + [e_stx]
            WAR["xlf"] = [e_dtx] + y_done
            WAR["dtbf"] = [e_tp_last]
            WAR["dt"] = WAR["dt"] + [e_tp_last]
            WAR["dtx"] = [e_tq_last]
            WAR[f"brep{p}"] = [e_dbx_last]
            WAR[f"crep{p}"] = [e_hc_last]
            WAR[f"xpg1{p}"] = [e_mm]
            for k_ in range(c * CBN // NCH, (c + 1) * CBN // NCH):
                emit_cast_block(k_)
            emit_dq_tile(2 * c, 0, dve)
            emit_dq_tile(2 * c + 1, 0, dve)
        P.full_barrier(m_stores + [cst["last"], dqsets[0]["prev"]])
    ph0.close()

    AGB = 2 if NCH % 2 == 0 else 1

    def ln_ag_pass(xin, xcons_unused):
        e_ar, xcons = stats_prepass(xin)
        ag_evs = []
        pend_st = []
        for c in range(NCH):
            p = c % 2
            ev_x = load_x(xin, c, p, waits=xcons[p])
            ev_xlnb, _, xreaders = ln_norm(c, p, ev_x, e_ar, want_f32=False)
            e_st = dma(f"st_xln{c % 2}", xln_c[c * DSL:(c + 1) * DSL, :].rearrange("(a p) t -> p a t", p=128),
                       xlb_sb[p][:, :].rearrange("p (a t) -> p a t", t=TC),
                       waits=[ev_xlnb], eng="pool")
            pend_st.append(e_st)
            if (c + 1) % AGB == 0:
                g = c // AGB
                e_ag = cc("AllGather", ALU.bypass,
                          [xln_c[g * AGB * DSL:(g + 1) * AGB * DSL, :].opt()],
                          [xg_d[g * AGB * D:(g + 1) * AGB * D, :].opt()], waits=pend_st)
                pend_st = []
                ag_evs += [e_ag] * AGB
            WAR[f"xlb{p}"] = [e_st]
            xcons[p] = [ev_xlnb] + xreaders
        return ag_evs, xcons

    def ffn_phase(xin, xout, tag):
        with contextlib.ExitStack() as fs:
            xg_sb = [sb(fs, f"xg_{tag}{p}", [128, NKT * TC], BF16, side="right") for p in range(2)]
            wt_sb = sb(fs, f"wt_{tag}", [128, NKT * DSL], BF16, side="right")
            e_wt = dma("ld_wt" + tag, wt_sb[:, :].rearrange("p (a o) -> p a o", o=DSL),
                       wtb_d[:, :].rearrange("(a p) o -> p a o", p=128))
            ag_evs, xcons = ln_ag_pass(xin, [[], []])
            stores = []
            for c in range(NCH):
                p = c % 2
                ev_x = load_x(xin, c, p, waits=xcons[p])
                xcons[p] = []
                ldg_evs_ = []
                g, cc_ = c // AGB, c % AGB
                for r_ in range(NCORES):
                    base_ = g * AGB * D + r_ * AGB * DSL + cc_ * DSL
                    e_ldg = dma(f"ld_xg{p}_{r_}",
                                xg_sb[p][:, r_ * NDT * TC:(r_ + 1) * NDT * TC].rearrange(
                                    "p (q t) -> p q t", t=TC),
                                xg_d[base_:base_ + DSL, :].rearrange("(q p) t -> p q t", p=128),
                                waits=[ag_evs[c]] + WAR[f"xg{p}{r_}"])
                    WAR[f"xg{p}{r_}"] = []
                    ldg_evs_.append((r_, e_ldg))
                ldg_map = dict(ldg_evs_)
                res_evs = []
                for o in range(NDT):
                    slot = f"PS{o % 4}"
                    e_mm = None
                    for a in range(NKT):
                        e_mm = mk_mm(slot, ps[slot][0:128, 0:TC],
                                     wt_sb[:, a * DSL + o * 128:a * DSL + (o + 1) * 128],
                                     xg_sb[p][:, a * TC:(a + 1) * TC],
                                     start=(a == 0), stop=(a == NKT - 1),
                                     waits=[ldg_map[a // NDT], e_wt])
                    e_res = dve(lambda v, o=o, p=p, slot=slot: v.tensor_tensor(
                        out=x_sb[p][:, o * TC:(o + 1) * TC],
                        in0=x_sb[p][:, o * TC:(o + 1) * TC],
                        in1=ps[slot][0:128, 0:TC], op=ALU.add), waits=[e_mm, ev_x])
                    ps_read(slot, e_res)
                    res_evs.append(e_res)
                creg = slice(c * TC, (c + 1) * TC)
                e_stx = dma(f"st_x{p}", xout[:, creg].rearrange("(a p) t -> p a t", p=128),
                            x_sb[p][:, :].rearrange("p (a t) -> p a t", t=TC),
                            waits=res_evs, eng="pool")
                stores.append(e_stx)
                xcons[p] = res_evs + [e_stx]
                for r_ in range(NCORES):
                    WAR[f"xg{p}{r_}"] = res_evs
            P.full_barrier(stores)

    ffn_phase(xstream[1], xstream[2], "f1")

    with contextlib.ExitStack() as js:
        xg_sb = [sb(js, f"xg_pj{p}", [128, NKT * TC], BF16, side="right") for p in range(2)]
        sl1_sb = [sb(js, f"sl1_{i}", [128, 2 * HLOC * DN], BF16, side="right") for i in range(4)]
        sl2_sb = [sb(js, f"sl2_{i}", [128, HLOC * DV], BF16, side="right") for i in range(4)]
        prj_sb = [sb(js, f"prj_{i}", [128, TC], BF16, side="right") for i in range(2)]
        vprj_sb = [sb(js, f"vprj_{i}", [128, HLOC * DV], BF16, side="right") for i in range(2)]

        ag_evs, xcons = ln_ag_pass(xstream[2], [[], []])
        prj_stores = []
        for c in range(NCH):
            p = c % 2
            creg = slice(c * TC, (c + 1) * TC)
            ldg_evs_ = []
            g, cc_ = c // AGB, c % AGB
            for r_ in range(NCORES):
                base_ = g * AGB * D + r_ * AGB * DSL + cc_ * DSL
                e_ldg = dma(f"ld_xg{p}_{r_}",
                            xg_sb[p][:, r_ * NDT * TC:(r_ + 1) * NDT * TC].rearrange(
                                "p (q t) -> p q t", t=TC),
                            xg_d[base_:base_ + DSL, :].rearrange("(q p) t -> p q t", p=128),
                            waits=[ag_evs[c]] + WAR[f"xg{p}{r_}"])
                WAR[f"xg{p}{r_}"] = []
                ldg_evs_.append((r_, e_ldg))
            ldg_map = dict(ldg_evs_)
            cons = []
            e_mm = None
            for a in range(NKT):
                pp = a % 4
                e_s1 = dma(f"ld_sl1{pp}", sl1_sb[pp][:, 0:HLOC * DN],
                           wkb_d[a * 128:(a + 1) * 128, :], waits=WAR[f"sl1{pp}"])
                e_s2 = dma(f"ld_sl1b{pp}", sl1_sb[pp][:, HLOC * DN:2 * HLOC * DN],
                           wqb_d[a * 128:(a + 1) * 128, :], waits=WAR[f"sl1{pp}"])
                WAR[f"sl1{pp}"] = []
                for ki in range(2):
                    for hh in range(HLOC):
                        slot = f"PS{ki * 4 + hh}"
                        e_mm = mk_mm(slot, ps[slot][0:128, 0:TC],
                                     sl1_sb[pp][:, (ki * HLOC + hh) * DN:(ki * HLOC + hh + 1) * DN],
                                     xg_sb[p][:, a * TC:(a + 1) * TC],
                                     start=(a == 0), stop=(a == NKT - 1),
                                     waits=[e_s1, e_s2, ldg_map[a // NDT]])
                        WAR[f"sl1{pp}"].append(e_mm)
            for ki, odram in ((0, k_dram), (1, q_dram)):
                for hh in range(HLOC):
                    slot = f"PS{ki * 4 + hh}"
                    bi = (ki * HLOC + hh) % 2
                    e_evac = act(lambda a_, slot=slot, bi=bi: a_.copy(
                        out=prj_sb[bi][:], in_=ps[slot][0:128, 0:TC]),
                        waits=[e_mm] + WAR[f"prj{bi}"])
                    ps_read(slot, e_evac)
                    e_sto = dma(f"st_prj{bi}", odram[hh * DN:(hh + 1) * DN, creg],
                                prj_sb[bi][:], waits=[e_evac], eng="pool")
                    WAR[f"prj{bi}"] = [e_sto]
                    cons.append(e_sto)
            e_mmv = None
            for a in range(NKT):
                pp = a % 4
                e_sv = dma(f"ld_sl2{pp}", sl2_sb[pp][:],
                           wvb_d[a * 128:(a + 1) * 128, :], waits=WAR[f"sl2{pp}"])
                WAR[f"sl2{pp}"] = []
                for tt in range(KT):
                    slot = f"PS{tt}"
                    e_mmv = mk_mm(slot, ps[slot][0:128, 0:HLOC * DV],
                                  xg_sb[p][:, a * TC + tt * 128:a * TC + (tt + 1) * 128],
                                  sl2_sb[pp][:],
                                  start=(a == 0), stop=(a == NKT - 1),
                                  waits=[e_sv, ldg_map[a // NDT]])
                    WAR[f"sl2{pp}"].append(e_mmv)
            for tt in range(KT):
                slot = f"PS{tt}"
                bi = tt % 2
                e_evv = act(lambda a_, slot=slot, bi=bi: a_.copy(
                    out=vprj_sb[bi][:], in_=ps[slot][0:128, 0:HLOC * DV]),
                    waits=[e_mmv] + WAR[f"vprj{bi}"])
                ps_read(slot, e_evv)
                e_stv = None
                for hh in range(HLOC):
                    e_stv = dma(f"st_vprj{bi}",
                                v_dram[hh, c, :, tt * DV:(tt + 1) * DV],
                                vprj_sb[bi][:, hh * DV:(hh + 1) * DV], waits=[e_evv], eng="pool")
                WAR[f"vprj{bi}"] = [e_stv]
                cons.append(e_stv)
            prj_stores += cons
            for r_ in range(NCORES):
                WAR[f"xg{p}{r_}"] = [e_mm, e_mmv]
        P.full_barrier(prj_stores)

    with contextlib.ExitStack() as as_:
        q_sb = [sb(as_, f"q_at{h}", [128, TC], BF16, side="right") for h in range(HLOC)]
        k_sb = [sb(as_, f"k_at{p}", [128, TC], BF16, side="right") for p in range(6)]
        v_sb = [sb(as_, f"v_at{p}", [128, KT * 128], BF16, side="right") for p in range(6)]
        if KT == 4:
            pbF_sb = [sb(as_, f"pbF{r}", [128, 4 * TC], BF16, side="right")
                      for r in range(4)]
            ssA_sb = [sb(as_, f"ssA{r}", [128, TC], BF16, side="right") for r in range(4)]
            ssB_sb = [sb(as_, f"ssB{r}", [128, 2 * TC], BF16, side="right") for r in range(4)]
            pb_sb = None
            mask4_sb = sb(as_, "mask4_sb", [128, KT * TC], F32, side="right")
            EV["mask4"] = dma("ld_c", mask4_sb[:], mask4_d[:, :])
            ev_pz = EV["memset"]
        else:
            pbA_sb = sb(as_, "pbA_at", [128, 4 * TC], BF16, side="right")
            pbB_sb = [sb(as_, f"pbB_at{p}", [128, 2 * TC], BF16, side="right") for p in range(2)]
            pb_sb = [pbA_sb[:, 0:2 * TC], pbA_sb[:, 2 * TC:4 * TC], pbB_sb[0][:, :], pbB_sb[1][:, :]]
            pdiag_sb = [sb(as_, f"pd_at{t}", [128, TC], BF16, side="right") for t in range(KT)]
            e_z = EV["memset"]
            for t in range(KT):
                e_z = dve(lambda v, t=t: v.memset(pdiag_sb[t][:], 0.0), waits=[e_z])
            ev_pz = e_z
        invd_sb = [sb(as_, f"invd{i}", [1, TC], F32, side="right") for i in range(2)]
        invb_sb = [sb(as_, f"invb{i}", [128, TC], F32, side="right") for i in range(2)]
        attn_sb = [sb(as_, f"attn_{h}", [128, TC], BF16, side="right") for h in range(HLOC)]
        wo_sb = sb(as_, "wo_sb", [128, HLOC * D], BF16, side="right")
        wop_sb = [sb(as_, f"wop_{i}", [128, TC], BF16, side="right") for i in range(2)]
        rs_sb = [sb(as_, f"rs_{p}", [128, NDT * TC], BF16, side="right") for p in range(2)]

        e_wo = dma("ld_wo_res", wo_sb[:, :].rearrange("p (h d) -> p h d", d=D),
                   wob_d[:, :].rearrange("(h p) d -> p h d", p=128))

        SETS = [("PS0", "PS1"), ("PS2", "PS3"), ("PS6", "PS7")]
        SETBASE = [(psa, 0), (psa, 1024), (psb, 0)]
        attn_rot = [0]
        att_stores = []
        xcons = [[], []]
        for qc in range(NCH):
            p = qc % 2
            ev_x = load_x(xstream[2], qc, p, waits=xcons[p], eng="pool")
            xcons[p] = []
            attn_ready = []
            e_qs = {}
            for hh in range(HLOC):
                e_qs[hh] = dma(f"ld_q{hh}", q_sb[hh][:],
                               q_dram[hh * DN:(hh + 1) * DN, qc * TC:(qc + 1) * TC],
                               waits=WAR[f"qat{hh}"])
                WAR[f"qat{hh}"] = []
            for hh in range(HLOC):
                hb = hh % 2 if KT == 4 else 0
                PVS = ("PS4", "PS6")[hb]
                DENS = ("PS5", "PS7")[hb]
                e_q = e_qs[hh]
                first = [True]
                e_pv = e_den = None
                pend_pv = []

                den_first = [True]
                pend_den = []

                def flush_pv(pend, upto):
                    nonlocal e_pv, e_den
                    while len(pend) > upto:
                        (x_exp, x_v, x_kp, parts, wkey, fin) = pend.pop(0)
                        for j, (kt, ap) in enumerate(parts):
                            stp = fin and j == len(parts) - 1
                            e_pv = mk_mm(PVS, ps[PVS][0:DV, 0:TC],
                                         v_sb[x_kp][:, kt * 128:(kt + 1) * 128],
                                         ap, start=first[0], stop=stp,
                                         waits=[x_exp, x_v])
                            if KT != 4:
                                e_den = mk_mm(DENS, ps[DENS][0:1, 0:TC], ones_bf[:, 0:1],
                                              ap, start=first[0], stop=stp,
                                              waits=[x_exp, EV["memset"]])
                            first[0] = False
                            WAR[f"vat{x_kp}"].append(e_pv)
                        if KT != 4:
                            WAR[wkey] = [e_pv, e_den]
                        else:
                            WAR[wkey].append(e_pv)

                def flush_den(upto):
                    nonlocal e_den
                    while len(pend_den) > upto:
                        (x_ss, r2, fin2) = pend_den.pop(0)
                        e_den = mk_mm(DENS, ps[DENS][0:1, 0:TC], ones_bf[:, 0:1],
                                      ssA_sb[r2][:], start=den_first[0], stop=fin2,
                                      waits=[x_ss, EV["memset"]])
                        den_first[0] = False
                        WAR[f"ssA{r2}"] = [e_den]

                for kc in range(qc + 1):
                    kp = kc % 6
                    e_k = dma(f"ld_k{kp}", k_sb[kp][:],
                              k_dram[hh * DN:(hh + 1) * DN, kc * TC:(kc + 1) * TC],
                              waits=WAR[f"kat{kp}"])
                    e_v = dma(f"ld_v{kp}", v_sb[kp][:],
                              v_dram[hh, kc, :, :],
                              waits=WAR[f"vat{kp}"])
                    WAR[f"kat{kp}"] = []
                    WAR[f"vat{kp}"] = []
                    diag = (kc == qc)
                    if KT == 4:
                        r = kc % 4
                        pbr = pbF_sb[r]
                        qk_evs = []
                        for kt in range(4):
                            sl = f"PS{kt}"
                            e_s_ = mk_mm(sl, ps[sl][0:128, 0:TC],
                                         k_sb[kp][:, kt * 128:(kt + 1) * 128],
                                         q_sb[hh][:], start=True, stop=True,
                                         waits=[e_k, e_q])
                            qk_evs.append((sl, e_s_))
                        WAR[f"kat{kp}"] += [e for _, e in qk_evs]
                        WAR[f"qat{hh}"] += [e for _, e in qk_evs]
                        prod = [e for _, e in qk_evs]
                        if diag:
                            e_msk = dve(lambda v: v.tensor_tensor(
                                out=psa[0:128, 0:2048].rearrange(
                                    "p (b c) -> p b c", c=512)[:, :, 0:TC],
                                in0=psa[0:128, 0:2048].rearrange(
                                    "p (b c) -> p b c", c=512)[:, :, 0:TC],
                                in1=mask4_sb[:, :].rearrange("p (b c) -> p b c", c=TC),
                                op=ALU.add), waits=prod + [EV["mask4"]])
                            for sl, _ in qk_evs:
                                ps_write_dve(sl, e_msk)
                            prod = [e_msk]
                        e_exp = act(lambda a_, pbr=pbr: a_.activation(
                            out=pbr[:, :].rearrange("p (b c) -> p b c", c=TC),
                            in_=psa[0:128, 0:2048].rearrange(
                                "p (b c) -> p b c", c=512)[:, :, 0:TC],
                            func=AF.Exp, scale=SCALE, bias=nshift_sb[:, :]),
                            waits=prod + WAR[f"pbat{r}"])
                        for sl, _ in qk_evs:
                            ps_read(sl, e_exp)
                        e_sa1 = dve(lambda v, pbr=pbr, r=r: v.tensor_tensor(
                            out=ssB_sb[r][:, :].rearrange("p (b c) -> p b c", c=TC),
                            in0=pbr[:, 0:2 * TC].rearrange("p (b c) -> p b c", c=TC),
                            in1=pbr[:, 2 * TC:4 * TC].rearrange("p (b c) -> p b c", c=TC),
                            op=ALU.add), waits=[e_exp] + WAR[f"ssB{r}"])
                        e_ss = dve(lambda v, r=r: v.tensor_tensor(
                            out=ssA_sb[r][:], in0=ssB_sb[r][:, 0:TC],
                            in1=ssB_sb[r][:, TC:2 * TC],
                            op=ALU.add), waits=[e_sa1] + WAR[f"ssA{r}"])
                        WAR[f"ssB{r}"] = [e_ss]
                        WAR[f"pbat{r}"] = [e_sa1]
                        pend_den.append((e_ss, r, diag))
                        flush_den(0 if diag else 3)
                        pend_pv.append((e_exp, e_v, kp,
                                        [(0, pbr[:, 0:TC]), (1, pbr[:, TC:2 * TC])],
                                        f"pbat{r}", False))
                        pend_pv.append((e_exp, e_v, kp,
                                        [(2, pbr[:, 2 * TC:3 * TC]),
                                         (3, pbr[:, 3 * TC:4 * TC])],
                                        f"pbat{r}", diag))
                        flush_pv(pend_pv, 4)
                    elif not diag:
                        for half in range(KT // 2):
                            rot = attn_rot[0] % 3
                            attn_rot[0] += 1
                            sA, sB = SETS[rot]
                            kt0, kt1 = 2 * half, 2 * half + 1
                            e_sa = mk_mm(sA, ps[sA][0:128, 0:TC],
                                         k_sb[kp][:, kt0 * 128:(kt0 + 1) * 128],
                                         q_sb[hh][:], start=True, stop=True,
                                         waits=[e_k, e_q])
                            e_sb2 = mk_mm(sB, ps[sB][0:128, 0:TC],
                                          k_sb[kp][:, kt1 * 128:(kt1 + 1) * 128],
                                          q_sb[hh][:], start=True, stop=True,
                                          waits=[e_k, e_q])
                            WAR[f"kat{kp}"] += [e_sa, e_sb2]
                            WAR[f"qat{hh}"] += [e_sa, e_sb2]
                            base, off = SETBASE[rot]
                            e_exp = act(lambda a_, base=base, off=off, rot=rot: a_.activation(
                                out=pb_sb[rot][:, :].rearrange("p (b c) -> p b c", c=TC),
                                in_=base[0:128, off:off + 1024].rearrange(
                                    "p (b c) -> p b c", c=512)[:, :, 0:TC],
                                func=AF.Exp, scale=SCALE, bias=nshift_sb[:, :]),
                                waits=[e_sa, e_sb2] + WAR[f"pbat{rot}"])
                            ps_read(sA, e_exp)
                            ps_read(sB, e_exp)
                            pend_pv.append((e_exp, e_v, kp,
                                            [(kt0, pb_sb[rot][:, 0:TC]),
                                             (kt1, pb_sb[rot][:, TC:2 * TC])],
                                            f"pbat{rot}", False))
                            flush_pv(pend_pv, 1)
                    else:
                        flush_pv(pend_pv, 0)
                    if diag and KT != 4:
                        for kt in range(KT):
                            sslot = f"PS{kt % 2}"
                            j0 = kt * 128
                            e_s = mk_mm(sslot, ps[sslot][0:128, j0:TC],
                                        k_sb[kp][:, kt * 128:(kt + 1) * 128],
                                        q_sb[hh][:, j0:TC], start=True, stop=True,
                                        waits=[e_k, e_q])
                            WAR[f"kat{kp}"].append(e_s)
                            WAR[f"qat{hh}"].append(e_s)
                            e_msk = dve(lambda v, s=sslot, kt=kt: v.tensor_tensor(
                                out=ps[s][0:128, kt * 128:(kt + 1) * 128],
                                in0=ps[s][0:128, kt * 128:(kt + 1) * 128],
                                in1=tri_sb[:], op=ALU.add), waits=[e_s, EV["tri"]])
                            ps_write_dve(sslot, e_msk)
                            pbuf = pdiag_sb[kt]
                            e_exp = act(lambda a_, s=sslot, pb=pbuf, j0=j0: a_.activation(
                                out=pb[:, j0:TC], in_=ps[s][0:128, j0:TC],
                                func=AF.Exp, scale=SCALE, bias=nshift_sb[:, :]),
                                waits=[e_msk, ev_pz] + WAR[f"pd{kt}"])
                            ps_read(sslot, e_exp)
                            last2 = (kt == KT - 1)
                            e_pv = mk_mm(PVS, ps[PVS][0:DV, j0:TC],
                                         v_sb[kp][:, kt * 128:(kt + 1) * 128],
                                         pbuf[:, j0:TC], start=first[0], stop=last2,
                                         waits=[e_exp, e_v])
                            e_den = mk_mm(DENS, ps[DENS][0:1, j0:TC], ones_bf[:, 0:1],
                                          pbuf[:, j0:TC], start=first[0], stop=last2,
                                          waits=[e_exp, EV["memset"]])
                            first[0] = False
                            WAR[f"vat{kp}"].append(e_pv)
                            WAR[f"pd{kt}"] = [e_pv, e_den]
                flush_pv(pend_pv, 0)
                e_rec = dve(lambda v, hb=hb, DENS=DENS: v.reciprocal(
                    out=invd_sb[hb][:], in_=ps[DENS][0:1, 0:TC]),
                    waits=[e_den] + WAR[f"invd{hb}"])
                ps_read(DENS, e_rec)
                e_bc = mk_mm(DENS, ps[DENS][0:128, 0:TC], ones_f1[0:1, :],
                             invd_sb[hb][0:1, :],
                             start=True, stop=True, waits=[e_rec])
                WAR[f"invd{hb}"] = [e_bc]
                e_ebc = dve(lambda v, hb=hb, DENS=DENS: v.tensor_copy(
                    out=invb_sb[hb][:], in_=ps[DENS][0:128, 0:TC]),
                    waits=[e_bc] + WAR[f"invb{hb}"])
                ps_read(DENS, e_ebc)
                e_at = dve(lambda v, hh=hh, hb=hb, PVS=PVS: v.tensor_tensor(
                    out=attn_sb[hh][:],
                    in0=ps[PVS][0:DV, 0:TC],
                    in1=invb_sb[hb][:], op=ALU.mult),
                    waits=[e_ebc, e_pv] + WAR[f"ah{hh}"])
                ps_read(PVS, e_at)
                WAR[f"invb{hb}"] = [e_at]
                attn_ready.append(e_at)
            wo_stores = []
            for grp in range(NKT // 4):
                for dsub in range(4):
                    dti = grp * 4 + dsub
                    slot = f"PS{dsub}"
                    e_wom = None
                    for hh in range(HLOC):
                        e_wom = mk_mm(slot, ps[slot][0:128, 0:TC],
                                      wo_sb[:, hh * D + dti * 128:hh * D + (dti + 1) * 128],
                                      attn_sb[hh][:], start=(hh == 0), stop=(hh == HLOC - 1),
                                      waits=[e_wo, attn_ready[hh]])
                        WAR[f"ah{hh}"].append(e_wom)
                    bi2 = dsub % 2
                    e_ew = dve(lambda v, slot=slot, bi2=bi2: v.tensor_copy(
                        out=wop_sb[bi2][:], in_=ps[slot][0:128, 0:TC]),
                        waits=[e_wom] + WAR[f"wop{bi2}"])
                    ps_read(slot, e_ew)
                    e_sw = dma(f"st_wop{bi2}", wop_c[qc * D + dti * 128:qc * D + (dti + 1) * 128, :],
                               wop_sb[bi2][:], waits=[e_ew], eng="pool")
                    WAR[f"wop{bi2}"] = [e_sw]
                    wo_stores.append(e_sw)
            creg = slice(qc * TC, (qc + 1) * TC)
            e_rs = cc("ReduceScatter", ALU.add, [wop_c[qc * D:(qc + 1) * D, :].opt()],
                      [wop_g[qc * DSL:(qc + 1) * DSL, :].opt()], waits=wo_stores)
            e_lrs = dma(f"ld_rs{p}", rs_sb[p][:, :].rearrange("p (a t) -> p a t", t=TC),
                        wop_g[qc * DSL:(qc + 1) * DSL, :].rearrange("(a p) t -> p a t", p=128),
                        waits=[e_rs] + WAR[f"rs{p}"], eng="pool")
            e_resid = dve(lambda v, p=p: v.tensor_tensor(out=x_sb[p][:], in0=x_sb[p][:],
                                                         in1=rs_sb[p][:], op=ALU.add),
                          waits=[e_lrs, ev_x])
            WAR[f"rs{p}"] = [e_resid]
            e_stx = dma(f"st_x{p}", xstream[3][:, creg].rearrange("(a p) t -> p a t", p=128),
                        x_sb[p][:, :].rearrange("p (a t) -> p a t", t=TC),
                        waits=[e_resid], eng="pool")
            att_stores.append(e_stx)
            xcons[p] = [e_resid, e_stx]
            for hh in range(HLOC):
                WAR[f"ah{hh}"] = list(wo_stores)
        P.full_barrier(att_stores)

    ffn_phase(xstream[3], xstream[4], "f2")
    P.op("pool", lambda g: g.nop())

    emit_engine = P.emit_into(nc, ctx)
    with nc.Block() as block:
        @block.sync
        def _(sync):
            emit_engine("sync", sync)

        @block.tensor
        def _(tensor):
            emit_engine("pe", tensor)

        @block.scalar
        def _(scalar):
            emit_engine("act", scalar)

        @block.vector
        def _(vector):
            emit_engine("dve", vector)

        @block.gpsimd
        def _(gpsimd):
            emit_engine("pool", gpsimd)

    ctx.close()
    return nc


def prep_inputs(cfg, inp):
    T, D, NH, TC = cfg["T"], cfg["D"], cfg["NH"], cfg["TC"]
    DSL, HLOC = cfg["DSL"], cfg["HLOC"]
    NDT, NDNT, NST, DTR = cfg["NDT"], cfg["NDNT"], cfg["NST"], cfg["DTR"]
    DN, DV, DQ, G = cfg["DN"], cfg["DV"], cfg["DQ"], cfg["G"]
    OUT2 = D // 2

    x = np.ascontiguousarray(np.asarray(inp["L0_attn_a_x"], np.float32)[0].T)
    wq = np.asarray(inp["L0_attn_b_wq"], np.float32)
    wkv = np.asarray(inp["L0_attn_b_wkv"], np.float32)
    wo = np.asarray(inp["L0_attn_b_wo"], np.float32)
    xpw = np.asarray(inp["L0_attn_a_x_proj_w"], np.float32)
    dtw = np.asarray(inp["L0_attn_a_dt_proj_w"], np.float32)
    dtb = np.asarray(inp["L0_attn_a_dt_proj_b"], np.float32)
    alog = np.asarray(inp["L0_attn_a_A_log"], np.float32)
    dvec = np.asarray(inp["L0_attn_a_D"], np.float32)
    Wq = np.asarray(inp["L0_ffn_W_q"], np.uint8)
    scl = np.asarray(inp["L0_ffn_scale"], np.float32).reshape(D, D // G)
    zro = np.asarray(inp["L0_ffn_zero"], np.float32).reshape(D, D // G)
    byte_T = np.ascontiguousarray(Wq.reshape(OUT2, D).T)

    rk = np.zeros((128, 16, 128), np.float32)
    sk = np.zeros((128, 16, 128), np.float32)
    for kk in range(16):
        for po in range(128):
            rk[8 * kk + po // 16, kk, po] = 1.0
            sk[po, kk, 8 * kk + po // 16] = 1.0
    rk = rk.reshape(128, 2048).astype(NPBF16)
    sk = sk.reshape(128, 2048).astype(NPBF16)
    rb = np.zeros((32, 2, 128), np.float32)
    for po in range(128):
        rb[po % 16, 0, po] = 1.0
        rb[16 + po % 16, 1, po] = 1.0
    rb = rb.reshape(32, 256).astype(NPBF16)
    tri = np.where(np.arange(128)[:, None] > np.arange(128)[None, :], -1e30, 0.0
                   ).astype(np.float32)
    if cfg["KT"] == 4:
        KT4 = cfg["KT"]
        kk = (np.arange(KT4)[:, None, None] * 128 + np.arange(128)[None, :, None])
        qq = np.arange(TC)[None, None, :]
        mask4 = np.where(kk > qq, -1e30, 0.0).astype(np.float32)
        mask4 = np.ascontiguousarray(mask4.transpose(1, 0, 2).reshape(128, KT4 * TC))

    in_maps = []
    for i in range(NCORES):
        dsl = slice(i * DSL, (i + 1) * DSL)
        hsl = range(i * HLOC, (i + 1) * HLOC)
        m = {}
        m["x0"] = np.ascontiguousarray(x[dsl])
        m["wq_sl"] = np.ascontiguousarray(
            np.concatenate([wq[:, h * DQ:h * DQ + DN] for h in hsl], axis=1))
        m["wk_sl"] = np.ascontiguousarray(
            np.concatenate([wkv[:, h * DN:(h + 1) * DN] for h in hsl], axis=1))
        m["wv_sl"] = np.ascontiguousarray(
            np.concatenate([wkv[:, NH * DN + h * DV:NH * DN + (h + 1) * DV] for h in hsl],
                           axis=1))
        m["wo_sl"] = np.ascontiguousarray(wo[i * HLOC * DV:(i + 1) * HLOC * DV, :])
        m["xpw_sl"] = np.ascontiguousarray(xpw[:, dsl].T)
        m["dtw_sl"] = np.ascontiguousarray(dtw[dsl, :].T)
        m["dtb_sl"] = np.ascontiguousarray(dtb[dsl].reshape(NDT, 128).T)
        m["alog_sl"] = np.ascontiguousarray(
            alog[dsl].reshape(NDNT, 128).T)
        m["dvec_sl"] = np.ascontiguousarray(dvec[dsl].reshape(NDT, 128).T)
        if i < NCORES // 2:
            cols = slice(i * DSL, (i + 1) * DSL)
            shift = 4
        else:
            cols = slice((i - NCORES // 2) * DSL, (i - NCORES // 2 + 1) * DSL)
            shift = 0
        m["wqt_sl"] = np.ascontiguousarray(byte_T[:, cols])
        osl = np.arange(i * DSL, (i + 1) * DSL)
        m["scl_sl"] = np.ascontiguousarray(np.repeat(scl[osl].T, G, axis=0))
        m["zro_sl"] = np.ascontiguousarray(np.repeat(zro[osl].T, G, axis=0))
        msel = 1.0 if shift == 4 else 0.0
        m["nibsel"] = np.ascontiguousarray(
            np.tile(np.array([[msel, 1.0 - msel]], np.float32), (128, 1)))
        m["rk_all"] = rk
        m["ident"] = np.eye(128, dtype=np.float32)
        m["sk_all"] = sk
        m["rb_bc"] = rb
        m["trimask"] = tri
        if cfg["KT"] == 4:
            m["mask4"] = mask4
        in_maps.append(m)
    return in_maps


DEFAULT_CFG = make_cfg()


LAST_PROG = [None]


def kernel(**inputs):
    cfg = DEFAULT_CFG
    in_maps = prep_inputs(cfg, inputs)
    nc = build_program(cfg)
    res = run_bass_kernel_spmd(nc, in_maps, core_ids=list(range(NCORES)))
    full = np.concatenate([np.asarray(r["xout"]) for r in res.results], axis=0)
    return np.ascontiguousarray(full.T)[None].astype(np.float32)


# version 43
# speedup vs baseline: 1.0874x; 1.0874x over previous
import contextlib
from collections import defaultdict

import numpy as np
import ml_dtypes
import concourse.bass as bass
import concourse.mybir as mybir
from concourse.bass_utils import run_bass_kernel_spmd

F32 = mybir.dt.float32
BF16 = mybir.dt.bfloat16
U8 = mybir.dt.uint8
AF = mybir.ActivationFunctionType
ALU = mybir.AluOpType
NPBF16 = ml_dtypes.bfloat16

NCORES = 8
ENGINES = ("sync", "pe", "act", "dve", "pool")


def make_cfg(T=8192, D=4096, NH=32, TC=512):
    c = {}
    c["T"], c["D"], c["NH"], c["TC"] = T, D, NH, TC
    c["DSL"] = D // NCORES
    c["HLOC"] = NH // NCORES
    c["NCH"] = T // TC
    c["NDT"] = c["DSL"] // 128
    c["NDNT"] = c["DSL"] * 16 // 128
    c["NST"] = 16
    c["DTR"] = 128
    c["DN"] = 128
    c["DV"] = 128
    c["DQ"] = 192
    c["KT"] = TC // 128
    c["G"] = 64
    c["SHIFT"] = 12.0
    return c


class Prog:

    def __init__(self):
        self.ops = []
        self.sem_idx = defaultdict(int)
        self.pending = {e: [] for e in ENGINES}
        self.last_ev = {e: None for e in ENGINES}

    def op(self, eng, emit, sem=None, waits=(), dma=False):
        sem = sem or eng
        idx = self.sem_idx[sem]
        self.sem_idx[sem] = idx + 1
        w = list(self.pending[eng]) + [x for x in waits if x is not None]
        self.pending[eng] = []
        import inspect as _insp
        fr = _insp.currentframe().f_back
        if fr.f_code.co_name in ("dma", "cc", "act", "dve", "pool", "mk_mm"):
            fr = fr.f_back
        self.ops.append((eng, emit, sem, w, 16 if dma else 1, dma, fr.f_lineno))
        self.last_ev[eng] = (sem, idx)
        return (sem, idx)

    def describe(self, sem, lo, hi):
        semc = 0
        for (eng, emit, s, w, amt, dma, ln) in self.ops:
            if s != sem:
                continue
            semc += 1
            if lo <= semc <= hi:
                print(f"sem {sem} #{semc}: eng={eng} line={ln}")

    def full_barrier(self, extra=()):
        evs = [self.last_ev[e] for e in ENGINES if self.last_ev[e] is not None]
        evs = evs + [x for x in extra if x is not None]
        for e in ENGINES:
            self.pending[e] = list(evs)

    def emit_into(self, nc, ctx):
        waited = defaultdict(set)
        per_sem = defaultdict(list)
        semidx = defaultdict(int)
        op_semidx = []
        for k, (eng, emit, sem, waits, amt, dma, _ln) in enumerate(self.ops):
            op_semidx.append(semidx[sem])
            semidx[sem] += 1
            per_sem[sem].append((k, amt, dma))
            for (s, i) in waits:
                waited[s].add(i)
        need_inc = {}
        val_of = {}
        for sem, lst in per_sem.items():
            wset = waited[sem]
            cum = 0
            for j, (k, amt, dma) in enumerate(lst):
                inc = dma or (j in wset)
                need_inc[k] = inc
                if inc:
                    cum += amt
                val_of[(sem, j)] = cum
        sems = {s: ctx.enter_context(nc.semaphore(f"S_{s}")) for s in per_sem}

        def emit_engine(eng, engobj):
            lastwait = {}
            for k, (e, emit, sem, waits, amt, dma, _ln) in enumerate(self.ops):
                if e != eng:
                    continue
                wmax = {}
                for (s, i) in waits:
                    v = val_of[(s, i)]
                    if v > wmax.get(s, -1):
                        wmax[s] = v
                for s, v in wmax.items():
                    if v == 0 or lastwait.get(s, -1) >= v:
                        continue
                    engobj.wait_ge(sems[s], v)
                    lastwait[s] = v
                instr = emit(engobj)
                if need_inc[k]:
                    if instr is None:
                        raise RuntimeError(f"op {k} on {eng}/{sem} must inc but emitted None")
                    instr.then_inc(sems[sem], amt)
        return emit_engine


def build_program(cfg):
    T, D, NH, TC = cfg["T"], cfg["D"], cfg["NH"], cfg["TC"]
    DSL, HLOC, NCH = cfg["DSL"], cfg["HLOC"], cfg["NCH"]
    NDT, NDNT, NST, DTR = cfg["NDT"], cfg["NDNT"], cfg["NST"], cfg["DTR"]
    DN, DV, KT, G = cfg["DN"], cfg["DV"], cfg["KT"], cfg["G"]
    XPR = DTR + 2 * NST
    NKT = D // 128
    SCALE = cfg["DQ"] ** -0.5
    SHIFT = cfg["SHIFT"]
    RG = [list(range(NCORES))]

    nc = bass.Bass("TRN2", target_bir_lowering=False)
    P = Prog()
    globals()['LAST_PROG'] = P
    ctx = contextlib.ExitStack()

    x0 = nc.dram_tensor("x0", [DSL, T], F32, kind="ExternalInput").ap()
    out = nc.dram_tensor("xout", [DSL, T], F32, kind="ExternalOutput").ap()
    wq_d = nc.dram_tensor("wq_sl", [D, HLOC * DN], F32, kind="ExternalInput").ap()
    wk_d = nc.dram_tensor("wk_sl", [D, HLOC * DN], F32, kind="ExternalInput").ap()
    wv_d = nc.dram_tensor("wv_sl", [D, HLOC * DV], F32, kind="ExternalInput").ap()
    wo_d = nc.dram_tensor("wo_sl", [HLOC * DV, D], F32, kind="ExternalInput").ap()
    xpw_d = nc.dram_tensor("xpw_sl", [DSL, XPR], F32, kind="ExternalInput").ap()
    dtw_d = nc.dram_tensor("dtw_sl", [DTR, DSL], F32, kind="ExternalInput").ap()
    dtb_d = nc.dram_tensor("dtb_sl", [128, NDT], F32, kind="ExternalInput").ap()
    alog_d = nc.dram_tensor("alog_sl", [128, NDNT], F32, kind="ExternalInput").ap()
    dvec_d = nc.dram_tensor("dvec_sl", [128, NDT], F32, kind="ExternalInput").ap()
    wqt_d = nc.dram_tensor("wqt_sl", [D, DSL], U8, kind="ExternalInput").ap()
    scl_d = nc.dram_tensor("scl_sl", [D, DSL], F32, kind="ExternalInput").ap()
    zro_d = nc.dram_tensor("zro_sl", [D, DSL], F32, kind="ExternalInput").ap()
    nibsel_d = nc.dram_tensor("nibsel", [128, 2], F32, kind="ExternalInput").ap()
    rk_d = nc.dram_tensor("rk_all", [128, 16 * 128], BF16, kind="ExternalInput").ap()
    sk_d = nc.dram_tensor("sk_all", [128, 16 * 128], BF16, kind="ExternalInput").ap()
    rb_d = nc.dram_tensor("rb_bc", [32, 2 * 128], BF16, kind="ExternalInput").ap()
    tri_d = nc.dram_tensor("trimask", [128, 128], F32, kind="ExternalInput").ap()
    if KT == 4:
        mask4_d = nc.dram_tensor("mask4", [128, KT * TC], F32, kind="ExternalInput").ap()
    iden_d = nc.dram_tensor("ident", [128, 128], F32, kind="ExternalInput").ap()

    xi = [nc.dram_tensor(f"x{i}_int", [DSL, T], F32).ap() for i in (1, 2, 3)]
    xstream = [x0, xi[0], xi[1], xi[2], out]
    wqb_d = nc.dram_tensor("wqb", [D, HLOC * DN], BF16).ap()
    wkb_d = nc.dram_tensor("wkb", [D, HLOC * DN], BF16).ap()
    wvb_d = nc.dram_tensor("wvb", [D, HLOC * DV], BF16).ap()
    wob_d = nc.dram_tensor("wob", [HLOC * DV, D], BF16).ap()
    wtb_d = nc.dram_tensor("wtb", [D, DSL], BF16).ap()
    stats_c = nc.dram_tensor("stats_c", [1, NCH * 2 * TC], F32).ap()
    stats_g = nc.dram_tensor("stats_g", [1, NCH * 2 * TC], F32).ap()
    xp_c = nc.dram_tensor("xp_c", [NCH * XPR, TC], F32).ap()
    xp_g = nc.dram_tensor("xp_g", [NCH * XPR, TC], F32).ap()
    xln_c = nc.dram_tensor("xln_c", [NCH * DSL, TC], BF16).ap()
    xg_d = nc.dram_tensor("xg", [NCH * D, TC], BF16).ap()
    k_dram = nc.dram_tensor("k_dram", [HLOC * DN, T], BF16).ap()
    q_dram = nc.dram_tensor("q_dram", [HLOC * DN, T], BF16).ap()
    v_dram = nc.dram_tensor("v_dram", [HLOC, NCH, 128, KT * DV], BF16).ap()
    wop_c = nc.dram_tensor("wop_c", [NCH * D, TC], BF16).ap()
    wop_g = nc.dram_tensor("wop_g", [NCH * DSL, TC], BF16).ap()
    at_dram = nc.dram_tensor("at_dram", [64, 128], F32).ap()

    psa = ctx.enter_context(nc.psum_tensor("PSA", [128, 2048], F32))
    psb = ctx.enter_context(nc.psum_tensor("PSB", [128, 1024], F32))
    ps = {f"PS{i}": psa[:, 512 * i:512 * (i + 1)] for i in range(4)}
    ps["PS6"] = psb[:, 0:512]
    ps["PS7"] = psb[:, 512:1024]
    for i in (4, 5):
        ps[f"PS{i}"] = ctx.enter_context(nc.psum_tensor(f"PS{i}", [128, 512], F32))[:, :]
    ps_writer = {k: None for k in ps}
    ps_readers = {k: [] for k in ps}

    def mk_mm(slot, out_ap, lhsT, rhs, start, stop, waits=()):
        w = list(waits)
        if start:
            w += ps_readers[slot] + ([ps_writer[slot]] if ps_writer[slot] else [])
        ev = P.op("pe", lambda e, o=out_ap, l=lhsT, r=rhs, s1=start, s2=stop:
                  e.matmul(out=o, lhsT=l, rhs=r, start=s1, stop=s2), waits=w)
        if start:
            ps_writer[slot] = ev
            ps_readers[slot] = []
        else:
            ps_writer[slot] = ev
        return ev

    def ps_read(slot, ev):
        ps_readers[slot].append(ev)
        return ev

    def ps_write_dve(slot, ev):
        ps_writer[slot] = ev
        return ev

    def dma(stream, outap, inap, waits=(), eng="sync"):
        return P.op(eng, lambda e, o=outap, i=inap: e.dma_start(out=o, in_=i),
                    sem=stream, waits=waits, dma=True)

    def cc(kind, op, ins, outs, waits=()):
        return P.op("pool", lambda e, k=kind, o=op, i=ins, u=outs:
                    e.collective_compute(k, o, replica_groups=RG, ins=i, outs=u),
                    sem="cc", waits=waits, dma=False)

    def act(emit, waits=()):
        return P.op("act", emit, waits=waits)

    def dve(emit, waits=()):
        return P.op("dve", emit, waits=waits)

    def pool(emit, waits=()):
        return P.op("pool", emit, waits=waits)

    WAR = defaultdict(list)
    RSB_EV = {}

    def sb(stack, name, shape, dt, side=None):
        if side:
            return stack.enter_context(nc.sbuf_tensor(name, shape, dt, side=side))
        return stack.enter_context(nc.sbuf_tensor(name, shape, dt))

    rk_sb = sb(ctx, "rk_sb", [128, 16 * 128], BF16)
    sk_sb = sb(ctx, "sk_sb", [128, 16 * 128], BF16)
    rb_sb = sb(ctx, "rb_sb", [32, 2 * 128], BF16)
    tri_sb = sb(ctx, "tri_sb", [128, 128], F32)
    iden_sb = sb(ctx, "iden_sb", [128, 128], F32)

    rka_sb = sb(ctx, "rka_sb", [128, 0], BF16) if False else None
    xpw_sb = sb(ctx, "xpw_sb", [128, NDT * XPR], BF16)
    dtw_sb = sb(ctx, "dtw_sb", [DTR, DSL], BF16)
    dtb_sb = sb(ctx, "dtb_sb", [128, NDT], F32)
    a_sb = sb(ctx, "a_sb", [128, NDNT], F32)
    dvec_sb = sb(ctx, "dvec_sb", [128, NDT], F32)
    nibsel_sb = sb(ctx, "nibsel_sb", [128, 2], F32)
    ones_bf = sb(ctx, "ones_bf", [128, 1], BF16)
    ones_f1 = sb(ctx, "ones_f1", [1, 128], F32)
    ones_fp = sb(ctx, "ones_fp", [128, 1], F32)
    eps_sb = sb(ctx, "eps_sb", [1, 1], F32)
    one_sb = sb(ctx, "one_sb", [128, 1], F32)
    nshift_sb = sb(ctx, "nshift_sb", [128, 1], F32)
    carry_sb = sb(ctx, "carry_sb", [128, NDNT], F32)
    rka_sb = sb(ctx, "rka_sb", [128, NDNT * 128], BF16)
    x_sb = [sb(ctx, f"x_sb{p}", [128, NDT * TC], F32) for p in range(2)]
    xsq_sb = sb(ctx, "xsq_sb", [128, NDT * TC], F32)
    xlb_sb = [sb(ctx, f"xlb_sb{p}", [128, NDT * TC], BF16) for p in range(2)]
    st1_sb = [sb(ctx, f"st1_sb{p}", [1, 2 * TC], F32) for p in range(2)]
    st2_sb = [sb(ctx, f"st2_sb{p}", [1, TC], F32) for p in range(2)]
    rsm_sb = [sb(ctx, f"rsm_sb{p}", [1, 2 * TC], F32) for p in range(2)]
    rsb_sb = [sb(ctx, f"rsb_sb{p}", [128, 2 * TC], F32) for p in range(2)]
    stc_sb = [sb(ctx, f"stc_sb{p}", [1, 2 * TC], F32) for p in range(2)]
    ph0 = contextlib.ExitStack()
    at_sb = sb(ph0, "at_sb", [64, 128], F32, side="right")
    atf_sb = sb(ph0, "atf_sb", [1, 16 * 128], F32, side="right")
    cast_sb = sb(ph0, "cast_sb", [128, 1024], F32, side="right")
    castb_sb = sb(ph0, "castb_sb", [128, 1024], BF16, side="right")

    def stats_prepass(xin):
        xc = [[], []]
        st_evs = []
        ar_evs = []
        for c in range(NCH):
            p = c % 2
            ev_x = load_x(xin, c, p, waits=xc[p])
            e_sq = dve(lambda v, p=p: v.tensor_tensor(out=xsq_sb[:], in0=x_sb[p][:],
                                                      in1=x_sb[p][:], op=ALU.mult),
                       waits=[ev_x] + WAR["xsq"])
            ev1 = ev2 = None
            for a in range(NDT):
                ev1 = mk_mm("PS6", ps["PS6"][0:1, 0:TC], ones_fp[:, 0:1],
                            x_sb[p][:, a * TC:(a + 1) * TC], start=(a == 0),
                            stop=(a == NDT - 1), waits=[ev_x, EV["memset"]])
                ev2 = mk_mm("PS7", ps["PS7"][0:1, 0:TC], ones_fp[:, 0:1],
                            xsq_sb[:, a * TC:(a + 1) * TC], start=(a == 0),
                            stop=(a == NDT - 1), waits=[e_sq])
            WAR["xsq"] = [ev2]
            e_e1 = act(lambda a_, p=p: a_.copy(out=stc_sb[p][:, 0:TC], in_=ps["PS6"][0:1, 0:TC]),
                       waits=[ev1] + WAR[f"stc{p}"])
            ps_read("PS6", e_e1)
            e_e2 = act(lambda a_, p=p: a_.copy(out=stc_sb[p][:, TC:2 * TC], in_=ps["PS7"][0:1, 0:TC]),
                       waits=[ev2])
            ps_read("PS7", e_e2)
            e_st = dma(f"st_stats{p}", stats_c[:, c * 2 * TC:(c + 1) * 2 * TC], stc_sb[p][:, :],
                       waits=[e_e1, e_e2], eng="pool")
            WAR[f"stc{p}"] = [e_st]
            st_evs.append(e_st)
            xc[p] = [ev1, ev2]
            Q = max(1, NCH // 4)
            if (c + 1) % Q == 0:
                qi = c // Q
                ar_evs.append(cc("AllReduce", ALU.add,
                                 [stats_c[:, qi * Q * 2 * TC:(qi + 1) * Q * 2 * TC].opt()],
                                 [stats_g[:, qi * Q * 2 * TC:(qi + 1) * Q * 2 * TC].opt()],
                                 waits=st_evs))
                st_evs = []
        return ar_evs, xc

    def ln_norm(c, p, ev_x, e_ar, want_f32, xlf_sb=None, neng=None):
        neng = neng or dve
        reg = slice(c * 2 * TC, (c + 1) * 2 * TC)
        e_ld = dma(f"ld_stats{p}", stc_sb[p][:], stats_g[:, reg],
                   waits=[e_ar[c // max(1, NCH // 4)]] + WAR[f"stc{p}"])
        e_m = dve(lambda v, p=p: v.tensor_scalar(out=st1_sb[p][:], in0=stc_sb[p][:],
                                                 scalar1=1.0 / D, scalar2=None, op0=ALU.mult),
                  waits=[e_ld] + WAR[f"st1{p}"])
        WAR[f"stc{p}"] = [e_st for e_st in ()] + [e_m]
        e_m2 = dve(lambda v, p=p: v.tensor_tensor(out=st2_sb[p][:], in0=st1_sb[p][:, 0:TC],
                                                  in1=st1_sb[p][:, 0:TC], op=ALU.mult),
                   waits=[e_m] + WAR[f"st2{p}"])
        e_var = dve(lambda v, p=p: v.tensor_tensor(out=st2_sb[p][:], in0=st1_sb[p][:, TC:2 * TC],
                                                   in1=st2_sb[p][:], op=ALU.subtract),
                    waits=[e_m2])
        e_ln = act(lambda a_, p=p: a_.activation(out=st2_sb[p][:], in_=st2_sb[p][:],
                                                 func=AF.Ln, bias=eps_sb[:, :]), waits=[e_var])
        e_rs = act(lambda a_, p=p: a_.activation(out=rsm_sb[p][:, 0:TC], in_=st2_sb[p][:],
                                                 func=AF.Exp, scale=-0.5),
                   waits=[e_ln] + WAR[f"rsm{p}"])
        WAR[f"st2{p}"] = [e_rs]
        e_mrs = dve(lambda v, p=p: v.tensor_tensor(out=rsm_sb[p][:, TC:2 * TC],
                                                   in0=st1_sb[p][:, 0:TC],
                                                   in1=rsm_sb[p][:, 0:TC], op=ALU.mult),
                    waits=[e_rs])
        WAR[f"st1{p}"] = [e_mrs]
        e_b1 = mk_mm("PS6", ps["PS6"][0:128, 0:TC], ones_f1[0:1, :], rsm_sb[p][0:1, 0:TC],
                     start=True, stop=True, waits=[e_rs])
        e_b2 = mk_mm("PS7", ps["PS7"][0:128, 0:TC], ones_f1[0:1, :], rsm_sb[p][0:1, TC:2 * TC],
                     start=True, stop=True, waits=[e_mrs])
        e_eb1 = act(lambda a_, p=p: a_.copy(out=rsb_sb[p][:, 0:TC], in_=ps["PS6"][0:128, 0:TC]),
                    waits=[e_b1] + WAR[f"rsb{p}"])
        ps_read("PS6", e_eb1)
        e_eb2 = act(lambda a_, p=p: a_.copy(out=rsb_sb[p][:, TC:2 * TC], in_=ps["PS7"][0:128, 0:TC]),
                    waits=[e_b2])
        ps_read("PS7", e_eb2)
        WAR[f"rsm{p}"] = [e_eb1, e_eb2]
        RSB_EV[p] = (e_eb1, e_eb2)
        x3 = x_sb[p][:, :].rearrange("q (a t) -> q a t", a=NDT)
        rs3 = rsb_sb[p][:, 0:TC].unsqueeze(1).broadcast_to([128, NDT, TC])
        mrs3 = rsb_sb[p][:, TC:2 * TC].unsqueeze(1).broadcast_to([128, NDT, TC])
        if want_f32:
            f3 = xlf_sb[:, :].rearrange("q (a t) -> q a t", a=NDT)
            e_n1 = dve(lambda v: v.tensor_tensor(out=f3, in0=x3, in1=rs3, op=ALU.mult),
                       waits=[e_eb1, ev_x] + WAR["xlf"])
            e_n2 = dve(lambda v: v.tensor_tensor(out=f3, in0=f3, in1=mrs3, op=ALU.subtract),
                       waits=[e_eb2, e_n1])
            e_n3 = dve(lambda v, p=p: v.tensor_copy(out=xlb_sb[p][:], in_=xlf_sb[:]),
                       waits=[e_n2] + WAR[f"xlb{p}"])
            WAR[f"rsb{p}"] = [e_n2]
            return e_n3, e_n2, [e_n1]
        else:
            s3 = xsq_sb[:, :].rearrange("q (a t) -> q a t", a=NDT)
            e_n1 = neng(lambda v, s3=s3, x3=x3, rs3=rs3: v.tensor_tensor(
                out=s3, in0=x3, in1=rs3, op=ALU.mult),
                waits=[e_eb1, ev_x] + WAR["xsq"])
            b3 = xlb_sb[p][:, :].rearrange("p (a t) -> p a t", t=TC)
            e_n2 = neng(lambda v, b3=b3, s3=s3, mrs3=mrs3: v.tensor_tensor(
                out=b3, in0=s3, in1=mrs3, op=ALU.subtract),
                waits=[e_eb2, e_n1] + WAR[f"xlb{p}"])
            WAR["xsq"] = [e_n2]
            WAR[f"rsb{p}"] = [e_n2]
            return e_n2, None, [e_n1]

    def load_x(xdram, c, p, waits=(), eng="sync"):
        return dma(f"ld_x{p}", x_sb[p][:, :].rearrange("p (a t) -> p a t", t=TC),
                   xdram[:, c * TC:(c + 1) * TC].rearrange("(a p) t -> p a t", p=128),
                   waits=list(waits), eng=eng)

    EV = {}

    e = dve(lambda v: v.memset(ones_bf[:], 1.0))
    e = dve(lambda v: v.memset(ones_f1[:], 1.0), waits=[e])
    e = dve(lambda v: v.memset(ones_fp[:], 1.0), waits=[e])
    e = dve(lambda v: v.memset(eps_sb[:], 1e-5), waits=[e])
    e = dve(lambda v: v.memset(one_sb[:], 1.0), waits=[e])
    e = dve(lambda v: v.memset(nshift_sb[:], -SHIFT), waits=[e])
    EV["memset"] = e

    dma("ld_c", rk_sb[:], rk_d[:, :])
    dma("ld_c", sk_sb[:], sk_d[:, :])
    dma("ld_c", rb_sb[:], rb_d[:, :])
    dma("ld_c", tri_sb[:], tri_d[:, :])
    dma("ld_c", nibsel_sb[:], nibsel_d[:, :])
    dma("ld_c", dtb_sb[:], dtb_d[:, :])
    dma("ld_c", dvec_sb[:], dvec_d[:, :])
    dma("ld_c", iden_sb[:], iden_d[:, :])
    e_al = dma("ld_c", a_sb[:], alog_d[:, :])
    for kk_ in ("rk", "sk", "rb", "tri", "sh", "dtb", "dv"):
        EV[kk_] = e_al

    e_a1 = act(lambda a_: a_.activation(out=a_sb[:], in_=a_sb[:], func=AF.Exp), waits=[e_al])
    EV["A"] = dve(lambda v: v.tensor_scalar(out=a_sb[:], in0=a_sb[:], scalar1=-1.0,
                                            scalar2=None, op0=ALU.mult), waits=[e_a1])
    e_tr = P.op("pe", lambda e: e.transpose(out=ps["PS6"][0:NDNT, 0:128],
                                            in_=a_sb[:, 0:NDNT], identity=iden_sb[:]),
                waits=[EV["A"], EV["tri"]] + [w for w in (ps_writer["PS6"],) if w]
                + ps_readers["PS6"])
    ps_writer["PS6"], ps_readers["PS6"] = e_tr, []
    e_at = act(lambda a_: a_.copy(out=at_sb[0:NDNT, :], in_=ps["PS6"][0:NDNT, 0:128]),
               waits=[e_tr])
    ps_read("PS6", e_at)
    e_atd = dma("ld_at", at_dram[0:NDNT, :], at_sb[0:NDNT, :], waits=[e_at], eng="pool")
    e_rka_prev = None
    e_atf = None
    for k in range(NDNT):
        if k % 16 == 0:
            kw = min(16, NDNT - k)
            e_atf = dma("ld_atf", atf_sb[0:1, 0:kw * 128],
                        at_dram[k:k + kw, :].rearrange("a b -> (a b)").unsqueeze(0),
                        waits=[e_atd] + ([e_rka_prev] if e_rka_prev else []))
        e_bc0 = mk_mm("PS7", ps["PS7"][0:128, 0:128], ones_f1[0:1, :],
                      atf_sb[0:1, (k % 16) * 128:((k % 16) + 1) * 128],
                      start=True, stop=True, waits=[e_atf])
        e_rka = dve(lambda v, k=k: v.tensor_tensor(
            out=rka_sb[:, k * 128:(k + 1) * 128],
            in0=rk_sb[:, (k % 16) * 128:((k % 16) + 1) * 128],
            in1=ps["PS7"][0:128, 0:128], op=ALU.mult),
            waits=[e_bc0, EV["rk"]] + ([e_rka_prev] if e_rka_prev else []))
        ps_read("PS7", e_rka)
        e_rka_prev = e_rka
    EV["rka"] = e_rka_prev

    e_l = dma("ld_w0", cast_sb[0:DTR, 0:DSL], dtw_d[:, :])
    EV["dtw"] = dve(lambda v: v.tensor_copy(out=dtw_sb[:], in_=cast_sb[0:DTR, 0:DSL]),
                    waits=[e_l])
    e_l = dma("ld_w0", cast_sb[0:128, 0:NDT * XPR].rearrange("p (a r) -> p a r", r=XPR),
              xpw_d[:, :].rearrange("(a p) r -> p a r", p=128), waits=[EV["dtw"]])
    EV["xpw"] = dve(lambda v: v.tensor_copy(out=xpw_sb[:], in_=cast_sb[0:128, 0:NDT * XPR]),
                    waits=[e_l])

    cast2_sb = sb(ph0, "cast2_sb", [128, 1024], F32, side="right")
    castb2_sb = sb(ph0, "castb2_sb", [128, 1024], BF16, side="right")
    cbufs = [(cast_sb, castb_sb), (cast2_sb, castb2_sb)]
    cst = {"i": 0, "c": [EV["xpw"], None], "s": [None, None], "last": None}
    cast_blocks = []
    for (s_, d_, r_, c_) in ((wq_d, wqb_d, D, HLOC * DN), (wk_d, wkb_d, D, HLOC * DN),
                             (wv_d, wvb_d, D, HLOC * DV), (wo_d, wob_d, HLOC * DV, D)):
        for r0 in range(0, r_, 128):
            for c0 in range(0, c_, 1024):
                cast_blocks.append((s_, d_, r0, c0, min(1024, c_ - c0)))

    def emit_cast_block(k):
        src, dst, r0, c0, cw = cast_blocks[k]
        b = cst["i"] % 2
        cs_, cb_ = cbufs[b]
        evl = dma("ld_wc", cs_[0:128, 0:cw], src[r0:r0 + 128, c0:c0 + cw],
                  waits=[cst["c"][b]] if cst["c"][b] else [])
        evc = act(lambda a_, cw=cw, cs_=cs_, cb_=cb_: a_.copy(
            out=cb_[0:128, 0:cw], in_=cs_[0:128, 0:cw]),
            waits=[evl] + ([cst["s"][b]] if cst["s"][b] else []))
        evs_ = dma("st_wc", dst[r0:r0 + 128, c0:c0 + cw], cb_[0:128, 0:cw],
                   waits=[evc], eng="act")
        cst["c"][b], cst["s"][b] = evc, evs_
        cst["i"] += 1
        cst["last"] = evs_

    dqsets = [dict(
        dq8=sb(ph0, "dq8_0", [128, DSL], U8, side="right"),
        dqn=sb(ph0, "dqn_0", [128, DSL], U8, side="right"),
        dqn2=sb(ph0, "dqn2_0", [128, DSL], U8, side="right"),
        dqz=sb(ph0, "dqz_0", [128, DSL], F32, side="right"),
        dqsc=sb(ph0, "dqsc_0", [128, DSL], F32, side="right"),
        dqt=sb(ph0, "dqt_0", [128, DSL], F32, side="right"),
        dqb=sb(ph0, "dqb_0", [128, DSL], BF16, side="right"),
        prev=None)]

    def emit_dq_tile(a, lane, engf):
        S = dqsets[0]
        w0 = [S["prev"]] if S["prev"] else []
        e1 = dma("ld_dq", S["dq8"][:], wqt_d[a * 128:(a + 1) * 128, :], waits=w0)
        e2 = dma("ld_dq2", S["dqz"][:], zro_d[a * 128:(a + 1) * 128, :], waits=w0)
        e3 = dma("ld_dq3", S["dqsc"][:], scl_d[a * 128:(a + 1) * 128, :], waits=w0)
        e4 = engf(lambda v, S=S: v.tensor_scalar(out=S["dqn"][:], in0=S["dq8"][:],
                                                 scalar1=4, scalar2=15,
                                                 op0=ALU.logical_shift_right,
                                                 op1=ALU.bitwise_and),
                  waits=[e1] + w0)
        e4b = engf(lambda v, S=S: v.tensor_scalar(out=S["dqn2"][:], in0=S["dq8"][:],
                                                  scalar1=15, scalar2=None,
                                                  op0=ALU.bitwise_and),
                   waits=[e1] + w0)
        e4c = engf(lambda v, S=S: v.tensor_scalar(out=S["dqt"][:], in0=S["dqn2"][:],
                                                  scalar1=nibsel_sb[:, 1:2], scalar2=None,
                                                  op0=ALU.mult),
                   waits=[e4b, EV["sh"]] + w0)
        e4d = engf(lambda v, S=S: v.scalar_tensor_tensor(out=S["dqt"][:], in0=S["dqn"][:],
                                                         scalar=nibsel_sb[:, 0:1],
                                                         in1=S["dqt"][:], op0=ALU.mult,
                                                         op1=ALU.add),
                   waits=[e4, e4c])
        e5 = engf(lambda v, S=S: v.tensor_tensor(out=S["dqt"][:], in0=S["dqt"][:],
                                                 in1=S["dqz"][:],
                                                 op=ALU.subtract), waits=[e4d, e2])
        e6 = engf(lambda v, S=S: v.tensor_tensor(out=S["dqb"][:], in0=S["dqt"][:],
                                                 in1=S["dqsc"][:],
                                                 op=ALU.mult), waits=[e5, e3])
        e7 = dma("st_dq", wtb_d[a * 128:(a + 1) * 128, :], S["dqb"][:], waits=[e6],
                 eng="act" if lane == 0 else "pool")
        S["prev"] = e7

    with contextlib.ExitStack() as ms:
        xlf_sb = sb(ms, "xlf_sb", [128, NDT * TC], F32, side="right")
        xp1_sb = [sb(ms, f"xp1_{p}", [DTR, TC], F32, side="right") for p in range(2)]
        xp2_sb = [sb(ms, f"xp2_{p}", [32, TC], F32, side="right") for p in range(2)]
        xpg1_sb = [sb(ms, f"xpg1_{p}", [DTR, TC], BF16, side="right") for p in range(2)]
        xpbc_sb = [sb(ms, f"xpbc_{p}", [32, TC], BF16, side="right") for p in range(2)]
        dt_sb = sb(ms, "dt_sb", [128, NDT * TC], BF16, side="right")
        dtx_sb = sb(ms, "dtx_sb", [128, NDT * TC], BF16, side="right")
        sp_sb = sb(ms, "sp_sb", [128, TC], F32, side="right")
        brep_sb = [sb(ms, f"brep_{p}", [128, TC], F32, side="right") for p in range(2)]
        crep_sb = [sb(ms, f"crep_{p}", [128, TC], F32, side="right") for p in range(2)]
        da_sb = [sb(ms, f"da_{q}", [128, 2 * TC], F32, side="right") for q in range(2)]
        dbx_sb = [sb(ms, f"dbx_{q}", [128, 2 * TC], F32, side="right") for q in range(2)]
        h_sb = [sb(ms, f"h_{q}", [128, 2 * TC], F32, side="right") for q in range(2)]
        hc_sb = [sb(ms, f"hc_{q}", [128, 2 * TC], BF16, side="right") for q in range(4)]
        mam_sb = sb(ms, "mam_sb", [128, TC], F32, side="right")

        e_ar0, xcons = stats_prepass(xstream[0])
        m_stores = []
        ar_xp = [None] * NCH
        evx = [None] * NCH

        def xp_stage(cs):
            pc_ = cs % 2
            ev_x = load_x(xstream[0], cs, pc_, waits=xcons[pc_])
            xcons[pc_] = []
            evx[cs] = ev_x
            ev_xlnb, _, xreaders = ln_norm(cs, pc_, ev_x, e_ar0, want_f32=False, neng=pool)
            e_xp1 = e_xp2 = None
            for a in range(NDT):
                e_xp1 = mk_mm("PS5", ps["PS5"][0:DTR, 0:TC],
                              xpw_sb[:, a * XPR:a * XPR + DTR],
                              xlb_sb[pc_][:, a * TC:(a + 1) * TC], start=(a == 0),
                              stop=(a == NDT - 1), waits=[ev_xlnb, EV["xpw"]])
                e_xp2 = mk_mm("PS7", ps["PS7"][0:32, 0:TC],
                              xpw_sb[:, a * XPR + DTR:(a + 1) * XPR],
                              xlb_sb[pc_][:, a * TC:(a + 1) * TC], start=(a == 0),
                              stop=(a == NDT - 1), waits=[])
            e_v1 = act(lambda a_, pc_=pc_: a_.copy(out=xp1_sb[pc_][:], in_=ps["PS5"][0:DTR, 0:TC]),
                       waits=[e_xp1] + WAR[f"xp1{pc_}"])
            ps_read("PS5", e_v1)
            e_v2 = act(lambda a_, pc_=pc_: a_.copy(out=xp2_sb[pc_][:], in_=ps["PS7"][0:32, 0:TC]),
                       waits=[e_xp2] + WAR[f"xp2{pc_}"])
            ps_read("PS7", e_v2)
            xpb_ = cs * XPR
            e_s1 = dma("st_xp", xp_c[xpb_:xpb_ + DTR, :], xp1_sb[pc_][:], waits=[e_v1], eng="act")
            e_s2 = dma("st_xp", xp_c[xpb_ + DTR:xpb_ + XPR, :], xp2_sb[pc_][:], waits=[e_v2], eng="act")
            ar_xp[cs] = cc("AllReduce", ALU.add, [xp_c[xpb_:xpb_ + XPR, :].opt()],
                           [xp_g[xpb_:xpb_ + XPR, :].opt()], waits=[e_s1, e_s2])
            WAR[f"xp1{pc_}"] = [e_s1]
            WAR[f"xp2{pc_}"] = [e_s2]
            WAR[f"xlb{pc_}"] = [e_xp1, e_xp2]
            xcons[pc_] = xreaders

        xp_stage(0)
        CBN = len(cast_blocks)
        for c in range(NCH):
            p = c % 2
            creg = slice(c * TC, (c + 1) * TC)
            xpb = c * XPR
            e_l1 = dma(f"ld_xpg{p}", xp1_sb[p][:], xp_g[xpb:xpb + DTR, :],
                       waits=[ar_xp[c]] + WAR[f"xp1{p}"])
            e_l2 = dma(f"ld_xpg2{p}", xp2_sb[p][:], xp_g[xpb + DTR:xpb + XPR, :],
                       waits=[ar_xp[c]] + WAR[f"xp2{p}"])
            WAR[f"xp1{p}"] = [e_l1]
            WAR[f"xp2{p}"] = [e_l2]
            if c + 1 < NCH:
                xp_stage(c + 1)
            ev_x = evx[c]
            e_c1 = act(lambda a_, p=p: a_.copy(out=xpg1_sb[p][:], in_=xp1_sb[p][:]),
                       waits=[e_l1] + WAR[f"xpg1{p}"])
            e_c2 = act(lambda a_, p=p: a_.copy(out=xpbc_sb[p][:], in_=xp2_sb[p][:]),
                       waits=[e_l2] + WAR[f"xpbc{p}"])
            WAR[f"xp1{p}"] += [e_c1]
            WAR[f"xp2{p}"] += [e_c2]
            e_bb = mk_mm("PS7", ps["PS7"][0:128, 0:TC], rb_sb[:, 0:128], xpbc_sb[p][:],
                         start=True, stop=True, waits=[e_c2, EV["rb"]])
            e_bc2 = mk_mm("PS5", ps["PS5"][0:128, 0:TC], rb_sb[:, 128:256], xpbc_sb[p][:],
                          start=True, stop=True, waits=[e_c2])
            e_brep = act(lambda a_, p=p: a_.copy(out=brep_sb[p][:], in_=ps["PS7"][0:128, 0:TC]),
                         waits=[e_bb] + WAR[f"brep{p}"])
            ps_read("PS7", e_brep)
            e_crep = act(lambda a_, p=p: a_.copy(out=crep_sb[p][:], in_=ps["PS5"][0:128, 0:TC]),
                         waits=[e_bc2] + WAR[f"crep{p}"])
            ps_read("PS5", e_crep)
            WAR[f"xpbc{p}"] = [e_bb, e_bc2]
            dt_evs = []
            for a in range(NDT):
                e_mm = mk_mm("PS7", ps["PS7"][0:128, 0:TC], dtw_sb[:, a * 128:(a + 1) * 128],
                             xpg1_sb[p][:], start=True, stop=True,
                             waits=[e_c1, EV["dtw"]])
                e_p1 = act(lambda a_, a2=a: a_.activation(out=sp_sb[:], in_=ps["PS7"][0:128, 0:TC],
                                                          func=AF.Abs, bias=dtb_sb[:, a2:a2 + 1]),
                           waits=[e_mm, EV["dtb"]] + WAR["sp"])
                e_p2 = act(lambda a_: a_.activation(out=sp_sb[:], in_=sp_sb[:],
                                                    func=AF.Exp, scale=-1.0), waits=[e_p1])
                e_p3 = act(lambda a_: a_.activation(out=sp_sb[:], in_=sp_sb[:],
                                                    func=AF.Ln, bias=one_sb[:, :]),
                           waits=[e_p2])
                e_p4 = act(lambda a_, a2=a: a_.activation(out=dt_sb[:, a2 * TC:(a2 + 1) * TC],
                                                          in_=ps["PS7"][0:128, 0:TC],
                                                          func=AF.Relu, bias=dtb_sb[:, a2:a2 + 1]),
                           waits=[e_p3] + WAR["dt"])
                ps_read("PS7", e_p4)
                e_p5 = pool(lambda v, a2=a: v.tensor_tensor(out=dt_sb[:, a2 * TC:(a2 + 1) * TC],
                                                            in0=dt_sb[:, a2 * TC:(a2 + 1) * TC],
                                                            in1=sp_sb[:], op=ALU.add),
                            waits=[e_p4])
                WAR["sp"] = [e_p5]
                dt_evs.append(e_p5)
            WAR["dt"] = []
            e_dtbf = dt_evs[-1]
            rs3 = rsb_sb[p][:, 0:TC].unsqueeze(1).broadcast_to([128, NDT, TC])
            mrs3 = rsb_sb[p][:, TC:2 * TC].unsqueeze(1).broadcast_to([128, NDT, TC])
            x3 = x_sb[p][:, :].rearrange("q (a t) -> q a t", a=NDT)
            f3 = xlf_sb[:, :].rearrange("q (a t) -> q a t", a=NDT)
            e_f1 = pool(lambda v, f3=f3, x3=x3, rs3=rs3: v.tensor_tensor(
                out=f3, in0=x3, in1=rs3, op=ALU.mult),
                waits=[RSB_EV[p][0], ev_x] + WAR["xlf"])
            e_f2 = pool(lambda v, f3=f3, mrs3=mrs3: v.tensor_tensor(
                out=f3, in0=f3, in1=mrs3, op=ALU.subtract),
                waits=[RSB_EV[p][1], e_f1])
            WAR[f"rsb{p}"] += [e_f2]
            e_dtx = pool(lambda v: v.tensor_tensor(out=dtx_sb[:], in0=dt_sb[:],
                                                   in1=xlf_sb[:], op=ALU.mult),
                         waits=dt_evs + [e_f2] + WAR["dtx"])
            WAR["dt"] = [e_dtx] + WAR["dtbf"]
            y_done = []
            pend_y = []
            pend_cr = []

            def flush_y(pend, upto):
                evs = []
                while len(pend) > upto:
                    (e_hc2, q2, k02, k12, jj2) = pend.pop(0)
                    e_ya = mk_mm("PS4", ps["PS4"][0:128, 0:TC],
                                 sk_sb[:, (k02 % 16) * 128:((k02 % 16) + 1) * 128],
                                 hc_sb[q2][:, 0:TC], start=(jj2 == 0), stop=False,
                                 waits=[e_hc2, EV["sk"]])
                    e_yb = mk_mm("PS4", ps["PS4"][0:128, 0:TC],
                                 sk_sb[:, (k12 % 16) * 128:((k12 % 16) + 1) * 128],
                                 hc_sb[q2][:, TC:2 * TC], start=False, stop=(jj2 == 7),
                                 waits=[e_hc2])
                    WAR[f"hc{q2}"] = [e_yb]
                    evs.append(e_yb)
                return evs

            for a in range(NDT):
                e_y = None
                for jj in range(8):
                    k0 = a * 16 + 2 * jj
                    k1 = k0 + 1
                    q = jj % 2
                    qh = jj % 4
                    dA_, dB_ = ("PS0", "PS1") if jj % 2 == 0 else ("PS6", "PS7")
                    dsrc = psa[0:128, 0:1024] if jj % 2 == 0 else psb[0:128, 0:1024]
                    e_tp0 = mk_mm(dA_, ps[dA_][0:128, 0:TC],
                                  rka_sb[:, k0 * 128:(k0 + 1) * 128],
                                  dt_sb[:, a * TC:(a + 1) * TC], start=True, stop=True,
                                  waits=[e_dtbf, EV["rka"]])
                    e_tp1 = mk_mm(dB_, ps[dB_][0:128, 0:TC],
                                  rka_sb[:, k1 * 128:(k1 + 1) * 128],
                                  dt_sb[:, a * TC:(a + 1) * TC], start=True, stop=True,
                                  waits=[e_dtbf])
                    e_tq0 = mk_mm("PS2", ps["PS2"][0:128, 0:TC],
                                  rk_sb[:, (k0 % 16) * 128:((k0 % 16) + 1) * 128],
                                  dtx_sb[:, a * TC:(a + 1) * TC], start=True, stop=True,
                                  waits=[e_dtx, EV["rk"]])
                    e_tq1 = mk_mm("PS3", ps["PS3"][0:128, 0:TC],
                                  rk_sb[:, (k1 % 16) * 128:((k1 % 16) + 1) * 128],
                                  dtx_sb[:, a * TC:(a + 1) * TC], start=True, stop=True,
                                  waits=[e_dtx])
                    e_da = act(lambda a_, q=q, dsrc=dsrc: a_.activation(
                        out=da_sb[q][:, :].rearrange("p (b c) -> p b c", c=TC),
                        in_=dsrc.rearrange("p (b c) -> p b c", c=512)[:, :, 0:TC],
                        func=AF.Exp), waits=[e_tp0, e_tp1] + WAR[f"da{q}"])
                    ps_read(dA_, e_da)
                    ps_read(dB_, e_da)
                    e_dbx = dve(lambda v, q=q, p=p: v.tensor_tensor(
                        out=dbx_sb[q][:, :].rearrange("p (b c) -> p b c", c=TC),
                        in0=psa[0:128, 1024:2048].rearrange("p (b c) -> p b c", c=512)[:, :, 0:TC],
                        in1=brep_sb[p][:, :].unsqueeze(1).broadcast_to([128, 2, TC]),
                        op=ALU.mult), waits=[e_tq0, e_tq1, e_brep] + WAR[f"dbx{q}"])
                    ps_read("PS2", e_dbx)
                    ps_read("PS3", e_dbx)
                    init0 = 0.0 if c == 0 else carry_sb[:, k0:k0 + 1]
                    init1 = 0.0 if c == 0 else carry_sb[:, k1:k1 + 1]
                    cw0 = WAR.get(f"carry{k0}", []) if c > 0 else []
                    cw1 = WAR.get(f"carry{k1}", []) if c > 0 else []
                    e_h0 = dve(lambda v, q=q, i=init0: v.tensor_tensor_scan(
                        out=h_sb[q][:, 0:TC], data0=da_sb[q][:, 0:TC], data1=dbx_sb[q][:, 0:TC],
                        initial=i, op0=ALU.mult, op1=ALU.add),
                        waits=[e_da, e_dbx] + WAR[f"h{q}"] + cw0)
                    e_h1 = dve(lambda v, q=q, i=init1: v.tensor_tensor_scan(
                        out=h_sb[q][:, TC:2 * TC], data0=da_sb[q][:, TC:2 * TC],
                        data1=dbx_sb[q][:, TC:2 * TC],
                        initial=i, op0=ALU.mult, op1=ALU.add),
                        waits=[e_da, e_dbx] + cw1)
                    pend_cr.append((q, k0, e_h0, e_h1))
                    if len(pend_cr) > 1:
                        (q2, k02, x_h0, x_h1) = pend_cr.pop(0)
                        e_cr = act(lambda a_, q=q2, k=k02: a_.copy(
                            out=carry_sb[:, k:k + 2].unsqueeze(2),
                            in_=h_sb[q][:, :].rearrange("p (b t) -> p b t", t=TC)[:, :, TC - 1:TC]),
                            waits=[x_h0, x_h1])
                        WAR[f"carry{k02}"] = [e_cr]
                        WAR[f"carry{k02 + 1}"] = [e_cr]
                        WAR[f"hcr{q2}"] = [e_cr]
                    e_hc = pool(lambda g, q=q, qh=qh, p=p: g.tensor_tensor(
                        out=hc_sb[qh][:, :].rearrange("p (b c) -> p b c", c=TC),
                        in0=h_sb[q][:, :].rearrange("p (b c) -> p b c", c=TC),
                        in1=crep_sb[p][:, :].unsqueeze(1).broadcast_to([128, 2, TC]),
                        op=ALU.mult), waits=[e_h0, e_h1, e_crep] + WAR[f"hc{qh}"])
                    pend_y.append((e_hc, qh, k0, k1, jj))
                    ys = flush_y(pend_y, 2)
                    if ys:
                        e_y = ys[-1]
                    WAR[f"da{q}"] = [e_h0, e_h1]
                    WAR[f"dbx{q}"] = [e_h0, e_h1]
                    WAR[f"h{q}"] = [e_hc] + WAR[f"hcr{q}"]
                    e_tp_last, e_tq_last = e_tp1, e_tq1
                    e_dbx_last, e_hc_last = e_dbx, e_hc
                ys = flush_y(pend_y, 0)
                if ys:
                    e_y = ys[-1]
                e_ev = dve(lambda v, a=a, p=p: v.scalar_tensor_tensor(
                    out=mam_sb[:], in0=xlf_sb[:, a * TC:(a + 1) * TC],
                    scalar=dvec_sb[:, a:a + 1], in1=ps["PS4"][0:128, 0:TC],
                    op0=ALU.mult, op1=ALU.add),
                    waits=[e_y, e_f2, EV["dv"]] + WAR["mam"])
                ps_read("PS4", e_ev)
                e_add = dve(lambda v, a=a, p=p: v.tensor_tensor(
                    out=x_sb[p][:, a * TC:(a + 1) * TC], in0=x_sb[p][:, a * TC:(a + 1) * TC],
                    in1=mam_sb[:], op=ALU.add), waits=[e_ev])
                WAR["mam"] = [e_add]
                y_done.append(e_add)
            while pend_cr:
                (q2, k02, x_h0, x_h1) = pend_cr.pop(0)
                e_cr = act(lambda a_, q=q2, k=k02: a_.copy(
                    out=carry_sb[:, k:k + 2].unsqueeze(2),
                    in_=h_sb[q][:, :].rearrange("p (b t) -> p b t", t=TC)[:, :, TC - 1:TC]),
                    waits=[x_h0, x_h1])
                WAR[f"carry{k02}"] = [e_cr]
                WAR[f"carry{k02 + 1}"] = [e_cr]
                WAR[f"hcr{q2}"] = [e_cr]
            e_stx = dma(f"st_x{p}", xstream[1][:, creg].rearrange("(a p) t -> p a t", p=128),
                        x_sb[p][:, :].rearrange("p (a t) -> p a t", t=TC), waits=y_done, eng="act")
            m_stores.append(e_stx)
            xcons[p] += y_done + [e_stx]
            WAR["xlf"] = [e_dtx] + y_done
            WAR["dtbf"] = [e_tp_last]
            WAR["dt"] = WAR["dt"] + [e_tp_last]
            WAR["dtx"] = [e_tq_last]
            WAR[f"brep{p}"] = [e_dbx_last]
            WAR[f"crep{p}"] = [e_hc_last]
            WAR[f"xpg1{p}"] = [e_mm]
            for k_ in range(c * CBN // NCH, (c + 1) * CBN // NCH):
                emit_cast_block(k_)
            emit_dq_tile(2 * c, 0, dve)
            emit_dq_tile(2 * c + 1, 0, dve)
        P.full_barrier(m_stores + [cst["last"], dqsets[0]["prev"]])
    ph0.close()

    AGB = 2 if NCH % 2 == 0 else 1

    def ln_ag_pass(xin, xcons_unused):
        e_ar, xcons = stats_prepass(xin)
        ag_evs = []
        pend_st = []
        for c in range(NCH):
            p = c % 2
            ev_x = load_x(xin, c, p, waits=xcons[p])
            ev_xlnb, _, xreaders = ln_norm(c, p, ev_x, e_ar, want_f32=False)
            e_st = dma(f"st_xln{c % 2}", xln_c[c * DSL:(c + 1) * DSL, :].rearrange("(a p) t -> p a t", p=128),
                       xlb_sb[p][:, :].rearrange("p (a t) -> p a t", t=TC),
                       waits=[ev_xlnb], eng="pool")
            pend_st.append(e_st)
            if (c + 1) % AGB == 0:
                g = c // AGB
                e_ag = cc("AllGather", ALU.bypass,
                          [xln_c[g * AGB * DSL:(g + 1) * AGB * DSL, :].opt()],
                          [xg_d[g * AGB * D:(g + 1) * AGB * D, :].opt()], waits=pend_st)
                pend_st = []
                ag_evs += [e_ag] * AGB
            WAR[f"xlb{p}"] = [e_st]
            xcons[p] = [ev_xlnb] + xreaders
        return ag_evs, xcons

    def ffn_phase(xin, xout, tag):
        with contextlib.ExitStack() as fs:
            xg_sb = [sb(fs, f"xg_{tag}{p}", [128, NKT * TC], BF16, side="right") for p in range(2)]
            wt_sb = sb(fs, f"wt_{tag}", [128, NKT * DSL], BF16, side="right")
            e_wt = dma("ld_wt" + tag, wt_sb[:, :].rearrange("p (a o) -> p a o", o=DSL),
                       wtb_d[:, :].rearrange("(a p) o -> p a o", p=128))
            ag_evs, xcons = ln_ag_pass(xin, [[], []])
            stores = []
            for c in range(NCH):
                p = c % 2
                ev_x = load_x(xin, c, p, waits=xcons[p])
                xcons[p] = []
                ldg_evs_ = []
                g, cc_ = c // AGB, c % AGB
                for r_ in range(NCORES):
                    base_ = g * AGB * D + r_ * AGB * DSL + cc_ * DSL
                    e_ldg = dma(f"ld_xg{p}_{r_}",
                                xg_sb[p][:, r_ * NDT * TC:(r_ + 1) * NDT * TC].rearrange(
                                    "p (q t) -> p q t", t=TC),
                                xg_d[base_:base_ + DSL, :].rearrange("(q p) t -> p q t", p=128),
                                waits=[ag_evs[c]] + WAR[f"xg{p}{r_}"])
                    WAR[f"xg{p}{r_}"] = []
                    ldg_evs_.append((r_, e_ldg))
                ldg_map = dict(ldg_evs_)
                res_evs = []
                for o in range(NDT):
                    slot = f"PS{o % 4}"
                    e_mm = None
                    for a in range(NKT):
                        e_mm = mk_mm(slot, ps[slot][0:128, 0:TC],
                                     wt_sb[:, a * DSL + o * 128:a * DSL + (o + 1) * 128],
                                     xg_sb[p][:, a * TC:(a + 1) * TC],
                                     start=(a == 0), stop=(a == NKT - 1),
                                     waits=[ldg_map[a // NDT], e_wt])
                    e_res = dve(lambda v, o=o, p=p, slot=slot: v.tensor_tensor(
                        out=x_sb[p][:, o * TC:(o + 1) * TC],
                        in0=x_sb[p][:, o * TC:(o + 1) * TC],
                        in1=ps[slot][0:128, 0:TC], op=ALU.add), waits=[e_mm, ev_x])
                    ps_read(slot, e_res)
                    res_evs.append(e_res)
                creg = slice(c * TC, (c + 1) * TC)
                e_stx = dma(f"st_x{p}", xout[:, creg].rearrange("(a p) t -> p a t", p=128),
                            x_sb[p][:, :].rearrange("p (a t) -> p a t", t=TC),
                            waits=res_evs, eng="pool")
                stores.append(e_stx)
                xcons[p] = res_evs + [e_stx]
                for r_ in range(NCORES):
                    WAR[f"xg{p}{r_}"] = res_evs
            P.full_barrier(stores)

    ffn_phase(xstream[1], xstream[2], "f1")

    with contextlib.ExitStack() as js:
        xg_sb = [sb(js, f"xg_pj{p}", [128, NKT * TC], BF16, side="right") for p in range(2)]
        sl1_sb = [sb(js, f"sl1_{i}", [128, 2 * HLOC * DN], BF16, side="right") for i in range(4)]
        sl2_sb = [sb(js, f"sl2_{i}", [128, HLOC * DV], BF16, side="right") for i in range(4)]
        prj_sb = [sb(js, f"prj_{i}", [128, TC], BF16, side="right") for i in range(2)]
        vprj_sb = [sb(js, f"vprj_{i}", [128, HLOC * DV], BF16, side="right") for i in range(2)]

        ag_evs, xcons = ln_ag_pass(xstream[2], [[], []])
        prj_stores = []
        for c in range(NCH):
            p = c % 2
            creg = slice(c * TC, (c + 1) * TC)
            ldg_evs_ = []
            g, cc_ = c // AGB, c % AGB
            for r_ in range(NCORES):
                base_ = g * AGB * D + r_ * AGB * DSL + cc_ * DSL
                e_ldg = dma(f"ld_xg{p}_{r_}",
                            xg_sb[p][:, r_ * NDT * TC:(r_ + 1) * NDT * TC].rearrange(
                                "p (q t) -> p q t", t=TC),
                            xg_d[base_:base_ + DSL, :].rearrange("(q p) t -> p q t", p=128),
                            waits=[ag_evs[c]] + WAR[f"xg{p}{r_}"])
                WAR[f"xg{p}{r_}"] = []
                ldg_evs_.append((r_, e_ldg))
            ldg_map = dict(ldg_evs_)
            cons = []
            e_mm = None
            for a in range(NKT):
                pp = a % 4
                e_s1 = dma(f"ld_sl1{pp}", sl1_sb[pp][:, 0:HLOC * DN],
                           wkb_d[a * 128:(a + 1) * 128, :], waits=WAR[f"sl1{pp}"])
                e_s2 = dma(f"ld_sl1b{pp}", sl1_sb[pp][:, HLOC * DN:2 * HLOC * DN],
                           wqb_d[a * 128:(a + 1) * 128, :], waits=WAR[f"sl1{pp}"])
                WAR[f"sl1{pp}"] = []
                for ki in range(2):
                    for hh in range(HLOC):
                        slot = f"PS{ki * 4 + hh}"
                        e_mm = mk_mm(slot, ps[slot][0:128, 0:TC],
                                     sl1_sb[pp][:, (ki * HLOC + hh) * DN:(ki * HLOC + hh + 1) * DN],
                                     xg_sb[p][:, a * TC:(a + 1) * TC],
                                     start=(a == 0), stop=(a == NKT - 1),
                                     waits=[e_s1, e_s2, ldg_map[a // NDT]])
                        WAR[f"sl1{pp}"].append(e_mm)
            for ki, odram in ((0, k_dram), (1, q_dram)):
                for hh in range(HLOC):
                    slot = f"PS{ki * 4 + hh}"
                    bi = (ki * HLOC + hh) % 2
                    e_evac = act(lambda a_, slot=slot, bi=bi: a_.copy(
                        out=prj_sb[bi][:], in_=ps[slot][0:128, 0:TC]),
                        waits=[e_mm] + WAR[f"prj{bi}"])
                    ps_read(slot, e_evac)
                    e_sto = dma(f"st_prj{bi}", odram[hh * DN:(hh + 1) * DN, creg],
                                prj_sb[bi][:], waits=[e_evac], eng="pool")
                    WAR[f"prj{bi}"] = [e_sto]
                    cons.append(e_sto)
            e_mmv = None
            for a in range(NKT):
                pp = a % 4
                e_sv = dma(f"ld_sl2{pp}", sl2_sb[pp][:],
                           wvb_d[a * 128:(a + 1) * 128, :], waits=WAR[f"sl2{pp}"])
                WAR[f"sl2{pp}"] = []
                for tt in range(KT):
                    slot = f"PS{tt}"
                    e_mmv = mk_mm(slot, ps[slot][0:128, 0:HLOC * DV],
                                  xg_sb[p][:, a * TC + tt * 128:a * TC + (tt + 1) * 128],
                                  sl2_sb[pp][:],
                                  start=(a == 0), stop=(a == NKT - 1),
                                  waits=[e_sv, ldg_map[a // NDT]])
                    WAR[f"sl2{pp}"].append(e_mmv)
            for tt in range(KT):
                slot = f"PS{tt}"
                bi = tt % 2
                e_evv = act(lambda a_, slot=slot, bi=bi: a_.copy(
                    out=vprj_sb[bi][:], in_=ps[slot][0:128, 0:HLOC * DV]),
                    waits=[e_mmv] + WAR[f"vprj{bi}"])
                ps_read(slot, e_evv)
                e_stv = None
                for hh in range(HLOC):
                    e_stv = dma(f"st_vprj{bi}",
                                v_dram[hh, c, :, tt * DV:(tt + 1) * DV],
                                vprj_sb[bi][:, hh * DV:(hh + 1) * DV], waits=[e_evv], eng="pool")
                WAR[f"vprj{bi}"] = [e_stv]
                cons.append(e_stv)
            prj_stores += cons
            for r_ in range(NCORES):
                WAR[f"xg{p}{r_}"] = [e_mm, e_mmv]
        P.full_barrier(prj_stores)

    with contextlib.ExitStack() as as_:
        q_sb = [sb(as_, f"q_at{h}", [128, TC], BF16, side="right") for h in range(HLOC)]
        k_sb = [sb(as_, f"k_at{p}", [128, TC], BF16, side="right") for p in range(6)]
        v_sb = [sb(as_, f"v_at{p}", [128, KT * 128], BF16, side="right") for p in range(6)]
        if KT == 4:
            pbF_sb = [sb(as_, f"pbF{r}", [128, 4 * TC], BF16, side="right")
                      for r in range(4)]
            ssA_sb = [sb(as_, f"ssA{r}", [128, TC], F32, side="right") for r in range(1)]
            ssB_sb = [sb(as_, f"ssB{r}", [128, 2 * TC], BF16, side="right") for r in range(2)]
            dacc_sb = [sb(as_, f"dacc{b}", [128, TC], F32, side="right") for b in range(2)]
            pb_sb = None
            mask4_sb = sb(as_, "mask4_sb", [128, KT * TC], F32, side="right")
            EV["mask4"] = dma("ld_c", mask4_sb[:], mask4_d[:, :])
            ev_pz = EV["memset"]
        else:
            pbA_sb = sb(as_, "pbA_at", [128, 4 * TC], BF16, side="right")
            pbB_sb = [sb(as_, f"pbB_at{p}", [128, 2 * TC], BF16, side="right") for p in range(2)]
            pb_sb = [pbA_sb[:, 0:2 * TC], pbA_sb[:, 2 * TC:4 * TC], pbB_sb[0][:, :], pbB_sb[1][:, :]]
            pdiag_sb = [sb(as_, f"pd_at{t}", [128, TC], BF16, side="right") for t in range(KT)]
            e_z = EV["memset"]
            for t in range(KT):
                e_z = dve(lambda v, t=t: v.memset(pdiag_sb[t][:], 0.0), waits=[e_z])
            ev_pz = e_z
        invd_sb = [sb(as_, f"invd{i}", [1, TC], F32, side="right") for i in range(2)]
        invb_sb = [sb(as_, f"invb{i}", [128, TC], F32, side="right") for i in range(2)]
        attn_sb = [sb(as_, f"attn_{h}", [128, TC], BF16, side="right") for h in range(HLOC)]
        wo_sb = sb(as_, "wo_sb", [128, HLOC * D], BF16, side="right")
        wop_sb = [sb(as_, f"wop_{i}", [128, TC], BF16, side="right") for i in range(2)]
        rs_sb = [sb(as_, f"rs_{p}", [128, NDT * TC], BF16, side="right") for p in range(2)]

        e_wo = dma("ld_wo_res", wo_sb[:, :].rearrange("p (h d) -> p h d", d=D),
                   wob_d[:, :].rearrange("(h p) d -> p h d", p=128))

        SETS = [("PS0", "PS1"), ("PS2", "PS3"), ("PS6", "PS7")]
        SETBASE = [(psa, 0), (psa, 1024), (psb, 0)]
        attn_rot = [0]
        att_stores = []
        xcons = [[], []]
        for qc in range(NCH):
            p = qc % 2
            ev_x = load_x(xstream[2], qc, p, waits=xcons[p], eng="pool")
            xcons[p] = []
            attn_ready = []
            e_qs = {}
            for hh in range(HLOC):
                e_qs[hh] = dma(f"ld_q{hh}", q_sb[hh][:],
                               q_dram[hh * DN:(hh + 1) * DN, qc * TC:(qc + 1) * TC],
                               waits=WAR[f"qat{hh}"])
                WAR[f"qat{hh}"] = []
            for hh in range(HLOC):
                hb = hh % 2 if KT == 4 else 0
                PVS = ("PS4", "PS6")[hb]
                DENS = ("PS5", "PS7")[hb]
                e_q = e_qs[hh]
                first = [True]
                e_pv = e_den = None
                pend_pv = []

                e_accprev = None

                def flush_pv(pend, upto):
                    nonlocal e_pv, e_den
                    while len(pend) > upto:
                        (x_exp, x_v, x_kp, parts, wkey, fin) = pend.pop(0)
                        for j, (kt, ap) in enumerate(parts):
                            stp = fin and j == len(parts) - 1
                            e_pv = mk_mm(PVS, ps[PVS][0:DV, 0:TC],
                                         v_sb[x_kp][:, kt * 128:(kt + 1) * 128],
                                         ap, start=first[0], stop=stp,
                                         waits=[x_exp, x_v])
                            if KT != 4:
                                e_den = mk_mm(DENS, ps[DENS][0:1, 0:TC], ones_bf[:, 0:1],
                                              ap, start=first[0], stop=stp,
                                              waits=[x_exp, EV["memset"]])
                            first[0] = False
                            WAR[f"vat{x_kp}"].append(e_pv)
                        if KT != 4:
                            WAR[wkey] = [e_pv, e_den]
                        else:
                            WAR[wkey].append(e_pv)

                for kc in range(qc + 1):
                    kp = kc % 6
                    e_k = dma(f"ld_k{kp}", k_sb[kp][:],
                              k_dram[hh * DN:(hh + 1) * DN, kc * TC:(kc + 1) * TC],
                              waits=WAR[f"kat{kp}"])
                    e_v = dma(f"ld_v{kp}", v_sb[kp][:],
                              v_dram[hh, kc, :, :],
                              waits=WAR[f"vat{kp}"])
                    WAR[f"kat{kp}"] = []
                    WAR[f"vat{kp}"] = []
                    diag = (kc == qc)
                    if KT == 4:
                        r = kc % 4
                        pbr = pbF_sb[r]
                        qk_evs = []
                        for kt in range(4):
                            sl = f"PS{kt}"
                            e_s_ = mk_mm(sl, ps[sl][0:128, 0:TC],
                                         k_sb[kp][:, kt * 128:(kt + 1) * 128],
                                         q_sb[hh][:], start=True, stop=True,
                                         waits=[e_k, e_q])
                            qk_evs.append((sl, e_s_))
                        WAR[f"kat{kp}"] += [e for _, e in qk_evs]
                        WAR[f"qat{hh}"] += [e for _, e in qk_evs]
                        prod = [e for _, e in qk_evs]
                        if diag:
                            e_msk = dve(lambda v: v.tensor_tensor(
                                out=psa[0:128, 0:2048].rearrange(
                                    "p (b c) -> p b c", c=512)[:, :, 0:TC],
                                in0=psa[0:128, 0:2048].rearrange(
                                    "p (b c) -> p b c", c=512)[:, :, 0:TC],
                                in1=mask4_sb[:, :].rearrange("p (b c) -> p b c", c=TC),
                                op=ALU.add), waits=prod + [EV["mask4"]])
                            for sl, _ in qk_evs:
                                ps_write_dve(sl, e_msk)
                            prod = [e_msk]
                        e_exp = act(lambda a_, pbr=pbr: a_.activation(
                            out=pbr[:, :].rearrange("p (b c) -> p b c", c=TC),
                            in_=psa[0:128, 0:2048].rearrange(
                                "p (b c) -> p b c", c=512)[:, :, 0:TC],
                            func=AF.Exp, scale=SCALE, bias=nshift_sb[:, :]),
                            waits=prod + WAR[f"pbat{r}"])
                        for sl, _ in qk_evs:
                            ps_read(sl, e_exp)
                        rb2 = r % 2
                        e_sa1 = dve(lambda v, pbr=pbr, rb2=rb2: v.tensor_tensor(
                            out=ssB_sb[rb2][:, :].rearrange("p (b c) -> p b c", c=TC),
                            in0=pbr[:, 0:2 * TC].rearrange("p (b c) -> p b c", c=TC),
                            in1=pbr[:, 2 * TC:4 * TC].rearrange("p (b c) -> p b c", c=TC),
                            op=ALU.add), waits=[e_exp] + WAR[f"ssB{rb2}"])
                        if kc == 0:
                            e_ss = dve(lambda v, rb2=rb2, hb=hb: v.tensor_tensor(
                                out=dacc_sb[hb][:], in0=ssB_sb[rb2][:, 0:TC],
                                in1=ssB_sb[rb2][:, TC:2 * TC],
                                op=ALU.add), waits=[e_sa1] + WAR[f"dacc{hb}"])
                            e_accprev = e_ss
                        else:
                            e_ss = dve(lambda v, rb2=rb2: v.tensor_tensor(
                                out=ssA_sb[0][:], in0=ssB_sb[rb2][:, 0:TC],
                                in1=ssB_sb[rb2][:, TC:2 * TC],
                                op=ALU.add), waits=[e_sa1] + WAR["ssA0"])
                            e_acc = dve(lambda v, hb=hb: v.tensor_tensor(
                                out=dacc_sb[hb][:], in0=dacc_sb[hb][:], in1=ssA_sb[0][:],
                                op=ALU.add), waits=[e_ss, e_accprev])
                            WAR["ssA0"] = [e_acc]
                            e_accprev = e_acc
                        WAR[f"ssB{rb2}"] = [e_ss]
                        WAR[f"pbat{r}"] = [e_sa1]
                        pend_pv.append((e_exp, e_v, kp,
                                        [(0, pbr[:, 0:TC]), (1, pbr[:, TC:2 * TC])],
                                        f"pbat{r}", False))
                        pend_pv.append((e_exp, e_v, kp,
                                        [(2, pbr[:, 2 * TC:3 * TC]),
                                         (3, pbr[:, 3 * TC:4 * TC])],
                                        f"pbat{r}", diag))
                        flush_pv(pend_pv, 4)
                    elif not diag:
                        for half in range(KT // 2):
                            rot = attn_rot[0] % 3
                            attn_rot[0] += 1
                            sA, sB = SETS[rot]
                            kt0, kt1 = 2 * half, 2 * half + 1
                            e_sa = mk_mm(sA, ps[sA][0:128, 0:TC],
                                         k_sb[kp][:, kt0 * 128:(kt0 + 1) * 128],
                                         q_sb[hh][:], start=True, stop=True,
                                         waits=[e_k, e_q])
                            e_sb2 = mk_mm(sB, ps[sB][0:128, 0:TC],
                                          k_sb[kp][:, kt1 * 128:(kt1 + 1) * 128],
                                          q_sb[hh][:], start=True, stop=True,
                                          waits=[e_k, e_q])
                            WAR[f"kat{kp}"] += [e_sa, e_sb2]
                            WAR[f"qat{hh}"] += [e_sa, e_sb2]
                            base, off = SETBASE[rot]
                            e_exp = act(lambda a_, base=base, off=off, rot=rot: a_.activation(
                                out=pb_sb[rot][:, :].rearrange("p (b c) -> p b c", c=TC),
                                in_=base[0:128, off:off + 1024].rearrange(
                                    "p (b c) -> p b c", c=512)[:, :, 0:TC],
                                func=AF.Exp, scale=SCALE, bias=nshift_sb[:, :]),
                                waits=[e_sa, e_sb2] + WAR[f"pbat{rot}"])
                            ps_read(sA, e_exp)
                            ps_read(sB, e_exp)
                            pend_pv.append((e_exp, e_v, kp,
                                            [(kt0, pb_sb[rot][:, 0:TC]),
                                             (kt1, pb_sb[rot][:, TC:2 * TC])],
                                            f"pbat{rot}", False))
                            flush_pv(pend_pv, 1)
                    else:
                        flush_pv(pend_pv, 0)
                    if diag and KT != 4:
                        for kt in range(KT):
                            sslot = f"PS{kt % 2}"
                            j0 = kt * 128
                            e_s = mk_mm(sslot, ps[sslot][0:128, j0:TC],
                                        k_sb[kp][:, kt * 128:(kt + 1) * 128],
                                        q_sb[hh][:, j0:TC], start=True, stop=True,
                                        waits=[e_k, e_q])
                            WAR[f"kat{kp}"].append(e_s)
                            WAR[f"qat{hh}"].append(e_s)
                            e_msk = dve(lambda v, s=sslot, kt=kt: v.tensor_tensor(
                                out=ps[s][0:128, kt * 128:(kt + 1) * 128],
                                in0=ps[s][0:128, kt * 128:(kt + 1) * 128],
                                in1=tri_sb[:], op=ALU.add), waits=[e_s, EV["tri"]])
                            ps_write_dve(sslot, e_msk)
                            pbuf = pdiag_sb[kt]
                            e_exp = act(lambda a_, s=sslot, pb=pbuf, j0=j0: a_.activation(
                                out=pb[:, j0:TC], in_=ps[s][0:128, j0:TC],
                                func=AF.Exp, scale=SCALE, bias=nshift_sb[:, :]),
                                waits=[e_msk, ev_pz] + WAR[f"pd{kt}"])
                            ps_read(sslot, e_exp)
                            last2 = (kt == KT - 1)
                            e_pv = mk_mm(PVS, ps[PVS][0:DV, j0:TC],
                                         v_sb[kp][:, kt * 128:(kt + 1) * 128],
                                         pbuf[:, j0:TC], start=first[0], stop=last2,
                                         waits=[e_exp, e_v])
                            e_den = mk_mm(DENS, ps[DENS][0:1, j0:TC], ones_bf[:, 0:1],
                                          pbuf[:, j0:TC], start=first[0], stop=last2,
                                          waits=[e_exp, EV["memset"]])
                            first[0] = False
                            WAR[f"vat{kp}"].append(e_pv)
                            WAR[f"pd{kt}"] = [e_pv, e_den]
                flush_pv(pend_pv, 0)
                if KT == 4:
                    e_den = mk_mm(DENS, ps[DENS][0:1, 0:TC], ones_fp[:, 0:1],
                                  dacc_sb[hb][:], start=True, stop=True,
                                  waits=[e_accprev, EV["memset"]])
                    WAR[f"dacc{hb}"] = [e_den]
                e_rec = dve(lambda v, hb=hb, DENS=DENS: v.reciprocal(
                    out=invd_sb[hb][:], in_=ps[DENS][0:1, 0:TC]),
                    waits=[e_den] + WAR[f"invd{hb}"])
                ps_read(DENS, e_rec)
                e_bc = mk_mm(DENS, ps[DENS][0:128, 0:TC], ones_f1[0:1, :],
                             invd_sb[hb][0:1, :],
                             start=True, stop=True, waits=[e_rec])
                WAR[f"invd{hb}"] = [e_bc]
                e_ebc = dve(lambda v, hb=hb, DENS=DENS: v.tensor_copy(
                    out=invb_sb[hb][:], in_=ps[DENS][0:128, 0:TC]),
                    waits=[e_bc] + WAR[f"invb{hb}"])
                ps_read(DENS, e_ebc)
                e_at = dve(lambda v, hh=hh, hb=hb, PVS=PVS: v.tensor_tensor(
                    out=attn_sb[hh][:],
                    in0=ps[PVS][0:DV, 0:TC],
                    in1=invb_sb[hb][:], op=ALU.mult),
                    waits=[e_ebc, e_pv] + WAR[f"ah{hh}"])
                ps_read(PVS, e_at)
                WAR[f"invb{hb}"] = [e_at]
                attn_ready.append(e_at)
            wo_stores = []
            for grp in range(NKT // 4):
                for dsub in range(4):
                    dti = grp * 4 + dsub
                    slot = f"PS{dsub}"
                    e_wom = None
                    for hh in range(HLOC):
                        e_wom = mk_mm(slot, ps[slot][0:128, 0:TC],
                                      wo_sb[:, hh * D + dti * 128:hh * D + (dti + 1) * 128],
                                      attn_sb[hh][:], start=(hh == 0), stop=(hh == HLOC - 1),
                                      waits=[e_wo, attn_ready[hh]])
                        WAR[f"ah{hh}"].append(e_wom)
                    bi2 = dsub % 2
                    e_ew = dve(lambda v, slot=slot, bi2=bi2: v.tensor_copy(
                        out=wop_sb[bi2][:], in_=ps[slot][0:128, 0:TC]),
                        waits=[e_wom] + WAR[f"wop{bi2}"])
                    ps_read(slot, e_ew)
                    e_sw = dma(f"st_wop{bi2}", wop_c[qc * D + dti * 128:qc * D + (dti + 1) * 128, :],
                               wop_sb[bi2][:], waits=[e_ew], eng="pool")
                    WAR[f"wop{bi2}"] = [e_sw]
                    wo_stores.append(e_sw)
            creg = slice(qc * TC, (qc + 1) * TC)
            e_rs = cc("ReduceScatter", ALU.add, [wop_c[qc * D:(qc + 1) * D, :].opt()],
                      [wop_g[qc * DSL:(qc + 1) * DSL, :].opt()], waits=wo_stores)
            e_lrs = dma(f"ld_rs{p}", rs_sb[p][:, :].rearrange("p (a t) -> p a t", t=TC),
                        wop_g[qc * DSL:(qc + 1) * DSL, :].rearrange("(a p) t -> p a t", p=128),
                        waits=[e_rs] + WAR[f"rs{p}"], eng="pool")
            e_resid = dve(lambda v, p=p: v.tensor_tensor(out=x_sb[p][:], in0=x_sb[p][:],
                                                         in1=rs_sb[p][:], op=ALU.add),
                          waits=[e_lrs, ev_x])
            WAR[f"rs{p}"] = [e_resid]
            e_stx = dma(f"st_x{p}", xstream[3][:, creg].rearrange("(a p) t -> p a t", p=128),
                        x_sb[p][:, :].rearrange("p (a t) -> p a t", t=TC),
                        waits=[e_resid], eng="pool")
            att_stores.append(e_stx)
            xcons[p] = [e_resid, e_stx]
            for hh in range(HLOC):
                WAR[f"ah{hh}"] = list(wo_stores)
        P.full_barrier(att_stores)

    ffn_phase(xstream[3], xstream[4], "f2")
    P.op("pool", lambda g: g.nop())

    emit_engine = P.emit_into(nc, ctx)
    with nc.Block() as block:
        @block.sync
        def _(sync):
            emit_engine("sync", sync)

        @block.tensor
        def _(tensor):
            emit_engine("pe", tensor)

        @block.scalar
        def _(scalar):
            emit_engine("act", scalar)

        @block.vector
        def _(vector):
            emit_engine("dve", vector)

        @block.gpsimd
        def _(gpsimd):
            emit_engine("pool", gpsimd)

    ctx.close()
    return nc


def prep_inputs(cfg, inp):
    T, D, NH, TC = cfg["T"], cfg["D"], cfg["NH"], cfg["TC"]
    DSL, HLOC = cfg["DSL"], cfg["HLOC"]
    NDT, NDNT, NST, DTR = cfg["NDT"], cfg["NDNT"], cfg["NST"], cfg["DTR"]
    DN, DV, DQ, G = cfg["DN"], cfg["DV"], cfg["DQ"], cfg["G"]
    OUT2 = D // 2

    x = np.ascontiguousarray(np.asarray(inp["L0_attn_a_x"], np.float32)[0].T)
    wq = np.asarray(inp["L0_attn_b_wq"], np.float32)
    wkv = np.asarray(inp["L0_attn_b_wkv"], np.float32)
    wo = np.asarray(inp["L0_attn_b_wo"], np.float32)
    xpw = np.asarray(inp["L0_attn_a_x_proj_w"], np.float32)
    dtw = np.asarray(inp["L0_attn_a_dt_proj_w"], np.float32)
    dtb = np.asarray(inp["L0_attn_a_dt_proj_b"], np.float32)
    alog = np.asarray(inp["L0_attn_a_A_log"], np.float32)
    dvec = np.asarray(inp["L0_attn_a_D"], np.float32)
    Wq = np.asarray(inp["L0_ffn_W_q"], np.uint8)
    scl = np.asarray(inp["L0_ffn_scale"], np.float32).reshape(D, D // G)
    zro = np.asarray(inp["L0_ffn_zero"], np.float32).reshape(D, D // G)
    byte_T = np.ascontiguousarray(Wq.reshape(OUT2, D).T)

    rk = np.zeros((128, 16, 128), np.float32)
    sk = np.zeros((128, 16, 128), np.float32)
    for kk in range(16):
        for po in range(128):
            rk[8 * kk + po // 16, kk, po] = 1.0
            sk[po, kk, 8 * kk + po // 16] = 1.0
    rk = rk.reshape(128, 2048).astype(NPBF16)
    sk = sk.reshape(128, 2048).astype(NPBF16)
    rb = np.zeros((32, 2, 128), np.float32)
    for po in range(128):
        rb[po % 16, 0, po] = 1.0
        rb[16 + po % 16, 1, po] = 1.0
    rb = rb.reshape(32, 256).astype(NPBF16)
    tri = np.where(np.arange(128)[:, None] > np.arange(128)[None, :], -1e30, 0.0
                   ).astype(np.float32)
    if cfg["KT"] == 4:
        KT4 = cfg["KT"]
        kk = (np.arange(KT4)[:, None, None] * 128 + np.arange(128)[None, :, None])
        qq = np.arange(TC)[None, None, :]
        mask4 = np.where(kk > qq, -1e30, 0.0).astype(np.float32)
        mask4 = np.ascontiguousarray(mask4.transpose(1, 0, 2).reshape(128, KT4 * TC))

    in_maps = []
    for i in range(NCORES):
        dsl = slice(i * DSL, (i + 1) * DSL)
        hsl = range(i * HLOC, (i + 1) * HLOC)
        m = {}
        m["x0"] = np.ascontiguousarray(x[dsl])
        m["wq_sl"] = np.ascontiguousarray(
            np.concatenate([wq[:, h * DQ:h * DQ + DN] for h in hsl], axis=1))
        m["wk_sl"] = np.ascontiguousarray(
            np.concatenate([wkv[:, h * DN:(h + 1) * DN] for h in hsl], axis=1))
        m["wv_sl"] = np.ascontiguousarray(
            np.concatenate([wkv[:, NH * DN + h * DV:NH * DN + (h + 1) * DV] for h in hsl],
                           axis=1))
        m["wo_sl"] = np.ascontiguousarray(wo[i * HLOC * DV:(i + 1) * HLOC * DV, :])
        m["xpw_sl"] = np.ascontiguousarray(xpw[:, dsl].T)
        m["dtw_sl"] = np.ascontiguousarray(dtw[dsl, :].T)
        m["dtb_sl"] = np.ascontiguousarray(dtb[dsl].reshape(NDT, 128).T)
        m["alog_sl"] = np.ascontiguousarray(
            alog[dsl].reshape(NDNT, 128).T)
        m["dvec_sl"] = np.ascontiguousarray(dvec[dsl].reshape(NDT, 128).T)
        if i < NCORES // 2:
            cols = slice(i * DSL, (i + 1) * DSL)
            shift = 4
        else:
            cols = slice((i - NCORES // 2) * DSL, (i - NCORES // 2 + 1) * DSL)
            shift = 0
        m["wqt_sl"] = np.ascontiguousarray(byte_T[:, cols])
        osl = np.arange(i * DSL, (i + 1) * DSL)
        m["scl_sl"] = np.ascontiguousarray(np.repeat(scl[osl].T, G, axis=0))
        m["zro_sl"] = np.ascontiguousarray(np.repeat(zro[osl].T, G, axis=0))
        msel = 1.0 if shift == 4 else 0.0
        m["nibsel"] = np.ascontiguousarray(
            np.tile(np.array([[msel, 1.0 - msel]], np.float32), (128, 1)))
        m["rk_all"] = rk
        m["ident"] = np.eye(128, dtype=np.float32)
        m["sk_all"] = sk
        m["rb_bc"] = rb
        m["trimask"] = tri
        if cfg["KT"] == 4:
            m["mask4"] = mask4
        in_maps.append(m)
    return in_maps


DEFAULT_CFG = make_cfg()


LAST_PROG = [None]


def kernel(**inputs):
    cfg = DEFAULT_CFG
    in_maps = prep_inputs(cfg, inputs)
    nc = build_program(cfg)
    res = run_bass_kernel_spmd(nc, in_maps, core_ids=list(range(NCORES)))
    full = np.concatenate([np.asarray(r["xout"]) for r in res.results], axis=0)
    return np.ascontiguousarray(full.T)[None].astype(np.float32)


# version 44 (final)
# speedup vs baseline: 1.1266x; 1.0361x over previous
import contextlib
from collections import defaultdict

import numpy as np
import ml_dtypes
import concourse.bass as bass
import concourse.mybir as mybir
from concourse.bass_utils import run_bass_kernel_spmd

F32 = mybir.dt.float32
BF16 = mybir.dt.bfloat16
U8 = mybir.dt.uint8
AF = mybir.ActivationFunctionType
ALU = mybir.AluOpType
NPBF16 = ml_dtypes.bfloat16

NCORES = 8
ENGINES = ("sync", "pe", "act", "dve", "pool")


def make_cfg(T=8192, D=4096, NH=32, TC=512):
    c = {}
    c["T"], c["D"], c["NH"], c["TC"] = T, D, NH, TC
    c["DSL"] = D // NCORES
    c["HLOC"] = NH // NCORES
    c["NCH"] = T // TC
    c["NDT"] = c["DSL"] // 128
    c["NDNT"] = c["DSL"] * 16 // 128
    c["NST"] = 16
    c["DTR"] = 128
    c["DN"] = 128
    c["DV"] = 128
    c["DQ"] = 192
    c["KT"] = TC // 128
    c["G"] = 64
    c["SHIFT"] = 12.0
    return c


class Prog:

    def __init__(self):
        self.ops = []
        self.sem_idx = defaultdict(int)
        self.pending = {e: [] for e in ENGINES}
        self.last_ev = {e: None for e in ENGINES}

    def op(self, eng, emit, sem=None, waits=(), dma=False):
        sem = sem or eng
        idx = self.sem_idx[sem]
        self.sem_idx[sem] = idx + 1
        w = list(self.pending[eng]) + [x for x in waits if x is not None]
        self.pending[eng] = []
        import inspect as _insp
        fr = _insp.currentframe().f_back
        if fr.f_code.co_name in ("dma", "cc", "act", "dve", "pool", "mk_mm"):
            fr = fr.f_back
        self.ops.append((eng, emit, sem, w, 16 if dma else 1, dma, fr.f_lineno))
        self.last_ev[eng] = (sem, idx)
        return (sem, idx)

    def describe(self, sem, lo, hi):
        semc = 0
        for (eng, emit, s, w, amt, dma, ln) in self.ops:
            if s != sem:
                continue
            semc += 1
            if lo <= semc <= hi:
                print(f"sem {sem} #{semc}: eng={eng} line={ln}")

    def full_barrier(self, extra=()):
        evs = [self.last_ev[e] for e in ENGINES if self.last_ev[e] is not None]
        evs = evs + [x for x in extra if x is not None]
        for e in ENGINES:
            self.pending[e] = list(evs)

    def emit_into(self, nc, ctx):
        waited = defaultdict(set)
        per_sem = defaultdict(list)
        semidx = defaultdict(int)
        op_semidx = []
        for k, (eng, emit, sem, waits, amt, dma, _ln) in enumerate(self.ops):
            op_semidx.append(semidx[sem])
            semidx[sem] += 1
            per_sem[sem].append((k, amt, dma))
            for (s, i) in waits:
                waited[s].add(i)
        need_inc = {}
        val_of = {}
        for sem, lst in per_sem.items():
            wset = waited[sem]
            cum = 0
            for j, (k, amt, dma) in enumerate(lst):
                inc = dma or (j in wset)
                need_inc[k] = inc
                if inc:
                    cum += amt
                val_of[(sem, j)] = cum
        sems = {s: ctx.enter_context(nc.semaphore(f"S_{s}")) for s in per_sem}

        def emit_engine(eng, engobj):
            lastwait = {}
            for k, (e, emit, sem, waits, amt, dma, _ln) in enumerate(self.ops):
                if e != eng:
                    continue
                wmax = {}
                for (s, i) in waits:
                    v = val_of[(s, i)]
                    if v > wmax.get(s, -1):
                        wmax[s] = v
                for s, v in wmax.items():
                    if v == 0 or lastwait.get(s, -1) >= v:
                        continue
                    engobj.wait_ge(sems[s], v)
                    lastwait[s] = v
                instr = emit(engobj)
                if need_inc[k]:
                    if instr is None:
                        raise RuntimeError(f"op {k} on {eng}/{sem} must inc but emitted None")
                    instr.then_inc(sems[sem], amt)
        return emit_engine


def build_program(cfg):
    T, D, NH, TC = cfg["T"], cfg["D"], cfg["NH"], cfg["TC"]
    DSL, HLOC, NCH = cfg["DSL"], cfg["HLOC"], cfg["NCH"]
    NDT, NDNT, NST, DTR = cfg["NDT"], cfg["NDNT"], cfg["NST"], cfg["DTR"]
    DN, DV, KT, G = cfg["DN"], cfg["DV"], cfg["KT"], cfg["G"]
    XPR = DTR + 2 * NST
    NKT = D // 128
    SCALE = cfg["DQ"] ** -0.5
    SHIFT = cfg["SHIFT"]
    RG = [list(range(NCORES))]

    nc = bass.Bass("TRN2", target_bir_lowering=False)
    P = Prog()
    globals()['LAST_PROG'] = P
    ctx = contextlib.ExitStack()

    x0 = nc.dram_tensor("x0", [DSL, T], F32, kind="ExternalInput").ap()
    out = nc.dram_tensor("xout", [DSL, T], F32, kind="ExternalOutput").ap()
    wq_d = nc.dram_tensor("wq_sl", [D, HLOC * DN], F32, kind="ExternalInput").ap()
    wk_d = nc.dram_tensor("wk_sl", [D, HLOC * DN], F32, kind="ExternalInput").ap()
    wv_d = nc.dram_tensor("wv_sl", [D, HLOC * DV], F32, kind="ExternalInput").ap()
    wo_d = nc.dram_tensor("wo_sl", [HLOC * DV, D], F32, kind="ExternalInput").ap()
    xpw_d = nc.dram_tensor("xpw_sl", [DSL, XPR], F32, kind="ExternalInput").ap()
    dtw_d = nc.dram_tensor("dtw_sl", [DTR, DSL], F32, kind="ExternalInput").ap()
    dtb_d = nc.dram_tensor("dtb_sl", [128, NDT], F32, kind="ExternalInput").ap()
    alog_d = nc.dram_tensor("alog_sl", [128, NDNT], F32, kind="ExternalInput").ap()
    dvec_d = nc.dram_tensor("dvec_sl", [128, NDT], F32, kind="ExternalInput").ap()
    wqt_d = nc.dram_tensor("wqt_sl", [D, DSL], U8, kind="ExternalInput").ap()
    scl_d = nc.dram_tensor("scl_sl", [D, DSL], F32, kind="ExternalInput").ap()
    zro_d = nc.dram_tensor("zro_sl", [D, DSL], F32, kind="ExternalInput").ap()
    nibsel_d = nc.dram_tensor("nibsel", [128, 2], F32, kind="ExternalInput").ap()
    rk_d = nc.dram_tensor("rk_all", [128, 16 * 128], BF16, kind="ExternalInput").ap()
    sk_d = nc.dram_tensor("sk_all", [128, 16 * 128], BF16, kind="ExternalInput").ap()
    rb_d = nc.dram_tensor("rb_bc", [32, 2 * 128], BF16, kind="ExternalInput").ap()
    tri_d = nc.dram_tensor("trimask", [128, 128], F32, kind="ExternalInput").ap()
    if KT == 4:
        mask4_d = nc.dram_tensor("mask4", [128, KT * TC], F32, kind="ExternalInput").ap()
    iden_d = nc.dram_tensor("ident", [128, 128], F32, kind="ExternalInput").ap()

    xi = [nc.dram_tensor(f"x{i}_int", [DSL, T], F32).ap() for i in (1, 2, 3)]
    xstream = [x0, xi[0], xi[1], xi[2], out]
    wqb_d = nc.dram_tensor("wqb", [D, HLOC * DN], BF16).ap()
    wkb_d = nc.dram_tensor("wkb", [D, HLOC * DN], BF16).ap()
    wvb_d = nc.dram_tensor("wvb", [D, HLOC * DV], BF16).ap()
    wob_d = nc.dram_tensor("wob", [HLOC * DV, D], BF16).ap()
    wtb_d = nc.dram_tensor("wtb", [D, DSL], BF16).ap()
    stats_c = nc.dram_tensor("stats_c", [1, NCH * 2 * TC], F32).ap()
    stats_g = nc.dram_tensor("stats_g", [1, NCH * 2 * TC], F32).ap()
    xp_c = nc.dram_tensor("xp_c", [NCH * XPR, TC], F32).ap()
    xp_g = nc.dram_tensor("xp_g", [NCH * XPR, TC], F32).ap()
    xln_c = nc.dram_tensor("xln_c", [NCH * DSL, TC], BF16).ap()
    xg_d = nc.dram_tensor("xg", [NCH * D, TC], BF16).ap()
    k_dram = nc.dram_tensor("k_dram", [HLOC * DN, T], BF16).ap()
    q_dram = nc.dram_tensor("q_dram", [HLOC * DN, T], BF16).ap()
    v_dram = nc.dram_tensor("v_dram", [HLOC, NCH, 128, KT * DV], BF16).ap()
    wop_c = nc.dram_tensor("wop_c", [NCH * D, TC], BF16).ap()
    wop_g = nc.dram_tensor("wop_g", [NCH * DSL, TC], BF16).ap()
    at_dram = nc.dram_tensor("at_dram", [64, 128], F32).ap()

    psa = ctx.enter_context(nc.psum_tensor("PSA", [128, 2048], F32))
    psb = ctx.enter_context(nc.psum_tensor("PSB", [128, 1024], F32))
    ps = {f"PS{i}": psa[:, 512 * i:512 * (i + 1)] for i in range(4)}
    ps["PS6"] = psb[:, 0:512]
    ps["PS7"] = psb[:, 512:1024]
    for i in (4, 5):
        ps[f"PS{i}"] = ctx.enter_context(nc.psum_tensor(f"PS{i}", [128, 512], F32))[:, :]
    ps_writer = {k: None for k in ps}
    ps_readers = {k: [] for k in ps}

    def mk_mm(slot, out_ap, lhsT, rhs, start, stop, waits=()):
        w = list(waits)
        if start:
            w += ps_readers[slot] + ([ps_writer[slot]] if ps_writer[slot] else [])
        ev = P.op("pe", lambda e, o=out_ap, l=lhsT, r=rhs, s1=start, s2=stop:
                  e.matmul(out=o, lhsT=l, rhs=r, start=s1, stop=s2), waits=w)
        if start:
            ps_writer[slot] = ev
            ps_readers[slot] = []
        else:
            ps_writer[slot] = ev
        return ev

    def ps_read(slot, ev):
        ps_readers[slot].append(ev)
        return ev

    def ps_write_dve(slot, ev):
        ps_writer[slot] = ev
        return ev

    def dma(stream, outap, inap, waits=(), eng="sync"):
        return P.op(eng, lambda e, o=outap, i=inap: e.dma_start(out=o, in_=i),
                    sem=stream, waits=waits, dma=True)

    def cc(kind, op, ins, outs, waits=()):
        return P.op("pool", lambda e, k=kind, o=op, i=ins, u=outs:
                    e.collective_compute(k, o, replica_groups=RG, ins=i, outs=u),
                    sem="cc", waits=waits, dma=False)

    def act(emit, waits=()):
        return P.op("act", emit, waits=waits)

    def dve(emit, waits=()):
        return P.op("dve", emit, waits=waits)

    def pool(emit, waits=()):
        return P.op("pool", emit, waits=waits)

    WAR = defaultdict(list)
    RSB_EV = {}

    def sb(stack, name, shape, dt, side=None):
        if side:
            return stack.enter_context(nc.sbuf_tensor(name, shape, dt, side=side))
        return stack.enter_context(nc.sbuf_tensor(name, shape, dt))

    rk_sb = sb(ctx, "rk_sb", [128, 16 * 128], BF16)
    sk_sb = sb(ctx, "sk_sb", [128, 16 * 128], BF16)
    rb_sb = sb(ctx, "rb_sb", [32, 2 * 128], BF16)
    tri_sb = sb(ctx, "tri_sb", [128, 128], F32)
    iden_sb = sb(ctx, "iden_sb", [128, 128], F32)

    rka_sb = sb(ctx, "rka_sb", [128, 0], BF16) if False else None
    xpw_sb = sb(ctx, "xpw_sb", [128, NDT * XPR], BF16)
    dtw_sb = sb(ctx, "dtw_sb", [DTR, DSL], BF16)
    dtb_sb = sb(ctx, "dtb_sb", [128, NDT], F32)
    a_sb = sb(ctx, "a_sb", [128, NDNT], F32)
    dvec_sb = sb(ctx, "dvec_sb", [128, NDT], F32)
    nibsel_sb = sb(ctx, "nibsel_sb", [128, 2], F32)
    ones_bf = sb(ctx, "ones_bf", [128, 1], BF16)
    ones_f1 = sb(ctx, "ones_f1", [1, 128], F32)
    ones_fp = sb(ctx, "ones_fp", [128, 1], F32)
    eps_sb = sb(ctx, "eps_sb", [1, 1], F32)
    one_sb = sb(ctx, "one_sb", [128, 1], F32)
    nshift_sb = sb(ctx, "nshift_sb", [128, 1], F32)
    carry_sb = sb(ctx, "carry_sb", [128, NDNT], F32)
    rka_sb = sb(ctx, "rka_sb", [128, NDNT * 128], BF16)
    x_sb = [sb(ctx, f"x_sb{p}", [128, NDT * TC], F32) for p in range(2)]
    xsq_sb = sb(ctx, "xsq_sb", [128, NDT * TC], F32)
    xlb_sb = [sb(ctx, f"xlb_sb{p}", [128, NDT * TC], BF16) for p in range(2)]
    st1_sb = [sb(ctx, f"st1_sb{p}", [1, 2 * TC], F32) for p in range(2)]
    st2_sb = [sb(ctx, f"st2_sb{p}", [1, TC], F32) for p in range(2)]
    rsm_sb = [sb(ctx, f"rsm_sb{p}", [1, 2 * TC], F32) for p in range(2)]
    rsb_sb = [sb(ctx, f"rsb_sb{p}", [128, 2 * TC], F32) for p in range(2)]
    stc_sb = [sb(ctx, f"stc_sb{p}", [1, 2 * TC], F32) for p in range(2)]
    ph0 = contextlib.ExitStack()
    at_sb = sb(ph0, "at_sb", [64, 128], F32, side="right")
    atf_sb = sb(ph0, "atf_sb", [1, 16 * 128], F32, side="right")
    cast_sb = sb(ph0, "cast_sb", [128, 1024], F32, side="right")
    castb_sb = sb(ph0, "castb_sb", [128, 1024], BF16, side="right")

    def stats_prepass(xin):
        xc = [[], []]
        st_evs = []
        ar_evs = []
        for c in range(NCH):
            p = c % 2
            ev_x = load_x(xin, c, p, waits=xc[p])
            e_sq = dve(lambda v, p=p: v.tensor_tensor(out=xsq_sb[:], in0=x_sb[p][:],
                                                      in1=x_sb[p][:], op=ALU.mult),
                       waits=[ev_x] + WAR["xsq"])
            ev1 = ev2 = None
            for a in range(NDT):
                ev1 = mk_mm("PS6", ps["PS6"][0:1, 0:TC], ones_fp[:, 0:1],
                            x_sb[p][:, a * TC:(a + 1) * TC], start=(a == 0),
                            stop=(a == NDT - 1), waits=[ev_x, EV["memset"]])
                ev2 = mk_mm("PS7", ps["PS7"][0:1, 0:TC], ones_fp[:, 0:1],
                            xsq_sb[:, a * TC:(a + 1) * TC], start=(a == 0),
                            stop=(a == NDT - 1), waits=[e_sq])
            WAR["xsq"] = [ev2]
            e_e1 = act(lambda a_, p=p: a_.copy(out=stc_sb[p][:, 0:TC], in_=ps["PS6"][0:1, 0:TC]),
                       waits=[ev1] + WAR[f"stc{p}"])
            ps_read("PS6", e_e1)
            e_e2 = act(lambda a_, p=p: a_.copy(out=stc_sb[p][:, TC:2 * TC], in_=ps["PS7"][0:1, 0:TC]),
                       waits=[ev2])
            ps_read("PS7", e_e2)
            e_st = dma(f"st_stats{p}", stats_c[:, c * 2 * TC:(c + 1) * 2 * TC], stc_sb[p][:, :],
                       waits=[e_e1, e_e2], eng="pool")
            WAR[f"stc{p}"] = [e_st]
            st_evs.append(e_st)
            xc[p] = [ev1, ev2]
            Q = max(1, NCH // 4)
            if (c + 1) % Q == 0:
                qi = c // Q
                ar_evs.append(cc("AllReduce", ALU.add,
                                 [stats_c[:, qi * Q * 2 * TC:(qi + 1) * Q * 2 * TC].opt()],
                                 [stats_g[:, qi * Q * 2 * TC:(qi + 1) * Q * 2 * TC].opt()],
                                 waits=st_evs))
                st_evs = []
        return ar_evs, xc

    def ln_norm(c, p, ev_x, e_ar, want_f32, xlf_sb=None, neng=None):
        neng = neng or dve
        reg = slice(c * 2 * TC, (c + 1) * 2 * TC)
        e_ld = dma(f"ld_stats{p}", stc_sb[p][:], stats_g[:, reg],
                   waits=[e_ar[c // max(1, NCH // 4)]] + WAR[f"stc{p}"])
        e_m = dve(lambda v, p=p: v.tensor_scalar(out=st1_sb[p][:], in0=stc_sb[p][:],
                                                 scalar1=1.0 / D, scalar2=None, op0=ALU.mult),
                  waits=[e_ld] + WAR[f"st1{p}"])
        WAR[f"stc{p}"] = [e_st for e_st in ()] + [e_m]
        e_m2 = dve(lambda v, p=p: v.tensor_tensor(out=st2_sb[p][:], in0=st1_sb[p][:, 0:TC],
                                                  in1=st1_sb[p][:, 0:TC], op=ALU.mult),
                   waits=[e_m] + WAR[f"st2{p}"])
        e_var = dve(lambda v, p=p: v.tensor_tensor(out=st2_sb[p][:], in0=st1_sb[p][:, TC:2 * TC],
                                                   in1=st2_sb[p][:], op=ALU.subtract),
                    waits=[e_m2])
        e_ln = act(lambda a_, p=p: a_.activation(out=st2_sb[p][:], in_=st2_sb[p][:],
                                                 func=AF.Ln, bias=eps_sb[:, :]), waits=[e_var])
        e_rs = act(lambda a_, p=p: a_.activation(out=rsm_sb[p][:, 0:TC], in_=st2_sb[p][:],
                                                 func=AF.Exp, scale=-0.5),
                   waits=[e_ln] + WAR[f"rsm{p}"])
        WAR[f"st2{p}"] = [e_rs]
        e_mrs = dve(lambda v, p=p: v.tensor_tensor(out=rsm_sb[p][:, TC:2 * TC],
                                                   in0=st1_sb[p][:, 0:TC],
                                                   in1=rsm_sb[p][:, 0:TC], op=ALU.mult),
                    waits=[e_rs])
        WAR[f"st1{p}"] = [e_mrs]
        e_b1 = mk_mm("PS6", ps["PS6"][0:128, 0:TC], ones_f1[0:1, :], rsm_sb[p][0:1, 0:TC],
                     start=True, stop=True, waits=[e_rs])
        e_b2 = mk_mm("PS7", ps["PS7"][0:128, 0:TC], ones_f1[0:1, :], rsm_sb[p][0:1, TC:2 * TC],
                     start=True, stop=True, waits=[e_mrs])
        e_eb1 = act(lambda a_, p=p: a_.copy(out=rsb_sb[p][:, 0:TC], in_=ps["PS6"][0:128, 0:TC]),
                    waits=[e_b1] + WAR[f"rsb{p}"])
        ps_read("PS6", e_eb1)
        e_eb2 = act(lambda a_, p=p: a_.copy(out=rsb_sb[p][:, TC:2 * TC], in_=ps["PS7"][0:128, 0:TC]),
                    waits=[e_b2])
        ps_read("PS7", e_eb2)
        WAR[f"rsm{p}"] = [e_eb1, e_eb2]
        RSB_EV[p] = (e_eb1, e_eb2)
        x3 = x_sb[p][:, :].rearrange("q (a t) -> q a t", a=NDT)
        rs3 = rsb_sb[p][:, 0:TC].unsqueeze(1).broadcast_to([128, NDT, TC])
        mrs3 = rsb_sb[p][:, TC:2 * TC].unsqueeze(1).broadcast_to([128, NDT, TC])
        if want_f32:
            f3 = xlf_sb[:, :].rearrange("q (a t) -> q a t", a=NDT)
            e_n1 = dve(lambda v: v.tensor_tensor(out=f3, in0=x3, in1=rs3, op=ALU.mult),
                       waits=[e_eb1, ev_x] + WAR["xlf"])
            e_n2 = dve(lambda v: v.tensor_tensor(out=f3, in0=f3, in1=mrs3, op=ALU.subtract),
                       waits=[e_eb2, e_n1])
            e_n3 = dve(lambda v, p=p: v.tensor_copy(out=xlb_sb[p][:], in_=xlf_sb[:]),
                       waits=[e_n2] + WAR[f"xlb{p}"])
            WAR[f"rsb{p}"] = [e_n2]
            return e_n3, e_n2, [e_n1]
        else:
            s3 = xsq_sb[:, :].rearrange("q (a t) -> q a t", a=NDT)
            e_n1 = neng(lambda v, s3=s3, x3=x3, rs3=rs3: v.tensor_tensor(
                out=s3, in0=x3, in1=rs3, op=ALU.mult),
                waits=[e_eb1, ev_x] + WAR["xsq"])
            b3 = xlb_sb[p][:, :].rearrange("p (a t) -> p a t", t=TC)
            e_n2 = neng(lambda v, b3=b3, s3=s3, mrs3=mrs3: v.tensor_tensor(
                out=b3, in0=s3, in1=mrs3, op=ALU.subtract),
                waits=[e_eb2, e_n1] + WAR[f"xlb{p}"])
            WAR["xsq"] = [e_n2]
            WAR[f"rsb{p}"] = [e_n2]
            return e_n2, None, [e_n1]

    def load_x(xdram, c, p, waits=(), eng="sync"):
        return dma(f"ld_x{p}", x_sb[p][:, :].rearrange("p (a t) -> p a t", t=TC),
                   xdram[:, c * TC:(c + 1) * TC].rearrange("(a p) t -> p a t", p=128),
                   waits=list(waits), eng=eng)

    EV = {}

    e = dve(lambda v: v.memset(ones_bf[:], 1.0))
    e = dve(lambda v: v.memset(ones_f1[:], 1.0), waits=[e])
    e = dve(lambda v: v.memset(ones_fp[:], 1.0), waits=[e])
    e = dve(lambda v: v.memset(eps_sb[:], 1e-5), waits=[e])
    e = dve(lambda v: v.memset(one_sb[:], 1.0), waits=[e])
    e = dve(lambda v: v.memset(nshift_sb[:], -SHIFT), waits=[e])
    EV["memset"] = e

    dma("ld_c", rk_sb[:], rk_d[:, :])
    dma("ld_c", sk_sb[:], sk_d[:, :])
    dma("ld_c", rb_sb[:], rb_d[:, :])
    dma("ld_c", tri_sb[:], tri_d[:, :])
    dma("ld_c", nibsel_sb[:], nibsel_d[:, :])
    dma("ld_c", dtb_sb[:], dtb_d[:, :])
    dma("ld_c", dvec_sb[:], dvec_d[:, :])
    dma("ld_c", iden_sb[:], iden_d[:, :])
    e_al = dma("ld_c", a_sb[:], alog_d[:, :])
    for kk_ in ("rk", "sk", "rb", "tri", "sh", "dtb", "dv"):
        EV[kk_] = e_al

    e_a1 = act(lambda a_: a_.activation(out=a_sb[:], in_=a_sb[:], func=AF.Exp), waits=[e_al])
    EV["A"] = dve(lambda v: v.tensor_scalar(out=a_sb[:], in0=a_sb[:], scalar1=-1.0,
                                            scalar2=None, op0=ALU.mult), waits=[e_a1])
    e_tr = P.op("pe", lambda e: e.transpose(out=ps["PS6"][0:NDNT, 0:128],
                                            in_=a_sb[:, 0:NDNT], identity=iden_sb[:]),
                waits=[EV["A"], EV["tri"]] + [w for w in (ps_writer["PS6"],) if w]
                + ps_readers["PS6"])
    ps_writer["PS6"], ps_readers["PS6"] = e_tr, []
    e_at = act(lambda a_: a_.copy(out=at_sb[0:NDNT, :], in_=ps["PS6"][0:NDNT, 0:128]),
               waits=[e_tr])
    ps_read("PS6", e_at)
    e_atd = dma("ld_at", at_dram[0:NDNT, :], at_sb[0:NDNT, :], waits=[e_at], eng="pool")
    e_rka_prev = None
    e_atf = None
    for k in range(NDNT):
        if k % 16 == 0:
            kw = min(16, NDNT - k)
            e_atf = dma("ld_atf", atf_sb[0:1, 0:kw * 128],
                        at_dram[k:k + kw, :].rearrange("a b -> (a b)").unsqueeze(0),
                        waits=[e_atd] + ([e_rka_prev] if e_rka_prev else []))
        e_bc0 = mk_mm("PS7", ps["PS7"][0:128, 0:128], ones_f1[0:1, :],
                      atf_sb[0:1, (k % 16) * 128:((k % 16) + 1) * 128],
                      start=True, stop=True, waits=[e_atf])
        e_rka = dve(lambda v, k=k: v.tensor_tensor(
            out=rka_sb[:, k * 128:(k + 1) * 128],
            in0=rk_sb[:, (k % 16) * 128:((k % 16) + 1) * 128],
            in1=ps["PS7"][0:128, 0:128], op=ALU.mult),
            waits=[e_bc0, EV["rk"]] + ([e_rka_prev] if e_rka_prev else []))
        ps_read("PS7", e_rka)
        e_rka_prev = e_rka
    EV["rka"] = e_rka_prev

    e_l = dma("ld_w0", cast_sb[0:DTR, 0:DSL], dtw_d[:, :])
    EV["dtw"] = dve(lambda v: v.tensor_copy(out=dtw_sb[:], in_=cast_sb[0:DTR, 0:DSL]),
                    waits=[e_l])
    e_l = dma("ld_w0", cast_sb[0:128, 0:NDT * XPR].rearrange("p (a r) -> p a r", r=XPR),
              xpw_d[:, :].rearrange("(a p) r -> p a r", p=128), waits=[EV["dtw"]])
    EV["xpw"] = dve(lambda v: v.tensor_copy(out=xpw_sb[:], in_=cast_sb[0:128, 0:NDT * XPR]),
                    waits=[e_l])

    cast2_sb = sb(ph0, "cast2_sb", [128, 1024], F32, side="right")
    castb2_sb = sb(ph0, "castb2_sb", [128, 1024], BF16, side="right")
    cbufs = [(cast_sb, castb_sb), (cast2_sb, castb2_sb)]
    cst = {"i": 0, "c": [EV["xpw"], None], "s": [None, None], "last": None}
    cast_blocks = []
    for (s_, d_, r_, c_) in ((wq_d, wqb_d, D, HLOC * DN), (wk_d, wkb_d, D, HLOC * DN),
                             (wv_d, wvb_d, D, HLOC * DV), (wo_d, wob_d, HLOC * DV, D)):
        for r0 in range(0, r_, 128):
            for c0 in range(0, c_, 1024):
                cast_blocks.append((s_, d_, r0, c0, min(1024, c_ - c0)))

    def emit_cast_block(k):
        src, dst, r0, c0, cw = cast_blocks[k]
        b = cst["i"] % 2
        cs_, cb_ = cbufs[b]
        evl = dma("ld_wc", cs_[0:128, 0:cw], src[r0:r0 + 128, c0:c0 + cw],
                  waits=[cst["c"][b]] if cst["c"][b] else [])
        evc = act(lambda a_, cw=cw, cs_=cs_, cb_=cb_: a_.copy(
            out=cb_[0:128, 0:cw], in_=cs_[0:128, 0:cw]),
            waits=[evl] + ([cst["s"][b]] if cst["s"][b] else []))
        evs_ = dma("st_wc", dst[r0:r0 + 128, c0:c0 + cw], cb_[0:128, 0:cw],
                   waits=[evc], eng="act")
        cst["c"][b], cst["s"][b] = evc, evs_
        cst["i"] += 1
        cst["last"] = evs_

    dqsets = [dict(
        dq8=sb(ph0, "dq8_0", [128, DSL], U8, side="right"),
        dqn=sb(ph0, "dqn_0", [128, DSL], U8, side="right"),
        dqn2=sb(ph0, "dqn2_0", [128, DSL], U8, side="right"),
        dqz=sb(ph0, "dqz_0", [128, DSL], F32, side="right"),
        dqsc=sb(ph0, "dqsc_0", [128, DSL], F32, side="right"),
        dqt=sb(ph0, "dqt_0", [128, DSL], F32, side="right"),
        dqb=sb(ph0, "dqb_0", [128, DSL], BF16, side="right"),
        prev=None)]

    def emit_dq_tile(a, lane, engf):
        S = dqsets[0]
        w0 = [S["prev"]] if S["prev"] else []
        e1 = dma("ld_dq", S["dq8"][:], wqt_d[a * 128:(a + 1) * 128, :], waits=w0)
        e2 = dma("ld_dq2", S["dqz"][:], zro_d[a * 128:(a + 1) * 128, :], waits=w0)
        e3 = dma("ld_dq3", S["dqsc"][:], scl_d[a * 128:(a + 1) * 128, :], waits=w0)
        e4 = engf(lambda v, S=S: v.tensor_scalar(out=S["dqn"][:], in0=S["dq8"][:],
                                                 scalar1=4, scalar2=15,
                                                 op0=ALU.logical_shift_right,
                                                 op1=ALU.bitwise_and),
                  waits=[e1] + w0)
        e4b = engf(lambda v, S=S: v.tensor_scalar(out=S["dqn2"][:], in0=S["dq8"][:],
                                                  scalar1=15, scalar2=None,
                                                  op0=ALU.bitwise_and),
                   waits=[e1] + w0)
        e4c = engf(lambda v, S=S: v.tensor_scalar(out=S["dqt"][:], in0=S["dqn2"][:],
                                                  scalar1=nibsel_sb[:, 1:2], scalar2=None,
                                                  op0=ALU.mult),
                   waits=[e4b, EV["sh"]] + w0)
        e4d = engf(lambda v, S=S: v.scalar_tensor_tensor(out=S["dqt"][:], in0=S["dqn"][:],
                                                         scalar=nibsel_sb[:, 0:1],
                                                         in1=S["dqt"][:], op0=ALU.mult,
                                                         op1=ALU.add),
                   waits=[e4, e4c])
        e5 = engf(lambda v, S=S: v.tensor_tensor(out=S["dqt"][:], in0=S["dqt"][:],
                                                 in1=S["dqz"][:],
                                                 op=ALU.subtract), waits=[e4d, e2])
        e6 = engf(lambda v, S=S: v.tensor_tensor(out=S["dqb"][:], in0=S["dqt"][:],
                                                 in1=S["dqsc"][:],
                                                 op=ALU.mult), waits=[e5, e3])
        e7 = dma("st_dq", wtb_d[a * 128:(a + 1) * 128, :], S["dqb"][:], waits=[e6],
                 eng="act" if lane == 0 else "pool")
        S["prev"] = e7

    with contextlib.ExitStack() as ms:
        xlf_sb = sb(ms, "xlf_sb", [128, NDT * TC], F32, side="right")
        xp1_sb = [sb(ms, f"xp1_{p}", [DTR, TC], F32, side="right") for p in range(2)]
        xp2_sb = [sb(ms, f"xp2_{p}", [32, TC], F32, side="right") for p in range(2)]
        xpg1_sb = [sb(ms, f"xpg1_{p}", [DTR, TC], BF16, side="right") for p in range(2)]
        xpbc_sb = [sb(ms, f"xpbc_{p}", [32, TC], BF16, side="right") for p in range(2)]
        dt_sb = sb(ms, "dt_sb", [128, NDT * TC], BF16, side="right")
        dtx_sb = sb(ms, "dtx_sb", [128, NDT * TC], BF16, side="right")
        sp_sb2 = [sb(ms, f"sp_sb{i}", [128, TC], F32, side="right") for i in range(2)]
        brep_sb = [sb(ms, f"brep_{p}", [128, TC], F32, side="right") for p in range(2)]
        crep_sb = [sb(ms, f"crep_{p}", [128, TC], F32, side="right") for p in range(2)]
        da_sb = [sb(ms, f"da_{q}", [128, 2 * TC], F32, side="right") for q in range(2)]
        dbx_sb = [sb(ms, f"dbx_{q}", [128, 2 * TC], F32, side="right") for q in range(2)]
        h_sb = [sb(ms, f"h_{q}", [128, 2 * TC], F32, side="right") for q in range(2)]
        hc_sb = [sb(ms, f"hc_{q}", [128, 2 * TC], BF16, side="right") for q in range(4)]
        mam_sb = sb(ms, "mam_sb", [128, TC], F32, side="right")

        e_ar0, xcons = stats_prepass(xstream[0])
        m_stores = []
        ar_xp = [None] * NCH
        evx = [None] * NCH

        def xp_stage(cs):
            pc_ = cs % 2
            ev_x = load_x(xstream[0], cs, pc_, waits=xcons[pc_])
            xcons[pc_] = []
            evx[cs] = ev_x
            ev_xlnb, _, xreaders = ln_norm(cs, pc_, ev_x, e_ar0, want_f32=False)
            e_xp1 = e_xp2 = None
            for a in range(NDT):
                e_xp1 = mk_mm("PS5", ps["PS5"][0:DTR, 0:TC],
                              xpw_sb[:, a * XPR:a * XPR + DTR],
                              xlb_sb[pc_][:, a * TC:(a + 1) * TC], start=(a == 0),
                              stop=(a == NDT - 1), waits=[ev_xlnb, EV["xpw"]])
                e_xp2 = mk_mm("PS7", ps["PS7"][0:32, 0:TC],
                              xpw_sb[:, a * XPR + DTR:(a + 1) * XPR],
                              xlb_sb[pc_][:, a * TC:(a + 1) * TC], start=(a == 0),
                              stop=(a == NDT - 1), waits=[])
            e_v1 = act(lambda a_, pc_=pc_: a_.copy(out=xp1_sb[pc_][:], in_=ps["PS5"][0:DTR, 0:TC]),
                       waits=[e_xp1] + WAR[f"xp1{pc_}"])
            ps_read("PS5", e_v1)
            e_v2 = act(lambda a_, pc_=pc_: a_.copy(out=xp2_sb[pc_][:], in_=ps["PS7"][0:32, 0:TC]),
                       waits=[e_xp2] + WAR[f"xp2{pc_}"])
            ps_read("PS7", e_v2)
            xpb_ = cs * XPR
            e_s1 = dma("st_xp", xp_c[xpb_:xpb_ + DTR, :], xp1_sb[pc_][:], waits=[e_v1], eng="act")
            e_s2 = dma("st_xp", xp_c[xpb_ + DTR:xpb_ + XPR, :], xp2_sb[pc_][:], waits=[e_v2], eng="act")
            ar_xp[cs] = cc("AllReduce", ALU.add, [xp_c[xpb_:xpb_ + XPR, :].opt()],
                           [xp_g[xpb_:xpb_ + XPR, :].opt()], waits=[e_s1, e_s2])
            WAR[f"xp1{pc_}"] = [e_s1]
            WAR[f"xp2{pc_}"] = [e_s2]
            WAR[f"xlb{pc_}"] = [e_xp1, e_xp2]
            xcons[pc_] = xreaders

        xp_stage(0)
        CBN = len(cast_blocks)
        for c in range(NCH):
            p = c % 2
            creg = slice(c * TC, (c + 1) * TC)
            xpb = c * XPR
            e_l1 = dma(f"ld_xpg{p}", xp1_sb[p][:], xp_g[xpb:xpb + DTR, :],
                       waits=[ar_xp[c]] + WAR[f"xp1{p}"])
            e_l2 = dma(f"ld_xpg2{p}", xp2_sb[p][:], xp_g[xpb + DTR:xpb + XPR, :],
                       waits=[ar_xp[c]] + WAR[f"xp2{p}"])
            WAR[f"xp1{p}"] = [e_l1]
            WAR[f"xp2{p}"] = [e_l2]
            if c + 1 < NCH:
                xp_stage(c + 1)
            ev_x = evx[c]
            e_c1 = act(lambda a_, p=p: a_.copy(out=xpg1_sb[p][:], in_=xp1_sb[p][:]),
                       waits=[e_l1] + WAR[f"xpg1{p}"])
            e_c2 = act(lambda a_, p=p: a_.copy(out=xpbc_sb[p][:], in_=xp2_sb[p][:]),
                       waits=[e_l2] + WAR[f"xpbc{p}"])
            WAR[f"xp1{p}"] += [e_c1]
            WAR[f"xp2{p}"] += [e_c2]
            e_bb = mk_mm("PS7", ps["PS7"][0:128, 0:TC], rb_sb[:, 0:128], xpbc_sb[p][:],
                         start=True, stop=True, waits=[e_c2, EV["rb"]])
            e_bc2 = mk_mm("PS5", ps["PS5"][0:128, 0:TC], rb_sb[:, 128:256], xpbc_sb[p][:],
                          start=True, stop=True, waits=[e_c2])
            e_brep = act(lambda a_, p=p: a_.copy(out=brep_sb[p][:], in_=ps["PS7"][0:128, 0:TC]),
                         waits=[e_bb] + WAR[f"brep{p}"])
            ps_read("PS7", e_brep)
            e_crep = act(lambda a_, p=p: a_.copy(out=crep_sb[p][:], in_=ps["PS5"][0:128, 0:TC]),
                         waits=[e_bc2] + WAR[f"crep{p}"])
            ps_read("PS5", e_crep)
            WAR[f"xpbc{p}"] = [e_bb, e_bc2]
            dt_evs = []
            for a in range(NDT):
                e_mm = mk_mm("PS7", ps["PS7"][0:128, 0:TC], dtw_sb[:, a * 128:(a + 1) * 128],
                             xpg1_sb[p][:], start=True, stop=True,
                             waits=[e_c1, EV["dtw"]])
                sp_sb = sp_sb2[a % 2]
                e_p1 = act(lambda a_, a2=a, sp_sb=sp_sb: a_.activation(
                    out=sp_sb[:], in_=ps["PS7"][0:128, 0:TC],
                    func=AF.Abs, bias=dtb_sb[:, a2:a2 + 1]),
                    waits=[e_mm, EV["dtb"]] + WAR[f"sp{a % 2}"])
                e_p2 = act(lambda a_, sp_sb=sp_sb: a_.activation(out=sp_sb[:], in_=sp_sb[:],
                                                                 func=AF.Exp, scale=-1.0), waits=[e_p1])
                e_p3 = act(lambda a_, sp_sb=sp_sb: a_.activation(out=sp_sb[:], in_=sp_sb[:],
                                                                 func=AF.Ln, bias=one_sb[:, :]),
                           waits=[e_p2])
                e_p4 = act(lambda a_, a2=a: a_.activation(out=dt_sb[:, a2 * TC:(a2 + 1) * TC],
                                                          in_=ps["PS7"][0:128, 0:TC],
                                                          func=AF.Relu, bias=dtb_sb[:, a2:a2 + 1]),
                           waits=[e_p3] + WAR["dt"])
                ps_read("PS7", e_p4)
                e_p5 = pool(lambda v, a2=a, sp_sb=sp_sb: v.tensor_tensor(
                    out=dt_sb[:, a2 * TC:(a2 + 1) * TC],
                    in0=dt_sb[:, a2 * TC:(a2 + 1) * TC],
                    in1=sp_sb[:], op=ALU.add),
                    waits=[e_p4])
                WAR[f"sp{a % 2}"] = [e_p5]
                dt_evs.append(e_p5)
            WAR["dt"] = []
            e_dtbf = dt_evs[-1]
            rs3 = rsb_sb[p][:, 0:TC].unsqueeze(1).broadcast_to([128, NDT, TC])
            mrs3 = rsb_sb[p][:, TC:2 * TC].unsqueeze(1).broadcast_to([128, NDT, TC])
            x3 = x_sb[p][:, :].rearrange("q (a t) -> q a t", a=NDT)
            f3 = xlf_sb[:, :].rearrange("q (a t) -> q a t", a=NDT)
            e_f1 = pool(lambda v, f3=f3, x3=x3, rs3=rs3: v.tensor_tensor(
                out=f3, in0=x3, in1=rs3, op=ALU.mult),
                waits=[RSB_EV[p][0], ev_x] + WAR["xlf"])
            e_f2 = pool(lambda v, f3=f3, mrs3=mrs3: v.tensor_tensor(
                out=f3, in0=f3, in1=mrs3, op=ALU.subtract),
                waits=[RSB_EV[p][1], e_f1])
            WAR[f"rsb{p}"] += [e_f2]
            e_dtx = pool(lambda v: v.tensor_tensor(out=dtx_sb[:], in0=dt_sb[:],
                                                   in1=xlf_sb[:], op=ALU.mult),
                         waits=dt_evs + [e_f2] + WAR["dtx"])
            WAR["dt"] = [e_dtx] + WAR["dtbf"]
            y_done = []
            pend_y = []
            pend_cr = []

            def flush_y(pend, upto):
                evs = []
                while len(pend) > upto:
                    (e_hc2, q2, k02, k12, jj2) = pend.pop(0)
                    e_ya = mk_mm("PS4", ps["PS4"][0:128, 0:TC],
                                 sk_sb[:, (k02 % 16) * 128:((k02 % 16) + 1) * 128],
                                 hc_sb[q2][:, 0:TC], start=(jj2 == 0), stop=False,
                                 waits=[e_hc2, EV["sk"]])
                    e_yb = mk_mm("PS4", ps["PS4"][0:128, 0:TC],
                                 sk_sb[:, (k12 % 16) * 128:((k12 % 16) + 1) * 128],
                                 hc_sb[q2][:, TC:2 * TC], start=False, stop=(jj2 == 7),
                                 waits=[e_hc2])
                    WAR[f"hc{q2}"] = [e_yb]
                    evs.append(e_yb)
                return evs

            for a in range(NDT):
                e_y = None
                for jj in range(8):
                    k0 = a * 16 + 2 * jj
                    k1 = k0 + 1
                    q = jj % 2
                    qh = jj % 4
                    dA_, dB_ = ("PS0", "PS1") if jj % 2 == 0 else ("PS6", "PS7")
                    dsrc = psa[0:128, 0:1024] if jj % 2 == 0 else psb[0:128, 0:1024]
                    e_tp0 = mk_mm(dA_, ps[dA_][0:128, 0:TC],
                                  rka_sb[:, k0 * 128:(k0 + 1) * 128],
                                  dt_sb[:, a * TC:(a + 1) * TC], start=True, stop=True,
                                  waits=[e_dtbf, EV["rka"]])
                    e_tp1 = mk_mm(dB_, ps[dB_][0:128, 0:TC],
                                  rka_sb[:, k1 * 128:(k1 + 1) * 128],
                                  dt_sb[:, a * TC:(a + 1) * TC], start=True, stop=True,
                                  waits=[e_dtbf])
                    e_tq0 = mk_mm("PS2", ps["PS2"][0:128, 0:TC],
                                  rk_sb[:, (k0 % 16) * 128:((k0 % 16) + 1) * 128],
                                  dtx_sb[:, a * TC:(a + 1) * TC], start=True, stop=True,
                                  waits=[e_dtx, EV["rk"]])
                    e_tq1 = mk_mm("PS3", ps["PS3"][0:128, 0:TC],
                                  rk_sb[:, (k1 % 16) * 128:((k1 % 16) + 1) * 128],
                                  dtx_sb[:, a * TC:(a + 1) * TC], start=True, stop=True,
                                  waits=[e_dtx])
                    e_da0 = act(lambda a_, q=q, dsrc=dsrc: a_.activation(
                        out=da_sb[q][:, 0:TC],
                        in_=dsrc.rearrange("p (b c) -> p b c", c=512)[:, 0, 0:TC],
                        func=AF.Exp), waits=[e_tp0] + WAR[f"da{q}"])
                    e_da = act(lambda a_, q=q, dsrc=dsrc: a_.activation(
                        out=da_sb[q][:, TC:2 * TC],
                        in_=dsrc.rearrange("p (b c) -> p b c", c=512)[:, 1, 0:TC],
                        func=AF.Exp), waits=[e_tp1])
                    ps_read(dA_, e_da0)
                    ps_read(dB_, e_da)
                    e_dbx0 = dve(lambda v, q=q, p=p: v.tensor_tensor(
                        out=dbx_sb[q][:, 0:TC],
                        in0=ps["PS2"][0:128, 0:TC],
                        in1=brep_sb[p][:, :],
                        op=ALU.mult), waits=[e_tq0, e_brep] + WAR[f"dbx{q}"])
                    e_dbx = dve(lambda v, q=q, p=p: v.tensor_tensor(
                        out=dbx_sb[q][:, TC:2 * TC],
                        in0=ps["PS3"][0:128, 0:TC],
                        in1=brep_sb[p][:, :],
                        op=ALU.mult), waits=[e_tq1])
                    ps_read("PS2", e_dbx0)
                    ps_read("PS3", e_dbx)
                    init0 = 0.0 if c == 0 else carry_sb[:, k0:k0 + 1]
                    init1 = 0.0 if c == 0 else carry_sb[:, k1:k1 + 1]
                    cw0 = WAR.get(f"carry{k0}", []) if c > 0 else []
                    cw1 = WAR.get(f"carry{k1}", []) if c > 0 else []
                    e_h0 = dve(lambda v, q=q, i=init0: v.tensor_tensor_scan(
                        out=h_sb[q][:, 0:TC], data0=da_sb[q][:, 0:TC], data1=dbx_sb[q][:, 0:TC],
                        initial=i, op0=ALU.mult, op1=ALU.add),
                        waits=[e_da0, e_dbx0] + WAR[f"h{q}"] + cw0)
                    e_h1 = dve(lambda v, q=q, i=init1: v.tensor_tensor_scan(
                        out=h_sb[q][:, TC:2 * TC], data0=da_sb[q][:, TC:2 * TC],
                        data1=dbx_sb[q][:, TC:2 * TC],
                        initial=i, op0=ALU.mult, op1=ALU.add),
                        waits=[e_da, e_dbx] + cw1)
                    pend_cr.append((q, k0, e_h0, e_h1))
                    if len(pend_cr) > 1:
                        (q2, k02, x_h0, x_h1) = pend_cr.pop(0)
                        e_cr = act(lambda a_, q=q2, k=k02: a_.copy(
                            out=carry_sb[:, k:k + 2].unsqueeze(2),
                            in_=h_sb[q][:, :].rearrange("p (b t) -> p b t", t=TC)[:, :, TC - 1:TC]),
                            waits=[x_h0, x_h1])
                        WAR[f"carry{k02}"] = [e_cr]
                        WAR[f"carry{k02 + 1}"] = [e_cr]
                        WAR[f"hcr{q2}"] = [e_cr]
                    e_hc = pool(lambda g, q=q, qh=qh, p=p: g.tensor_tensor(
                        out=hc_sb[qh][:, :].rearrange("p (b c) -> p b c", c=TC),
                        in0=h_sb[q][:, :].rearrange("p (b c) -> p b c", c=TC),
                        in1=crep_sb[p][:, :].unsqueeze(1).broadcast_to([128, 2, TC]),
                        op=ALU.mult), waits=[e_h0, e_h1, e_crep] + WAR[f"hc{qh}"])
                    pend_y.append((e_hc, qh, k0, k1, jj))
                    ys = flush_y(pend_y, 2)
                    if ys:
                        e_y = ys[-1]
                    WAR[f"da{q}"] = [e_h0, e_h1]
                    WAR[f"dbx{q}"] = [e_h0, e_h1]
                    WAR[f"h{q}"] = [e_hc] + WAR[f"hcr{q}"]
                    e_tp_last, e_tq_last = e_tp1, e_tq1
                    e_dbx_last, e_hc_last = e_dbx, e_hc
                ys = flush_y(pend_y, 0)
                if ys:
                    e_y = ys[-1]
                e_ev = dve(lambda v, a=a, p=p: v.scalar_tensor_tensor(
                    out=mam_sb[:], in0=xlf_sb[:, a * TC:(a + 1) * TC],
                    scalar=dvec_sb[:, a:a + 1], in1=ps["PS4"][0:128, 0:TC],
                    op0=ALU.mult, op1=ALU.add),
                    waits=[e_y, e_f2, EV["dv"]] + WAR["mam"])
                ps_read("PS4", e_ev)
                e_add = dve(lambda v, a=a, p=p: v.tensor_tensor(
                    out=x_sb[p][:, a * TC:(a + 1) * TC], in0=x_sb[p][:, a * TC:(a + 1) * TC],
                    in1=mam_sb[:], op=ALU.add), waits=[e_ev])
                WAR["mam"] = [e_add]
                y_done.append(e_add)
            while pend_cr:
                (q2, k02, x_h0, x_h1) = pend_cr.pop(0)
                e_cr = act(lambda a_, q=q2, k=k02: a_.copy(
                    out=carry_sb[:, k:k + 2].unsqueeze(2),
                    in_=h_sb[q][:, :].rearrange("p (b t) -> p b t", t=TC)[:, :, TC - 1:TC]),
                    waits=[x_h0, x_h1])
                WAR[f"carry{k02}"] = [e_cr]
                WAR[f"carry{k02 + 1}"] = [e_cr]
                WAR[f"hcr{q2}"] = [e_cr]
            e_stx = dma(f"st_x{p}", xstream[1][:, creg].rearrange("(a p) t -> p a t", p=128),
                        x_sb[p][:, :].rearrange("p (a t) -> p a t", t=TC), waits=y_done, eng="act")
            m_stores.append(e_stx)
            xcons[p] += y_done + [e_stx]
            WAR["xlf"] = [e_dtx] + y_done
            WAR["dtbf"] = [e_tp_last]
            WAR["dt"] = WAR["dt"] + [e_tp_last]
            WAR["dtx"] = [e_tq_last]
            WAR[f"brep{p}"] = [e_dbx_last]
            WAR[f"crep{p}"] = [e_hc_last]
            WAR[f"xpg1{p}"] = [e_mm]
            for k_ in range(c * CBN // NCH, (c + 1) * CBN // NCH):
                emit_cast_block(k_)
            emit_dq_tile(2 * c, 0, dve)
            emit_dq_tile(2 * c + 1, 0, dve)
        P.full_barrier(m_stores + [cst["last"], dqsets[0]["prev"]])
    ph0.close()

    AGB = 2 if NCH % 2 == 0 else 1

    def ln_ag_pass(xin, xcons_unused):
        e_ar, xcons = stats_prepass(xin)
        ag_evs = []
        pend_st = []
        for c in range(NCH):
            p = c % 2
            ev_x = load_x(xin, c, p, waits=xcons[p])
            ev_xlnb, _, xreaders = ln_norm(c, p, ev_x, e_ar, want_f32=False)
            e_st = dma(f"st_xln{c % 2}", xln_c[c * DSL:(c + 1) * DSL, :].rearrange("(a p) t -> p a t", p=128),
                       xlb_sb[p][:, :].rearrange("p (a t) -> p a t", t=TC),
                       waits=[ev_xlnb], eng="pool")
            pend_st.append(e_st)
            if (c + 1) % AGB == 0:
                g = c // AGB
                e_ag = cc("AllGather", ALU.bypass,
                          [xln_c[g * AGB * DSL:(g + 1) * AGB * DSL, :].opt()],
                          [xg_d[g * AGB * D:(g + 1) * AGB * D, :].opt()], waits=pend_st)
                pend_st = []
                ag_evs += [e_ag] * AGB
            WAR[f"xlb{p}"] = [e_st]
            xcons[p] = [ev_xlnb] + xreaders
        return ag_evs, xcons

    def ffn_phase(xin, xout, tag):
        with contextlib.ExitStack() as fs:
            xg_sb = [sb(fs, f"xg_{tag}{p}", [128, NKT * TC], BF16, side="right") for p in range(2)]
            wt_sb = sb(fs, f"wt_{tag}", [128, NKT * DSL], BF16, side="right")
            e_wt = dma("ld_wt" + tag, wt_sb[:, :].rearrange("p (a o) -> p a o", o=DSL),
                       wtb_d[:, :].rearrange("(a p) o -> p a o", p=128))
            ag_evs, xcons = ln_ag_pass(xin, [[], []])
            stores = []
            for c in range(NCH):
                p = c % 2
                ev_x = load_x(xin, c, p, waits=xcons[p])
                xcons[p] = []
                ldg_evs_ = []
                g, cc_ = c // AGB, c % AGB
                for r_ in range(NCORES):
                    base_ = g * AGB * D + r_ * AGB * DSL + cc_ * DSL
                    e_ldg = dma(f"ld_xg{p}_{r_}",
                                xg_sb[p][:, r_ * NDT * TC:(r_ + 1) * NDT * TC].rearrange(
                                    "p (q t) -> p q t", t=TC),
                                xg_d[base_:base_ + DSL, :].rearrange("(q p) t -> p q t", p=128),
                                waits=[ag_evs[c]] + WAR[f"xg{p}{r_}"])
                    WAR[f"xg{p}{r_}"] = []
                    ldg_evs_.append((r_, e_ldg))
                ldg_map = dict(ldg_evs_)
                res_evs = []
                for o in range(NDT):
                    slot = f"PS{o % 4}"
                    e_mm = None
                    for a in range(NKT):
                        e_mm = mk_mm(slot, ps[slot][0:128, 0:TC],
                                     wt_sb[:, a * DSL + o * 128:a * DSL + (o + 1) * 128],
                                     xg_sb[p][:, a * TC:(a + 1) * TC],
                                     start=(a == 0), stop=(a == NKT - 1),
                                     waits=[ldg_map[a // NDT], e_wt])
                    e_res = dve(lambda v, o=o, p=p, slot=slot: v.tensor_tensor(
                        out=x_sb[p][:, o * TC:(o + 1) * TC],
                        in0=x_sb[p][:, o * TC:(o + 1) * TC],
                        in1=ps[slot][0:128, 0:TC], op=ALU.add), waits=[e_mm, ev_x])
                    ps_read(slot, e_res)
                    res_evs.append(e_res)
                creg = slice(c * TC, (c + 1) * TC)
                e_stx = dma(f"st_x{p}", xout[:, creg].rearrange("(a p) t -> p a t", p=128),
                            x_sb[p][:, :].rearrange("p (a t) -> p a t", t=TC),
                            waits=res_evs, eng="pool")
                stores.append(e_stx)
                xcons[p] = res_evs + [e_stx]
                for r_ in range(NCORES):
                    WAR[f"xg{p}{r_}"] = res_evs
            P.full_barrier(stores)

    ffn_phase(xstream[1], xstream[2], "f1")

    with contextlib.ExitStack() as js:
        xg_sb = [sb(js, f"xg_pj{p}", [128, NKT * TC], BF16, side="right") for p in range(2)]
        sl1_sb = [sb(js, f"sl1_{i}", [128, 2 * HLOC * DN], BF16, side="right") for i in range(4)]
        sl2_sb = [sb(js, f"sl2_{i}", [128, HLOC * DV], BF16, side="right") for i in range(4)]
        prj_sb = [sb(js, f"prj_{i}", [128, TC], BF16, side="right") for i in range(2)]
        vprj_sb = [sb(js, f"vprj_{i}", [128, HLOC * DV], BF16, side="right") for i in range(2)]

        ag_evs, xcons = ln_ag_pass(xstream[2], [[], []])
        prj_stores = []
        for c in range(NCH):
            p = c % 2
            creg = slice(c * TC, (c + 1) * TC)
            ldg_evs_ = []
            g, cc_ = c // AGB, c % AGB
            for r_ in range(NCORES):
                base_ = g * AGB * D + r_ * AGB * DSL + cc_ * DSL
                e_ldg = dma(f"ld_xg{p}_{r_}",
                            xg_sb[p][:, r_ * NDT * TC:(r_ + 1) * NDT * TC].rearrange(
                                "p (q t) -> p q t", t=TC),
                            xg_d[base_:base_ + DSL, :].rearrange("(q p) t -> p q t", p=128),
                            waits=[ag_evs[c]] + WAR[f"xg{p}{r_}"])
                WAR[f"xg{p}{r_}"] = []
                ldg_evs_.append((r_, e_ldg))
            ldg_map = dict(ldg_evs_)
            cons = []
            e_mm = None
            for a in range(NKT):
                pp = a % 4
                e_s1 = dma(f"ld_sl1{pp}", sl1_sb[pp][:, 0:HLOC * DN],
                           wkb_d[a * 128:(a + 1) * 128, :], waits=WAR[f"sl1{pp}"])
                e_s2 = dma(f"ld_sl1b{pp}", sl1_sb[pp][:, HLOC * DN:2 * HLOC * DN],
                           wqb_d[a * 128:(a + 1) * 128, :], waits=WAR[f"sl1{pp}"])
                WAR[f"sl1{pp}"] = []
                for ki in range(2):
                    for hh in range(HLOC):
                        slot = f"PS{ki * 4 + hh}"
                        e_mm = mk_mm(slot, ps[slot][0:128, 0:TC],
                                     sl1_sb[pp][:, (ki * HLOC + hh) * DN:(ki * HLOC + hh + 1) * DN],
                                     xg_sb[p][:, a * TC:(a + 1) * TC],
                                     start=(a == 0), stop=(a == NKT - 1),
                                     waits=[e_s1, e_s2, ldg_map[a // NDT]])
                        WAR[f"sl1{pp}"].append(e_mm)
            for ki, odram in ((0, k_dram), (1, q_dram)):
                for hh in range(HLOC):
                    slot = f"PS{ki * 4 + hh}"
                    bi = (ki * HLOC + hh) % 2
                    e_evac = act(lambda a_, slot=slot, bi=bi: a_.copy(
                        out=prj_sb[bi][:], in_=ps[slot][0:128, 0:TC]),
                        waits=[e_mm] + WAR[f"prj{bi}"])
                    ps_read(slot, e_evac)
                    e_sto = dma(f"st_prj{bi}", odram[hh * DN:(hh + 1) * DN, creg],
                                prj_sb[bi][:], waits=[e_evac], eng="pool")
                    WAR[f"prj{bi}"] = [e_sto]
                    cons.append(e_sto)
            e_mmv = None
            for a in range(NKT):
                pp = a % 4
                e_sv = dma(f"ld_sl2{pp}", sl2_sb[pp][:],
                           wvb_d[a * 128:(a + 1) * 128, :], waits=WAR[f"sl2{pp}"])
                WAR[f"sl2{pp}"] = []
                for tt in range(KT):
                    slot = f"PS{tt}"
                    e_mmv = mk_mm(slot, ps[slot][0:128, 0:HLOC * DV],
                                  xg_sb[p][:, a * TC + tt * 128:a * TC + (tt + 1) * 128],
                                  sl2_sb[pp][:],
                                  start=(a == 0), stop=(a == NKT - 1),
                                  waits=[e_sv, ldg_map[a // NDT]])
                    WAR[f"sl2{pp}"].append(e_mmv)
            for tt in range(KT):
                slot = f"PS{tt}"
                bi = tt % 2
                e_evv = act(lambda a_, slot=slot, bi=bi: a_.copy(
                    out=vprj_sb[bi][:], in_=ps[slot][0:128, 0:HLOC * DV]),
                    waits=[e_mmv] + WAR[f"vprj{bi}"])
                ps_read(slot, e_evv)
                e_stv = None
                for hh in range(HLOC):
                    e_stv = dma(f"st_vprj{bi}",
                                v_dram[hh, c, :, tt * DV:(tt + 1) * DV],
                                vprj_sb[bi][:, hh * DV:(hh + 1) * DV], waits=[e_evv], eng="pool")
                WAR[f"vprj{bi}"] = [e_stv]
                cons.append(e_stv)
            prj_stores += cons
            for r_ in range(NCORES):
                WAR[f"xg{p}{r_}"] = [e_mm, e_mmv]
        P.full_barrier(prj_stores)

    with contextlib.ExitStack() as as_:
        q_sb = [sb(as_, f"q_at{h}", [128, TC], BF16, side="right") for h in range(HLOC)]
        k_sb = [sb(as_, f"k_at{p}", [128, TC], BF16, side="right") for p in range(6)]
        v_sb = [sb(as_, f"v_at{p}", [128, KT * 128], BF16, side="right") for p in range(6)]
        if KT == 4:
            pbF_sb = [sb(as_, f"pbF{r}", [128, 4 * TC], BF16, side="right")
                      for r in range(4)]
            pb_sb = None
            mask4_sb = sb(as_, "mask4_sb", [128, KT * TC], F32, side="right")
            EV["mask4"] = dma("ld_c", mask4_sb[:], mask4_d[:, :])
            ev_pz = EV["memset"]
        else:
            pbA_sb = sb(as_, "pbA_at", [128, 4 * TC], BF16, side="right")
            pbB_sb = [sb(as_, f"pbB_at{p}", [128, 2 * TC], BF16, side="right") for p in range(2)]
            pb_sb = [pbA_sb[:, 0:2 * TC], pbA_sb[:, 2 * TC:4 * TC], pbB_sb[0][:, :], pbB_sb[1][:, :]]
            pdiag_sb = [sb(as_, f"pd_at{t}", [128, TC], BF16, side="right") for t in range(KT)]
            e_z = EV["memset"]
            for t in range(KT):
                e_z = dve(lambda v, t=t: v.memset(pdiag_sb[t][:], 0.0), waits=[e_z])
            ev_pz = e_z
        invd_sb = [sb(as_, f"invd{i}", [1, TC], F32, side="right") for i in range(2)]
        invb_sb = [sb(as_, f"invb{i}", [128, TC], F32, side="right") for i in range(2)]
        attn_sb = [sb(as_, f"attn_{h}", [128, TC], BF16, side="right") for h in range(HLOC)]
        wo_sb = sb(as_, "wo_sb", [128, HLOC * D], BF16, side="right")
        wop_sb = [sb(as_, f"wop_{i}", [128, TC], BF16, side="right") for i in range(2)]
        rs_sb = [sb(as_, f"rs_{p}", [128, NDT * TC], BF16, side="right") for p in range(2)]

        e_wo = dma("ld_wo_res", wo_sb[:, :].rearrange("p (h d) -> p h d", d=D),
                   wob_d[:, :].rearrange("(h p) d -> p h d", p=128))

        SETS = [("PS0", "PS1"), ("PS2", "PS3"), ("PS6", "PS7")]
        SETBASE = [(psa, 0), (psa, 1024), (psb, 0)]
        attn_rot = [0]
        att_stores = []
        xcons = [[], []]
        for qc in range(NCH):
            p = qc % 2
            ev_x = load_x(xstream[2], qc, p, waits=xcons[p], eng="pool")
            xcons[p] = []
            attn_ready = []
            e_qs = {}
            for hh in range(HLOC):
                e_qs[hh] = dma(f"ld_q{hh}", q_sb[hh][:],
                               q_dram[hh * DN:(hh + 1) * DN, qc * TC:(qc + 1) * TC],
                               waits=WAR[f"qat{hh}"])
                WAR[f"qat{hh}"] = []
            for hh in range(HLOC):
                hb = hh % 2 if KT == 4 else 0
                PVS = ("PS4", "PS6")[hb]
                DENS = ("PS5", "PS7")[hb]
                e_q = e_qs[hh]
                first = [True]
                e_pv = e_den = None
                pend_pv = []

                def flush_pv(pend, upto):
                    nonlocal e_pv, e_den
                    while len(pend) > upto:
                        (x_exp, x_v, x_kp, parts, wkey, fin) = pend.pop(0)
                        for j, (kt, ap) in enumerate(parts):
                            stp = fin and j == len(parts) - 1
                            e_pv = mk_mm(PVS, ps[PVS][0:DV, 0:TC],
                                         v_sb[x_kp][:, kt * 128:(kt + 1) * 128],
                                         ap, start=first[0], stop=stp,
                                         waits=[x_exp, x_v])
                            e_den = mk_mm(DENS, ps[DENS][0:1, 0:TC], ones_bf[:, 0:1],
                                          ap, start=first[0], stop=stp,
                                          waits=[x_exp, EV["memset"]])
                            first[0] = False
                            WAR[f"vat{x_kp}"].append(e_pv)
                        WAR[wkey] = [e_pv, e_den]

                for kc in range(qc + 1):
                    kp = kc % 6
                    e_k = dma(f"ld_k{kp}", k_sb[kp][:],
                              k_dram[hh * DN:(hh + 1) * DN, kc * TC:(kc + 1) * TC],
                              waits=WAR[f"kat{kp}"])
                    e_v = dma(f"ld_v{kp}", v_sb[kp][:],
                              v_dram[hh, kc, :, :],
                              waits=WAR[f"vat{kp}"])
                    WAR[f"kat{kp}"] = []
                    WAR[f"vat{kp}"] = []
                    diag = (kc == qc)
                    if KT == 4:
                        r = kc % 4
                        pbr = pbF_sb[r]
                        qk_evs = []
                        for kt in range(4):
                            sl = f"PS{kt}"
                            e_s_ = mk_mm(sl, ps[sl][0:128, 0:TC],
                                         k_sb[kp][:, kt * 128:(kt + 1) * 128],
                                         q_sb[hh][:], start=True, stop=True,
                                         waits=[e_k, e_q])
                            qk_evs.append((sl, e_s_))
                        WAR[f"kat{kp}"] += [e for _, e in qk_evs]
                        WAR[f"qat{hh}"] += [e for _, e in qk_evs]
                        prod = [e for _, e in qk_evs]
                        if diag:
                            e_msk = dve(lambda v: v.tensor_tensor(
                                out=psa[0:128, 0:2048].rearrange(
                                    "p (b c) -> p b c", c=512)[:, :, 0:TC],
                                in0=psa[0:128, 0:2048].rearrange(
                                    "p (b c) -> p b c", c=512)[:, :, 0:TC],
                                in1=mask4_sb[:, :].rearrange("p (b c) -> p b c", c=TC),
                                op=ALU.add), waits=prod + [EV["mask4"]])
                            for sl, _ in qk_evs:
                                ps_write_dve(sl, e_msk)
                            prod = [e_msk]
                        e_exp = act(lambda a_, pbr=pbr: a_.activation(
                            out=pbr[:, :].rearrange("p (b c) -> p b c", c=TC),
                            in_=psa[0:128, 0:2048].rearrange(
                                "p (b c) -> p b c", c=512)[:, :, 0:TC],
                            func=AF.Exp, scale=SCALE, bias=nshift_sb[:, :]),
                            waits=prod + WAR[f"pbat{r}"])
                        for sl, _ in qk_evs:
                            ps_read(sl, e_exp)
                        pend_pv.append((e_exp, e_v, kp,
                                        [(0, pbr[:, 0:TC]), (1, pbr[:, TC:2 * TC])],
                                        f"pbat{r}", False))
                        pend_pv.append((e_exp, e_v, kp,
                                        [(2, pbr[:, 2 * TC:3 * TC]),
                                         (3, pbr[:, 3 * TC:4 * TC])],
                                        f"pbat{r}", diag))
                        flush_pv(pend_pv, 4)
                    elif not diag:
                        for half in range(KT // 2):
                            rot = attn_rot[0] % 3
                            attn_rot[0] += 1
                            sA, sB = SETS[rot]
                            kt0, kt1 = 2 * half, 2 * half + 1
                            e_sa = mk_mm(sA, ps[sA][0:128, 0:TC],
                                         k_sb[kp][:, kt0 * 128:(kt0 + 1) * 128],
                                         q_sb[hh][:], start=True, stop=True,
                                         waits=[e_k, e_q])
                            e_sb2 = mk_mm(sB, ps[sB][0:128, 0:TC],
                                          k_sb[kp][:, kt1 * 128:(kt1 + 1) * 128],
                                          q_sb[hh][:], start=True, stop=True,
                                          waits=[e_k, e_q])
                            WAR[f"kat{kp}"] += [e_sa, e_sb2]
                            WAR[f"qat{hh}"] += [e_sa, e_sb2]
                            base, off = SETBASE[rot]
                            e_exp = act(lambda a_, base=base, off=off, rot=rot: a_.activation(
                                out=pb_sb[rot][:, :].rearrange("p (b c) -> p b c", c=TC),
                                in_=base[0:128, off:off + 1024].rearrange(
                                    "p (b c) -> p b c", c=512)[:, :, 0:TC],
                                func=AF.Exp, scale=SCALE, bias=nshift_sb[:, :]),
                                waits=[e_sa, e_sb2] + WAR[f"pbat{rot}"])
                            ps_read(sA, e_exp)
                            ps_read(sB, e_exp)
                            pend_pv.append((e_exp, e_v, kp,
                                            [(kt0, pb_sb[rot][:, 0:TC]),
                                             (kt1, pb_sb[rot][:, TC:2 * TC])],
                                            f"pbat{rot}", False))
                            flush_pv(pend_pv, 1)
                    else:
                        flush_pv(pend_pv, 0)
                    if diag and KT != 4:
                        for kt in range(KT):
                            sslot = f"PS{kt % 2}"
                            j0 = kt * 128
                            e_s = mk_mm(sslot, ps[sslot][0:128, j0:TC],
                                        k_sb[kp][:, kt * 128:(kt + 1) * 128],
                                        q_sb[hh][:, j0:TC], start=True, stop=True,
                                        waits=[e_k, e_q])
                            WAR[f"kat{kp}"].append(e_s)
                            WAR[f"qat{hh}"].append(e_s)
                            e_msk = dve(lambda v, s=sslot, kt=kt: v.tensor_tensor(
                                out=ps[s][0:128, kt * 128:(kt + 1) * 128],
                                in0=ps[s][0:128, kt * 128:(kt + 1) * 128],
                                in1=tri_sb[:], op=ALU.add), waits=[e_s, EV["tri"]])
                            ps_write_dve(sslot, e_msk)
                            pbuf = pdiag_sb[kt]
                            e_exp = act(lambda a_, s=sslot, pb=pbuf, j0=j0: a_.activation(
                                out=pb[:, j0:TC], in_=ps[s][0:128, j0:TC],
                                func=AF.Exp, scale=SCALE, bias=nshift_sb[:, :]),
                                waits=[e_msk, ev_pz] + WAR[f"pd{kt}"])
                            ps_read(sslot, e_exp)
                            last2 = (kt == KT - 1)
                            e_pv = mk_mm(PVS, ps[PVS][0:DV, j0:TC],
                                         v_sb[kp][:, kt * 128:(kt + 1) * 128],
                                         pbuf[:, j0:TC], start=first[0], stop=last2,
                                         waits=[e_exp, e_v])
                            e_den = mk_mm(DENS, ps[DENS][0:1, j0:TC], ones_bf[:, 0:1],
                                          pbuf[:, j0:TC], start=first[0], stop=last2,
                                          waits=[e_exp, EV["memset"]])
                            first[0] = False
                            WAR[f"vat{kp}"].append(e_pv)
                            WAR[f"pd{kt}"] = [e_pv, e_den]
                flush_pv(pend_pv, 0)
                e_rec = dve(lambda v, hb=hb, DENS=DENS: v.reciprocal(
                    out=invd_sb[hb][:], in_=ps[DENS][0:1, 0:TC]),
                    waits=[e_den] + WAR[f"invd{hb}"])
                ps_read(DENS, e_rec)
                e_bc = mk_mm(DENS, ps[DENS][0:128, 0:TC], ones_f1[0:1, :],
                             invd_sb[hb][0:1, :],
                             start=True, stop=True, waits=[e_rec])
                WAR[f"invd{hb}"] = [e_bc]
                e_ebc = dve(lambda v, hb=hb, DENS=DENS: v.tensor_copy(
                    out=invb_sb[hb][:], in_=ps[DENS][0:128, 0:TC]),
                    waits=[e_bc] + WAR[f"invb{hb}"])
                ps_read(DENS, e_ebc)
                e_at = dve(lambda v, hh=hh, hb=hb, PVS=PVS: v.tensor_tensor(
                    out=attn_sb[hh][:],
                    in0=ps[PVS][0:DV, 0:TC],
                    in1=invb_sb[hb][:], op=ALU.mult),
                    waits=[e_ebc, e_pv] + WAR[f"ah{hh}"])
                ps_read(PVS, e_at)
                WAR[f"invb{hb}"] = [e_at]
                attn_ready.append(e_at)
            wo_stores = []
            for grp in range(NKT // 4):
                for dsub in range(4):
                    dti = grp * 4 + dsub
                    slot = f"PS{dsub}"
                    e_wom = None
                    for hh in range(HLOC):
                        e_wom = mk_mm(slot, ps[slot][0:128, 0:TC],
                                      wo_sb[:, hh * D + dti * 128:hh * D + (dti + 1) * 128],
                                      attn_sb[hh][:], start=(hh == 0), stop=(hh == HLOC - 1),
                                      waits=[e_wo, attn_ready[hh]])
                        WAR[f"ah{hh}"].append(e_wom)
                    bi2 = dsub % 2
                    e_ew = dve(lambda v, slot=slot, bi2=bi2: v.tensor_copy(
                        out=wop_sb[bi2][:], in_=ps[slot][0:128, 0:TC]),
                        waits=[e_wom] + WAR[f"wop{bi2}"])
                    ps_read(slot, e_ew)
                    e_sw = dma(f"st_wop{bi2}", wop_c[qc * D + dti * 128:qc * D + (dti + 1) * 128, :],
                               wop_sb[bi2][:], waits=[e_ew], eng="pool")
                    WAR[f"wop{bi2}"] = [e_sw]
                    wo_stores.append(e_sw)
            creg = slice(qc * TC, (qc + 1) * TC)
            e_rs = cc("ReduceScatter", ALU.add, [wop_c[qc * D:(qc + 1) * D, :].opt()],
                      [wop_g[qc * DSL:(qc + 1) * DSL, :].opt()], waits=wo_stores)
            e_lrs = dma(f"ld_rs{p}", rs_sb[p][:, :].rearrange("p (a t) -> p a t", t=TC),
                        wop_g[qc * DSL:(qc + 1) * DSL, :].rearrange("(a p) t -> p a t", p=128),
                        waits=[e_rs] + WAR[f"rs{p}"], eng="pool")
            e_resid = dve(lambda v, p=p: v.tensor_tensor(out=x_sb[p][:], in0=x_sb[p][:],
                                                         in1=rs_sb[p][:], op=ALU.add),
                          waits=[e_lrs, ev_x])
            WAR[f"rs{p}"] = [e_resid]
            e_stx = dma(f"st_x{p}", xstream[3][:, creg].rearrange("(a p) t -> p a t", p=128),
                        x_sb[p][:, :].rearrange("p (a t) -> p a t", t=TC),
                        waits=[e_resid], eng="pool")
            att_stores.append(e_stx)
            xcons[p] = [e_resid, e_stx]
            for hh in range(HLOC):
                WAR[f"ah{hh}"] = list(wo_stores)
        P.full_barrier(att_stores)

    ffn_phase(xstream[3], xstream[4], "f2")
    P.op("pool", lambda g: g.nop())

    emit_engine = P.emit_into(nc, ctx)
    with nc.Block() as block:
        @block.sync
        def _(sync):
            emit_engine("sync", sync)

        @block.tensor
        def _(tensor):
            emit_engine("pe", tensor)

        @block.scalar
        def _(scalar):
            emit_engine("act", scalar)

        @block.vector
        def _(vector):
            emit_engine("dve", vector)

        @block.gpsimd
        def _(gpsimd):
            emit_engine("pool", gpsimd)

    ctx.close()
    return nc


def prep_inputs(cfg, inp):
    T, D, NH, TC = cfg["T"], cfg["D"], cfg["NH"], cfg["TC"]
    DSL, HLOC = cfg["DSL"], cfg["HLOC"]
    NDT, NDNT, NST, DTR = cfg["NDT"], cfg["NDNT"], cfg["NST"], cfg["DTR"]
    DN, DV, DQ, G = cfg["DN"], cfg["DV"], cfg["DQ"], cfg["G"]
    OUT2 = D // 2

    x = np.ascontiguousarray(np.asarray(inp["L0_attn_a_x"], np.float32)[0].T)
    wq = np.asarray(inp["L0_attn_b_wq"], np.float32)
    wkv = np.asarray(inp["L0_attn_b_wkv"], np.float32)
    wo = np.asarray(inp["L0_attn_b_wo"], np.float32)
    xpw = np.asarray(inp["L0_attn_a_x_proj_w"], np.float32)
    dtw = np.asarray(inp["L0_attn_a_dt_proj_w"], np.float32)
    dtb = np.asarray(inp["L0_attn_a_dt_proj_b"], np.float32)
    alog = np.asarray(inp["L0_attn_a_A_log"], np.float32)
    dvec = np.asarray(inp["L0_attn_a_D"], np.float32)
    Wq = np.asarray(inp["L0_ffn_W_q"], np.uint8)
    scl = np.asarray(inp["L0_ffn_scale"], np.float32).reshape(D, D // G)
    zro = np.asarray(inp["L0_ffn_zero"], np.float32).reshape(D, D // G)
    byte_T = np.ascontiguousarray(Wq.reshape(OUT2, D).T)

    rk = np.zeros((128, 16, 128), np.float32)
    sk = np.zeros((128, 16, 128), np.float32)
    for kk in range(16):
        for po in range(128):
            rk[8 * kk + po // 16, kk, po] = 1.0
            sk[po, kk, 8 * kk + po // 16] = 1.0
    rk = rk.reshape(128, 2048).astype(NPBF16)
    sk = sk.reshape(128, 2048).astype(NPBF16)
    rb = np.zeros((32, 2, 128), np.float32)
    for po in range(128):
        rb[po % 16, 0, po] = 1.0
        rb[16 + po % 16, 1, po] = 1.0
    rb = rb.reshape(32, 256).astype(NPBF16)
    tri = np.where(np.arange(128)[:, None] > np.arange(128)[None, :], -1e30, 0.0
                   ).astype(np.float32)
    if cfg["KT"] == 4:
        KT4 = cfg["KT"]
        kk = (np.arange(KT4)[:, None, None] * 128 + np.arange(128)[None, :, None])
        qq = np.arange(TC)[None, None, :]
        mask4 = np.where(kk > qq, -1e30, 0.0).astype(np.float32)
        mask4 = np.ascontiguousarray(mask4.transpose(1, 0, 2).reshape(128, KT4 * TC))

    in_maps = []
    for i in range(NCORES):
        dsl = slice(i * DSL, (i + 1) * DSL)
        hsl = range(i * HLOC, (i + 1) * HLOC)
        m = {}
        m["x0"] = np.ascontiguousarray(x[dsl])
        m["wq_sl"] = np.ascontiguousarray(
            np.concatenate([wq[:, h * DQ:h * DQ + DN] for h in hsl], axis=1))
        m["wk_sl"] = np.ascontiguousarray(
            np.concatenate([wkv[:, h * DN:(h + 1) * DN] for h in hsl], axis=1))
        m["wv_sl"] = np.ascontiguousarray(
            np.concatenate([wkv[:, NH * DN + h * DV:NH * DN + (h + 1) * DV] for h in hsl],
                           axis=1))
        m["wo_sl"] = np.ascontiguousarray(wo[i * HLOC * DV:(i + 1) * HLOC * DV, :])
        m["xpw_sl"] = np.ascontiguousarray(xpw[:, dsl].T)
        m["dtw_sl"] = np.ascontiguousarray(dtw[dsl, :].T)
        m["dtb_sl"] = np.ascontiguousarray(dtb[dsl].reshape(NDT, 128).T)
        m["alog_sl"] = np.ascontiguousarray(
            alog[dsl].reshape(NDNT, 128).T)
        m["dvec_sl"] = np.ascontiguousarray(dvec[dsl].reshape(NDT, 128).T)
        if i < NCORES // 2:
            cols = slice(i * DSL, (i + 1) * DSL)
            shift = 4
        else:
            cols = slice((i - NCORES // 2) * DSL, (i - NCORES // 2 + 1) * DSL)
            shift = 0
        m["wqt_sl"] = np.ascontiguousarray(byte_T[:, cols])
        osl = np.arange(i * DSL, (i + 1) * DSL)
        m["scl_sl"] = np.ascontiguousarray(np.repeat(scl[osl].T, G, axis=0))
        m["zro_sl"] = np.ascontiguousarray(np.repeat(zro[osl].T, G, axis=0))
        msel = 1.0 if shift == 4 else 0.0
        m["nibsel"] = np.ascontiguousarray(
            np.tile(np.array([[msel, 1.0 - msel]], np.float32), (128, 1)))
        m["rk_all"] = rk
        m["ident"] = np.eye(128, dtype=np.float32)
        m["sk_all"] = sk
        m["rb_bc"] = rb
        m["trimask"] = tri
        if cfg["KT"] == 4:
            m["mask4"] = mask4
        in_maps.append(m)
    return in_maps


DEFAULT_CFG = make_cfg()


LAST_PROG = [None]


def kernel(**inputs):
    cfg = DEFAULT_CFG
    in_maps = prep_inputs(cfg, inputs)
    nc = build_program(cfg)
    res = run_bass_kernel_spmd(nc, in_maps, core_ids=list(range(NCORES)))
    full = np.concatenate([np.asarray(r["xout"]) for r in res.results], axis=0)
    return np.ascontiguousarray(full.T)[None].astype(np.float32)
